# version 01
# baseline (speedup 1.0000x reference)
import contextlib
import numpy as np
import ml_dtypes
import concourse.bass as bass
import concourse.mybir as mybir
from concourse.bass_utils import run_bass_kernel_spmd
from concourse.bass_core_rust import IndirectOffsetOnAxis

F32 = mybir.dt.float32
BF16 = mybir.dt.bfloat16
I32 = mybir.dt.int32
U32 = mybir.dt.uint32
AF = mybir.ActivationFunctionType
ALU = mybir.AluOpType
AX = mybir.AxisListType
BFD = ml_dtypes.bfloat16

D = 1024
DC = 8
HID = 1536
HC = 12
GH = 2048
HD = 128
QH = 8
NE = 8
FH = 4096
WIN = 4096
OWN = 2048
PRE = 2048
KV0 = 1024
ZLEN = 3072
BAND = 1280
WSZ = 1024
CAP = 1024
TRASH = NE * CAP
NEG = -1.0e30
EPS = 1e-5
SCL = float(1.0 / np.sqrt(128.0))

STRICT = True
DEBUG_PHASE = None

ENGINES = ("sync", "gpsimd", "vector", "scalar", "tensor")
SEMS = ("ds", "dg", "qv", "qs", "qt", "qg")


class Plan:
    def __init__(self, nc, strict):
        self.nc = nc
        self.strict = strict
        self.streams = {e: [] for e in ENGINES}
        self.sem = {}
        self.cnt = {s: 0 for s in SEMS}
        self.seen = {e: {s: 0 for s in SEMS} for e in ENGINES}

    def ev(self, name):
        return (name, self.cnt[name])

    def all_ev(self):
        return [self.ev(s) for s in SEMS]

    def emit(self, eng, f, waits=(), inc=None, n=1):
        if self.strict:
            waits = self.all_ev()
        pruned = []
        seen = self.seen[eng]
        for w in waits:
            if w is None:
                continue
            s, v = w
            if v <= 0 or seen[s] >= v:
                continue
            seen[s] = v
            pruned.append((s, v))
        if inc is not None:
            self.cnt[inc] += n
        sems = self.sem

        def runner(e, f=f, pruned=tuple(pruned), inc=inc, n=n):
            for s, v in pruned:
                e.wait_ge(sems[s], v)
            ins = f(e)
            if inc is not None:
                ins.then_inc(sems[inc], n)
        self.streams[eng].append(runner)
        return self.ev(inc) if inc is not None else None

    def wait(self, eng, waits):
        pruned = []
        seen = self.seen[eng]
        for w in waits:
            if w is None:
                continue
            s, v = w
            if v <= 0 or seen[s] >= v:
                continue
            seen[s] = v
            pruned.append((s, v))
        if not pruned:
            return
        sems = self.sem

        def runner(e, pruned=tuple(pruned)):
            for s, v in pruned:
                e.wait_ge(sems[s], v)
        self.streams[eng].append(runner)

    def dma(self, eng, out, in_, waits=(), **kw):
        sem = "ds" if eng in ("sync", "scalar") else "dg"
        return self.emit(eng, lambda e, o=out, i=in_: e.dma_start(out=o, in_=i, **kw),
                         waits=waits, inc=sem, n=16)

    def barrier(self):
        evs = self.all_ev()
        for eng in ENGINES:
            self.wait(eng, evs)


def _r(ap, pat, **kw):
    return ap.rearrange(pat, **kw)


def build_program(debug_phase=None, strict=STRICT):
    nc = bass.Bass("TRN2", target_bir_lowering=False, debug=False)

    def din(name, shape, dt=F32):
        return nc.dram_tensor(name, list(shape), dt, kind="ExternalInput").ap()

    xwin = din("xwin", [WIN, D])
    flagc = din("flagc", [128, 1])
    costab = din("costab", [128, ZLEN], BF16)
    sintab = din("sintab", [128, ZLEN], BF16)
    mconst = din("mconst", [128, BAND], BF16)
    mbias = din("mbias", [16, BAND], BF16)
    identf = din("identf", [128, 128])
    identb = din("identb", [128, 128], BF16)
    onesb = din("onesb", [1, 128], BF16)
    iota8 = din("iota8", [128, NE])
    bgrep = din("bgrep", [128, NE])
    winT = din("winT", [D, 2 * HID])
    wgT = din("wgT", [HID, 2 * HID])
    woutT = din("woutT", [HID, D])
    cwi = din("cwi", [HID, 4])
    cbi = din("cbi", [HID, 1])
    hbg = din("hbg", [2 * HID, 1])
    hfb = din("hfb", [HID, 1])
    g1gT = din("g1gT", [D, 2 * GH])
    g1sT = din("g1sT", [GH, D])
    g2gT = din("g2gT", [D, 2 * GH])
    g2sT = din("g2sT", [GH, D])
    wqT = din("wqT", [D, D])
    wkvT = din("wkvT", [D, 2 * HD])
    woT = din("woT", [D, D])
    fw1 = din("fw1", [NE, D, FH])
    fw2 = din("fw2", [NE, FH, D])
    fwg = din("fwg", [D, NE])
    gnh = din("gnh", [D, 1])
    gn1 = din("gn1", [D, 1])
    gns = din("gns", [D, 1])
    gn2 = din("gn2", [D, 1])
    epsc = din("epsc", [128, 1])

    yout = nc.dram_tensor("y", [OWN, D], F32, kind="ExternalOutput").ap()

    gate_hbm = nc.dram_tensor("gate_hbm", [HID, ZLEN], BF16).ap()
    hconv_hbm = nc.dram_tensor("hconv_hbm", [HID, WIN], BF16).ap()
    hst_hbm = nc.dram_tensor("hst_hbm", [HID, ZLEN], BF16).ap()
    xg_hbm = nc.dram_tensor("xg_hbm", [TRASH + 1, D], BF16).ap()
    yg_hbm = nc.dram_tensor("yg_hbm", [TRASH + 1, D], BF16).ap()
    xmid_hbm = nc.dram_tensor("xmid_hbm", [OWN, D], F32).ap()

    taps = {}
    if debug_phase is not None:
        for nm, shape, dt in [
            ("t_u", [128, DC * WIN], BF16), ("t_hconv", [128, HC * WIN], BF16),
            ("t_gate", [128, HC * ZLEN], BF16), ("t_hst", [128, HC * ZLEN], BF16),
            ("t_z", [128, DC * OWN], F32), ("t_zh", [128, DC * KV0], BF16),
            ("t_k", [128, 3200], BF16), ("t_v", [128, 3200], BF16),
            ("t_logits", [128, 16 * NE], F32), ("t_slot", [128, 16], I32),
            ("t_xmid", [128, 16 * D], F32),
        ]:
            taps[nm] = nc.dram_tensor(nm, shape, dt, kind="ExternalOutput").ap()

    es = contextlib.ExitStack()
    with es:
        P = Plan(nc, strict)
        for s in SEMS:
            P.sem[s] = es.enter_context(nc.semaphore(s))

        _uid = [0]

        def SB(st, name, shape, dt):
            _uid[0] += 1
            return st.enter_context(nc.sbuf_tensor(f"{name}_{_uid[0]}", list(shape), dt))

        def PS(st, name, shape, dt):
            _uid[0] += 1
            return st.enter_context(nc.psum_tensor(f"{name}_{_uid[0]}", list(shape), dt))

        idf_sb = SB(es, "idf", [128, 128], F32)
        idb_sb = SB(es, "idb", [128, 128], BF16)
        ones_sb = SB(es, "ones1", [1, 128], BF16)
        flag_sb = SB(es, "flag", [128, 1], F32)
        gn_sb = SB(es, "gn", [128, DC * 4], F32)
        small = SB(es, "small", [128, 64], F32)
        cw_sb = SB(es, "cw4", [128, HC * 4], F32)
        stat = SB(es, "stat", [128, 8], F32)
        s6 = SB(es, "s6", [128, 12], F32)
        slot_i = SB(es, "slot_i", [128, 16], I32)
        eps_sb = SB(es, "eps_sb", [128, 1], F32)
        idxf = SB(es, "idxf", [128, 16], F32)

        P.dma("sync", idf_sb[:], identf)
        P.dma("sync", idb_sb[:], identb)
        P.dma("sync", ones_sb[:], onesb)
        P.dma("sync", flag_sb[:], flagc)
        P.dma("sync", eps_sb[:], epsc)
        for i, g in enumerate((gnh, gn1, gns, gn2)):
            P.dma("sync", _r(gn_sb[:, i * DC:(i + 1) * DC], "p (c one) -> p c one", c=DC),
                  _r(g, "(c p) one -> p c one", p=128), allow_slow_non_contiguous=True)
        P.dma("sync", _r(cw_sb[:], "p (c k) -> p c k", c=HC), _r(cwi, "(c p) k -> p c k", p=128), allow_slow_non_contiguous=True)
        P.dma("sync", _r(small[:, 36:48], "p (c o) -> p c o", c=HC),
              _r(cbi, "(c p) one -> p c one", p=128), allow_slow_non_contiguous=True)
        P.dma("sync", _r(small[:, 48:60], "p (c o) -> p c o", c=HC),
              _r(hfb, "(c p) one -> p c one", p=128), allow_slow_non_contiguous=True)
        P.dma("sync", _r(small[:, 12:24], "p (c o) -> p c o", c=HC),
              _r(hbg[0:HID], "(c p) one -> p c one", p=128), allow_slow_non_contiguous=True)
        P.dma("sync", _r(small[:, 24:36], "p (c o) -> p c o", c=HC),
              _r(hbg[HID:2 * HID], "(c p) one -> p c one", p=128), allow_slow_non_contiguous=True)
        ldev = P.ev("ds")
        fb = small[:, 48:60]
        P.emit("scalar", lambda e: e.activation(small[:, 0:12], fb, AF.Abs),
               waits=[ldev], inc="qs")
        P.emit("scalar", lambda e: e.activation(small[:, 0:12], small[:, 0:12], AF.Exp,
                                                scale=-1.0), waits=[P.ev("qs")], inc="qs")
        P.emit("scalar", lambda e: e.activation(small[:, 0:12], small[:, 0:12], AF.Ln,
                                                bias=1.0), waits=[P.ev("qs")], inc="qs")
        P.emit("vector", lambda e: e.tensor_scalar_max(stat[:, 0:6], fb[:, 0:6], 0.0),
               waits=[ldev], inc="qv")
        P.emit("vector", lambda e: e.tensor_scalar_max(s6[:, 0:6], fb[:, 6:12], 0.0),
               waits=[ldev], inc="qv")
        P.emit("vector", lambda e: e.tensor_add(small[:, 0:6], small[:, 0:6], stat[:, 0:6]),
               waits=[P.ev("qs"), P.ev("qv")], inc="qv")
        P.emit("vector", lambda e: e.tensor_add(small[:, 6:12], small[:, 6:12], s6[:, 0:6]),
               waits=[P.ev("qv")], inc="qv")
        P.emit("vector", lambda e: e.tensor_scalar_mul(small[:, 0:12], small[:, 0:12], -4.0),
               waits=[P.ev("qv")], inc="qv")
        P.emit("vector", lambda e: e.tensor_scalar_mul(small[:, 12:36], small[:, 12:36], 0.5),
               waits=[ldev], inc="qv")
        consts_ev = P.ev("qv")

        def load_wT(eng, dst_sb, src_ap, kdim, ncols, waits=()):
            kc = kdim // 128
            return P.dma(eng, _r(dst_sb[:, 0:kc * ncols], "p (c n) -> p c n", c=kc),
                         _r(src_ap, "(c p) n -> p c n", p=128), waits=waits)

        def stats_on(tok_tile, waits=()):
            P.emit("vector", lambda e: e.bn_stats(s6[:, 0:6], tok_tile[:, 0:512]),
                   waits=list(waits), inc="qv")
            P.emit("vector", lambda e: e.bn_stats(s6[:, 6:12], tok_tile[:, 512:1024]),
                   waits=[P.ev("qv")], inc="qv")
            return P.emit("vector", lambda e: e.bn_aggr(stat[:, 0:2], s6[:]),
                          waits=[P.ev("qv")], inc="qv")

        def mk_scale(col_in, col_sq, col_rs, eps, aggr_ev):
            bias = 0.0 if eps is None else eps_sb[:, 0:1]
            if eps is not None:
                aggr_ev = P.emit("vector", lambda e: e.tensor_scalar_add(
                    stat[:, col_in:col_in + 1], stat[:, col_in:col_in + 1], EPS),
                    waits=[aggr_ev], inc="qv")
            e1 = P.emit("scalar", lambda e: e.activation(
                stat[:, col_sq:col_sq + 1], stat[:, col_in:col_in + 1], AF.Sqrt),
                waits=[aggr_ev], inc="qs")
            e2 = P.emit("vector", lambda e: e.reciprocal(
                stat[:, col_rs:col_rs + 1], stat[:, col_sq:col_sq + 1]),
                waits=[e1], inc="qv")
            e3 = P.emit("vector", lambda e: e.tensor_mul(
                stat[:, col_sq:col_sq + 1], stat[:, col_rs:col_rs + 1],
                stat[:, col_rs:col_rs + 1]), waits=[e2], inc="qv")
            e4 = P.emit("vector", lambda e: e.tensor_mul(
                stat[:, col_sq:col_sq + 1], stat[:, col_sq:col_sq + 1],
                stat[:, col_in:col_in + 1]), waits=[e3], inc="qv")
            e5 = P.emit("vector", lambda e: e.tensor_scalar(
                stat[:, col_sq:col_sq + 1], stat[:, col_sq:col_sq + 1], -0.5, 1.5,
                ALU.mult, ALU.add), waits=[e4], inc="qv")
            return P.emit("vector", lambda e: e.tensor_mul(
                stat[:, col_rs:col_rs + 1], stat[:, col_rs:col_rs + 1],
                stat[:, col_sq:col_sq + 1]), waits=[e5], inc="qv")

        def neg_m_rs(col_rs, col_out, rs_ev):
            return P.emit("vector", lambda e: e.scalar_tensor_tensor(
                stat[:, col_out:col_out + 1], stat[:, 0:1], -1.0,
                stat[:, col_rs:col_rs + 1], ALU.mult, ALU.mult),
                waits=[rs_ev], inc="qv")

        def transpose_tokmaj_to_fm(src_tok, pst, dtype_f32, extra_waits=()):
            bank_cols = 4 if dtype_f32 else 8
            idm = idf_sb if dtype_f32 else idb_sb
            for c in range(DC):
                P.emit("tensor", lambda e, c=c: e.matmul(
                    pst[:, c * 128:(c + 1) * 128], src_tok[:, c * 128:(c + 1) * 128],
                    idm[:], is_transpose=True,
                    start=(c % bank_cols == 0), stop=(c == DC - 1)),
                    waits=(list(extra_waits) if c == 0 else []),
                    inc="qt" if c == DC - 1 else None)
            return P.ev("qt")

        def transpose_fm_to_tokmaj(src_fm_slices, pst, dtype_f32, extra_waits=()):
            bank_cols = 4 if dtype_f32 else 8
            idm = idf_sb if dtype_f32 else idb_sb
            for c, sl in enumerate(src_fm_slices):
                P.emit("tensor", lambda e, c=c, sl=sl: e.matmul(
                    pst[:, c * 128:(c + 1) * 128], sl, idm[:], is_transpose=True,
                    start=(c % bank_cols == 0), stop=(c == DC - 1)),
                    waits=(list(extra_waits) if c == 0 else []),
                    inc="qt" if c == DC - 1 else None)
            return P.ev("qt")

        def z_slices(wp, width=128):
            own = wp >= PRE
            src = z if own else zh
            base = (wp - PRE) if own else (wp - KV0)
            w = OWN if own else KV0
            return [src[:, c * w + base: c * w + base + width] for c in range(DC)], own

        zstack = es.enter_context(contextlib.ExitStack())

        with contextlib.ExitStack() as pha:
            wg_sb = SB(pha, "wg_sb", [128, HC * 2 * HID], BF16)

            with contextlib.ExitStack() as psa:
                win_sb = SB(psa, "win_sb", [128, DC * 2 * HID], BF16)
                xa = SB(psa, "xa", [128, 2 * D], F32)
                u_fm = SB(psa, "u_fm", [128, DC * 1024], BF16)
                utok = SB(psa, "utok", [128, D], BF16)
                hpre = SB(psa, "hpre", [128, 1024 + 3], BF16)
                hcv = SB(psa, "hcv", [128, 1024], BF16)
                gtmp = SB(psa, "gtmp", [128, 1024], BF16)
                carry3 = SB(psa, "carry3", [128, HC * 3], BF16)
                psT = PS(psa, "psT", [128, 1024], BF16)
                pm0 = PS(psa, "pm0", [128, 1024], F32)
                pm1 = PS(psa, "pm1", [128, 1024], F32)

                wev = load_wT("gpsimd", win_sb, winT, D, 2 * HID)
                for c in range(DC):
                    P.emit("vector", lambda e, c=c: e.tensor_scalar_mul(
                        win_sb[:, c * 2 * HID:(c + 1) * 2 * HID],
                        win_sb[:, c * 2 * HID:(c + 1) * 2 * HID], gn_sb[:, c:c + 1]),
                        waits=[wev, consts_ev], inc="qv")
                win_ready = P.ev("qv")
                wg_ev = load_wT("gpsimd", wg_sb, wgT, HID, 2 * HID)
                P.emit("gpsimd", lambda e: e.memset(carry3[:], 0.0), inc="qg")
                cz_ev = P.ev("qg")

                pm_free = {0: None, 1: None}
                TCH = 1024
                for ch in range(WIN // TCH):
                    t0 = ch * TCH
                    utile_evs = []
                    for i in range(TCH // 128):
                        buf = xa[:, (i % 2) * D:(i % 2) * D + D]
                        xev = P.dma("sync", buf, xwin[t0 + i * 128: t0 + (i + 1) * 128, :],
                                    waits=[P.ev("qs")])
                        agg = stats_on(buf, waits=[xev])
                        rse = mk_scale(1, 2, 4, None, agg)
                        nme = neg_m_rs(4, 6, rse)
                        aev = P.emit("scalar", lambda e, buf=buf: e.activation(
                            utok[:], buf, AF.Identity, bias=stat[:, 6:7], scale=stat[:, 4:5]),
                            waits=[nme], inc="qs")
                        tev = transpose_tokmaj_to_fm(utok, psT, False,
                                                     extra_waits=[aev, win_ready, P.ev("qv")])
                        cev = P.emit("vector", lambda e, i=i: e.tensor_copy(
                            _r(u_fm[:], "p (c n) -> p c n", c=DC)[:, :, i * 128:(i + 1) * 128],
                            _r(psT[:], "p (c n) -> p c n", c=DC)),
                            waits=[tev], inc="qv")
                        utile_evs.append(cev)
                    u_ready = P.ev("qv")
                    if debug_phase == "A1" and taps:
                        P.dma("sync", _r(taps["t_u"], "p (c w) -> p c w", c=DC)[:, :, t0:t0 + TCH],
                              _r(u_fm[:], "p (c n) -> p c n", c=DC), waits=[u_ready])

                    do_gate = t0 >= KV0
                    mlist = [("h", m) for m in range(HC)] + \
                            ([("g", m) for m in range(HC)] if do_gate else [])
                    for kind, m in mlist:
                        col0 = (HID if kind == "h" else 0) + m * 128
                        pmi = m % 2
                        pm = pm0 if pmi == 0 else pm1
                        for nb in range(2):
                            for kc in range(DC):
                                P.emit("tensor", lambda e, kc=kc, nb=nb, col0=col0, pm=pm: e.matmul(
                                    pm[:, nb * 512:(nb + 1) * 512],
                                    win_sb[:, kc * 2 * HID + col0: kc * 2 * HID + col0 + 128],
                                    u_fm[:, kc * 1024 + nb * 512: kc * 1024 + (nb + 1) * 512],
                                    start=(kc == 0), stop=(kc == DC - 1)),
                                    waits=([u_ready, pm_free[pmi], wg_ev if kind == "h" else None]
                                           if (kc == 0 and nb == 0) else []),
                                    inc="qt" if (kc == DC - 1 and nb == 1) else None)
                        mmev = P.ev("qt")
                        if kind == "h":
                            e1 = P.emit("vector", lambda e, pm=pm: e.tensor_copy(
                                hpre[:, 3:1027], pm[:]), waits=[mmev, P.ev("dg"), P.ev("qv")],
                                inc="qv")
                            pm_free[pmi] = e1
                            e2 = P.emit("vector", lambda e, m=m: e.tensor_copy(
                                hpre[:, 0:3], carry3[:, m * 3:m * 3 + 3]),
                                waits=[e1, cz_ev], inc="qv")
                            if t0 == PRE:
                                e2 = P.emit("vector", lambda e: e.tensor_scalar_mul(
                                    hpre[:, 0:3], hpre[:, 0:3], flag_sb[:, 0:1]),
                                    waits=[e2], inc="qv")
                            e3 = P.emit("vector", lambda e, m=m: e.tensor_copy(
                                carry3[:, m * 3:m * 3 + 3], hpre[:, 1024:1027]),
                                waits=[e1], inc="qv")
                            c0 = P.emit("vector", lambda e, m=m: e.tensor_scalar(
                                hcv[:], hpre[:, 0:1024], cw_sb[:, m * 4:m * 4 + 1],
                                small[:, 36 + m:37 + m], ALU.mult, ALU.add),
                                waits=[e2, e3], inc="qv")
                            for k in range(1, 4):
                                c0 = P.emit("vector", lambda e, k=k, m=m: e.scalar_tensor_tensor(
                                    hcv[:], hpre[:, k:k + 1024], cw_sb[:, m * 4 + k:m * 4 + k + 1],
                                    hcv[:], ALU.mult, ALU.add), waits=[c0], inc="qv")
                            P.dma("gpsimd", hconv_hbm[m * 128:(m + 1) * 128, t0:t0 + TCH],
                                  hcv[:], waits=[c0])
                            P.wait("vector", [P.ev("dg")])
                        else:
                            gev = P.emit("scalar", lambda e, pm=pm: e.activation(
                                gtmp[:], pm[:], AF.Identity), waits=[mmev, P.ev("dg")], inc="qs")
                            pm_free[pmi] = gev
                            P.dma("gpsimd",
                                  gate_hbm[m * 128:(m + 1) * 128, t0 - KV0:t0 - KV0 + TCH],
                                  gtmp[:], waits=[gev])
                            P.wait("scalar", [P.ev("dg")])
                P.barrier()

                if debug_phase == "A1":
                    tmp = SB(psa, "dbg1", [128, WIN], BF16)
                    for m in range(HC):
                        ev = P.dma("sync", tmp[:], hconv_hbm[m * 128:(m + 1) * 128, :])
                        P.dma("sync", taps["t_hconv"][:, m * WIN:(m + 1) * WIN], tmp[:],
                              waits=[ev])
                    _finish(P, nc, es)
                    return nc

            with contextlib.ExitStack() as psa:
                hcb = SB(psa, "hcb", [128, 2 * HC * 512], BF16)
                tf = SB(psa, "tf", [128, 512], F32)
                al = SB(psa, "al", [128, 512], F32)
                ti = SB(psa, "ti", [128, 512], F32)
                sg = SB(psa, "sg", [128, 512], F32)
                w1s = SB(psa, "w1s", [128, 512], F32)
                xs_ = SB(psa, "xs_", [128, 512], F32)
                hsc = SB(psa, "hsc", [128, 512], F32)
                hsb = SB(psa, "hsb", [128, 512], BF16)
                stcol = SB(psa, "stcol", [128, HC], F32)
                pf = PS(psa, "pf", [128, 512], F32)
                pi_ = PS(psa, "pi", [128, 512], F32)

                P.emit("vector", lambda e: e.memset(stcol[:], 0.0), waits=[P.ev("qt")], inc="qv")
                TB = 512
                hc_free = {0: None, 1: None}
                for tb in range(WIN // TB):
                    t0 = tb * TB
                    half = tb % 2
                    dst = hcb[:, half * HC * 512:(half + 1) * HC * 512]
                    hev = P.dma("sync", _r(dst, "p (c n) -> p c n", c=HC),
                                _r(hconv_hbm[:, t0:t0 + TB], "(c p) n -> p c n", p=128),
                                waits=[hc_free[half]])
                    for c in range(HC):
                        for pp, colbase in ((pf, 0), (pi_, HID)):
                            for kc in range(HC):
                                P.emit("tensor", lambda e, kc=kc, pp=pp, colbase=colbase, c=c,
                                       dst=dst: e.matmul(
                                    pp[:],
                                    wg_sb[:, kc * 2 * HID + colbase + c * 128:
                                          kc * 2 * HID + colbase + c * 128 + 128],
                                    dst[:, kc * 512:(kc + 1) * 512],
                                    start=(kc == 0), stop=(kc == HC - 1)),
                                    waits=([hev, P.ev("qs"), P.ev("qg")] if kc == 0 else []),
                                    inc="qt" if kc == HC - 1 else None)
                        mmev = P.ev("qt")
                        e1 = P.emit("scalar", lambda e, c=c: e.activation(
                            tf[:], pf[:], AF.Tanh, bias=small[:, 12 + c:13 + c], scale=0.5),
                            waits=[mmev, P.ev("qg"), P.ev("qv")], inc="qs")
                        e2 = P.emit("scalar", lambda e, c=c: e.activation(
                            al[:], tf[:], AF.Exp, bias=small[:, c:c + 1], scale=small[:, c:c + 1]),
                            waits=[e1], inc="qs")
                        e3 = P.emit("scalar", lambda e, c=c: e.activation(
                            ti[:], pi_[:], AF.Tanh, bias=small[:, 24 + c:25 + c], scale=0.5),
                            waits=[e2], inc="qs")
                        g1 = P.emit("gpsimd", lambda e: e.tensor_scalar(
                            sg[:], ti[:], 0.5, 0.5, ALU.mult, ALU.add), waits=[e3], inc="qg")
                        g2 = P.emit("gpsimd", lambda e: e.tensor_mul(w1s[:], al[:], al[:]),
                                    waits=[e2], inc="qg")
                        g3 = P.emit("gpsimd", lambda e: e.tensor_scalar(
                            w1s[:], w1s[:], -1.0, 1.0 + 1e-6, ALU.mult, ALU.add),
                            waits=[g2], inc="qg")
                        e4 = P.emit("scalar", lambda e: e.activation(w1s[:], w1s[:], AF.Ln),
                                    waits=[g3], inc="qs")
                        e5 = P.emit("scalar", lambda e: e.activation(w1s[:], w1s[:], AF.Exp,
                                                                     scale=0.5),
                                    waits=[e4], inc="qs")
                        g4 = P.emit("gpsimd", lambda e: e.tensor_mul(xs_[:], sg[:], w1s[:]),
                                    waits=[e5, g1], inc="qg")
                        g5 = P.emit("gpsimd", lambda e, c=c, dst=dst: e.tensor_mul(
                            xs_[:], xs_[:], dst[:, c * 512:(c + 1) * 512]),
                            waits=[g4], inc="qg")
                        init_ev = None
                        if tb == PRE // TB:
                            init_ev = P.emit("vector", lambda e, c=c: e.tensor_scalar_mul(
                                stcol[:, c:c + 1], stcol[:, c:c + 1], flag_sb[:, 0:1]),
                                waits=[P.ev("qv")], inc="qv")
                        sev = P.emit("vector", lambda e, c=c: e.tensor_tensor_scan(
                            hsc[:], al[:], xs_[:], stcol[:, c:c + 1], ALU.mult, ALU.add),
                            waits=[g5, init_ev, P.ev("qs")], inc="qv")
                        P.emit("vector", lambda e, c=c: e.tensor_copy(
                            stcol[:, c:c + 1], hsc[:, 511:512]), waits=[sev], inc="qv")
                        if t0 >= KV0:
                            bev = P.emit("vector", lambda e: e.tensor_copy(hsb[:], hsc[:]),
                                         waits=[sev, P.ev("dg")], inc="qv")
                            P.dma("gpsimd",
                                  hst_hbm[c * 128:(c + 1) * 128, t0 - KV0:t0 - KV0 + TB],
                                  hsb[:], waits=[bev])
                    hc_free[half] = P.ev("qt")
                P.barrier()

            if debug_phase == "A2":
                tmp = SB(pha, "dbg2", [128, ZLEN], BF16)
                for m in range(HC):
                    ev = P.dma("sync", tmp[:], hst_hbm[m * 128:(m + 1) * 128, :])
                    P.dma("sync", taps["t_hst"][:, m * ZLEN:(m + 1) * ZLEN], tmp[:], waits=[ev])
                    ev = P.dma("sync", tmp[:], gate_hbm[m * 128:(m + 1) * 128, :])
                    P.dma("sync", taps["t_gate"][:, m * ZLEN:(m + 1) * ZLEN], tmp[:], waits=[ev])
                _finish(P, nc, es)
                return nc

        z = SB(zstack, "z", [128, DC * OWN], F32)
        zh = SB(zstack, "zh", [128, DC * KV0], BF16)

        def dump_z():
            P.dma("sync", taps["t_z"], z[:], waits=P.all_ev())
            P.dma("sync", taps["t_zh"], zh[:], waits=P.all_ev())
            _finish(P, nc, es)

        with contextlib.ExitStack() as phb, contextlib.ExitStack() as psb:
            wo_sb = SB(phb, "wo_sb", [128, HC * D], BF16)
            gh_sb = SB(phb, "gh_sb", [128, HC * 1024], BF16)
            gl_sb = SB(phb, "gl_sb", [128, 1024], BF16)
            gld = SB(phb, "gld", [128, 2 * 1024], BF16)
            hld = SB(phb, "hld", [128, 2 * 1024], BF16)
            xb2 = SB(phb, "xb2", [128, 2 * D], F32)
            ztok = SB(phb, "ztok", [128, D], F32)
            psz = PS(psb, "psz", [128, 1024], F32)
            pw0 = PS(psb, "pw0", [128, 1024], F32)

            woev = load_wT("gpsimd", wo_sb, woutT, HID, D)
            TCH = 1024
            for ci in range(ZLEN // TCH):
                w0 = KV0 + ci * TCH
                for c in range(HC):
                    half = c % 2
                    gev = P.dma("sync", gld[:, half * 1024:(half + 1) * 1024],
                                gate_hbm[c * 128:(c + 1) * 128, ci * TCH:(ci + 1) * TCH],
                                waits=[P.ev("qs")])
                    hev = P.dma("sync", hld[:, half * 1024:(half + 1) * 1024],
                                hst_hbm[c * 128:(c + 1) * 128, ci * TCH:(ci + 1) * TCH],
                                waits=[P.ev("qv")])
                    sev = P.emit("scalar", lambda e, half=half: e.activation(
                        gl_sb[:], gld[:, half * 1024:(half + 1) * 1024], AF.Gelu),
                        waits=[gev, P.ev("qv")], inc="qs")
                    P.emit("vector", lambda e, c=c, half=half: e.tensor_mul(
                        gh_sb[:, c * 1024:(c + 1) * 1024], gl_sb[:],
                        hld[:, half * 1024:(half + 1) * 1024]),
                        waits=[sev, hev], inc="qv")
                gh_ready = P.ev("qv")
                for i in range(TCH // 128):
                    wp = w0 + i * 128
                    buf = xb2[:, (i % 2) * D:(i % 2) * D + D]
                    xev = P.dma("sync", buf, xwin[wp:wp + 128, :], waits=[P.ev("qs")])
                    agg = stats_on(buf, waits=[xev])
                    rse = mk_scale(1, 3, 5, True, agg)
                    nme = neg_m_rs(5, 7, rse)
                    aev = P.emit("scalar", lambda e, buf=buf: e.activation(
                        ztok[:], buf, AF.Identity, bias=stat[:, 7:8], scale=stat[:, 5:6]),
                        waits=[nme], inc="qs")
                    tev = transpose_tokmaj_to_fm(ztok, psz, True,
                                                 extra_waits=[aev, woev, P.ev("qv")])
                    sls, own = z_slices(wp)
                    tgt = (z if own else zh)
                    w = OWN if own else KV0
                    base = (wp - PRE) if own else (wp - KV0)
                    P.emit("vector", lambda e, tgt=tgt, w=w, base=base: e.tensor_copy(
                        _r(tgt[:], "p (c n) -> p c n", c=DC)[:, :, base:base + 128],
                        _r(psz[:], "p (c n) -> p c n", c=DC)),
                        waits=[tev], inc="qv")
                z0_ready = P.ev("qv")
                for dm in range(DC):
                    for nb in range(2):
                        for kc in range(HC):
                            P.emit("tensor", lambda e, dm=dm, nb=nb, kc=kc: e.matmul(
                                pw0[:, nb * 512:(nb + 1) * 512],
                                wo_sb[:, kc * D + dm * 128: kc * D + dm * 128 + 128],
                                gh_sb[:, kc * 1024 + nb * 512: kc * 1024 + (nb + 1) * 512],
                                start=(kc == 0), stop=(kc == HC - 1)),
                                waits=([gh_ready, z0_ready, P.ev("qv")]
                                       if (kc == 0 and nb == 0) else []),
                                inc="qt" if (kc == HC - 1 and nb == 1) else None)
                    mmev = P.ev("qt")
                    own = w0 >= PRE
                    zt = (z[:, dm * OWN + (w0 - PRE): dm * OWN + (w0 - PRE) + TCH] if own
                          else zh[:, dm * KV0 + (w0 - KV0): dm * KV0 + (w0 - KV0) + TCH])
                    P.emit("vector", lambda e, zt=zt: e.tensor_add(zt, zt, pw0[:]),
                           waits=[mmev], inc="qv")
            P.barrier()

        if debug_phase == "B":
            dump_z()
            return nc

        def gmlp(gT, sT, gcol, rng_start, rng_len):
            with contextlib.ExitStack() as phc, contextlib.ExitStack() as psc:
                u_c = SB(phc, "u_c", [128, DC * 1024], BF16)
                utk = SB(phc, "utk", [128, D], BF16)
                gel = SB(phc, "gel", [128, 1024], BF16)
                ghm = SB(phc, "ghm", [128, 16 * 1024], BF16)
                wmA = SB(phc, "wmA", [128, 2 * DC * 128], BF16)
                wmB = SB(phc, "wmB", [128, 2 * DC * 128], BF16)
                wsA = SB(phc, "wsA", [128, 2 * 16 * 128], BF16)
                ptk = PS(psc, "ptk", [128, 1024], F32)
                ptb = PS(psc, "ptb", [128, 1024], BF16)
                pg = PS(psc, "pg", [128, 1024], F32)
                ph_ = PS(psc, "ph_", [128, 1024], F32)

                for ci in range(rng_len // 1024):
                    w0 = rng_start + ci * 1024
                    for i in range(8):
                        wp = w0 + i * 128
                        sls, own = z_slices(wp)
                        pin = ptk if own else ptb
                        tev = transpose_fm_to_tokmaj(sls, pin, own,
                                                     extra_waits=[P.ev("qv"), P.ev("qs")])
                        agg = stats_on(pin[:], waits=[tev])
                        e1 = P.emit("vector", lambda e: e.scalar_tensor_tensor(
                            stat[:, 2:3], stat[:, 0:1], 1.0, stat[:, 0:1], ALU.mult, ALU.mult),
                            waits=[agg], inc="qv")
                        e2 = P.emit("vector", lambda e: e.tensor_add(
                            stat[:, 2:3], stat[:, 2:3], stat[:, 1:2]), waits=[e1], inc="qv")
                        e4 = mk_scale(2, 3, 4, None, e2)
                        aev = P.emit("scalar", lambda e, pin=pin: e.activation(
                            utk[:], pin[:], AF.Identity, scale=stat[:, 4:5]),
                            waits=[e4], inc="qs")
                        tev = transpose_tokmaj_to_fm(utk, ptb, False,
                                                     extra_waits=[aev, P.ev("qv")])
                        P.emit("vector", lambda e, i=i: e.tensor_copy(
                            _r(u_c[:], "p (c n) -> p c n", c=DC)[:, :, i * 128:(i + 1) * 128],
                            _r(ptb[:], "p (c n) -> p c n", c=DC)),
                            waits=[tev], inc="qv")
                    for c in range(DC):
                        P.emit("vector", lambda e, c=c: e.tensor_scalar_mul(
                            u_c[:, c * 1024:(c + 1) * 1024], u_c[:, c * 1024:(c + 1) * 1024],
                            gn_sb[:, gcol * DC + c:gcol * DC + c + 1]),
                            waits=[P.ev("qv")], inc="qv")
                    u_ready = P.ev("qv")

                    wflip = 0
                    wfree = {0: None, 1: None}
                    for m in range(16):
                        wA = wmA[:, wflip * DC * 128:(wflip + 1) * DC * 128]
                        wB = wmB[:, wflip * DC * 128:(wflip + 1) * DC * 128]
                        evA = P.dma("gpsimd", _r(wA, "p (c n) -> p c n", c=DC),
                                    _r(gT[:, m * 128:(m + 1) * 128], "(c p) n -> p c n", p=128),
                                    waits=[wfree[wflip]])
                        evB = P.dma("gpsimd", _r(wB, "p (c n) -> p c n", c=DC),
                                    _r(gT[:, GH + m * 128: GH + (m + 1) * 128],
                                       "(c p) n -> p c n", p=128))
                        for nb in range(2):
                            for kc in range(DC):
                                P.emit("tensor", lambda e, kc=kc, nb=nb, wA=wA: e.matmul(
                                    pg[:, nb * 512:(nb + 1) * 512],
                                    wA[:, kc * 128:(kc + 1) * 128],
                                    u_c[:, kc * 1024 + nb * 512: kc * 1024 + (nb + 1) * 512],
                                    start=(kc == 0), stop=(kc == DC - 1)),
                                    waits=([u_ready, evA, P.ev("qs")]
                                           if (kc == 0 and nb == 0) else []),
                                    inc="qt" if (kc == DC - 1 and nb == 1) else None)
                        for nb in range(2):
                            for kc in range(DC):
                                P.emit("tensor", lambda e, kc=kc, nb=nb, wB=wB: e.matmul(
                                    ph_[:, nb * 512:(nb + 1) * 512],
                                    wB[:, kc * 128:(kc + 1) * 128],
                                    u_c[:, kc * 1024 + nb * 512: kc * 1024 + (nb + 1) * 512],
                                    start=(kc == 0), stop=(kc == DC - 1)),
                                    waits=([evB, P.ev("qv")] if (kc == 0 and nb == 0) else []),
                                    inc="qt" if (kc == DC - 1 and nb == 1) else None)
                        mmev = P.ev("qt")
                        gev = P.emit("scalar", lambda e: e.activation(gel[:], pg[:], AF.Gelu),
                                     waits=[mmev], inc="qs")
                        mev = P.emit("vector", lambda e, m=m: e.tensor_mul(
                            ghm[:, m * 1024:(m + 1) * 1024], gel[:], ph_[:]),
                            waits=[gev], inc="qv")
                        wfree[wflip] = mev
                        wflip ^= 1
                    gh_ready = P.ev("qv")

                    sflip = 0
                    sfree = {0: None, 1: None}
                    for dm in range(DC):
                        wS = wsA[:, sflip * 16 * 128:(sflip + 1) * 16 * 128]
                        evS = P.dma("gpsimd", _r(wS, "p (c n) -> p c n", c=16),
                                    _r(sT[:, dm * 128:(dm + 1) * 128], "(c p) n -> p c n", p=128),
                                    waits=[sfree[sflip]])
                        for nb in range(2):
                            for kc in range(16):
                                P.emit("tensor", lambda e, kc=kc, nb=nb, wS=wS: e.matmul(
                                    pg[:, nb * 512:(nb + 1) * 512],
                                    wS[:, kc * 128:(kc + 1) * 128],
                                    ghm[:, kc * 1024 + nb * 512: kc * 1024 + (nb + 1) * 512],
                                    start=(kc == 0), stop=(kc == 15)),
                                    waits=([gh_ready, evS, P.ev("qv"), P.ev("qs")]
                                           if (kc == 0 and nb == 0) else []),
                                    inc="qt" if (kc == 15 and nb == 1) else None)
                        mmev = P.ev("qt")
                        own = w0 >= PRE
                        zt = (z[:, dm * OWN + (w0 - PRE): dm * OWN + (w0 - PRE) + 1024] if own
                              else zh[:, dm * KV0 + (w0 - KV0): dm * KV0 + (w0 - KV0) + 1024])
                        sfree[sflip] = P.emit("vector", lambda e, zt=zt: e.tensor_add(
                            zt, zt, pg[:]), waits=[mmev], inc="qv")
                        sflip ^= 1
                P.barrier()

        gmlp(g1gT, g1sT, 1, KV0, ZLEN)
        if debug_phase == "C":
            dump_z()
            return nc

        with contextlib.ExitStack() as phd:
            wq_sb = SB(phd, "wq_sb", [128, DC * D], BF16)
            wkv_sb = SB(phd, "wkv_sb", [128, DC * 256], BF16)
            kfull = SB(phd, "kfull", [128, 3200], BF16)
            vfull = SB(phd, "vfull", [128, 3200], BF16)
            vtok = SB(phd, "vtok", [128, 25 * 128], BF16)
            qrot = SB(phd, "qrot", [128, QH * 1024], BF16)
            u_c = SB(phd, "u_cD", [128, DC * 1024], BF16)
            utk = SB(phd, "utkD", [128, D], BF16)
            cosl = SB(phd, "cosl", [128, ZLEN], BF16)
            sinl = SB(phd, "sinl", [128, ZLEN], BF16)
            tsh = SB(phd, "tsh", [128, 1024], BF16)
            tmix = SB(phd, "tmix", [128, 1024], BF16)
            p_sb = SB(phd, "p_sb", [128, BAND], BF16)
            pt_sb = SB(phd, "pt_sb", [128, BAND], BF16)
            mstage = SB(phd, "mstage", [1, BAND], BF16)
            mc_sb = SB(phd, "mc_sb", [128, BAND], BF16)
            otile = SB(phd, "otile", [128, QH * 128], BF16)
            wod = SB(phd, "wod", [128, 2 * QH * 128], BF16)

            wqe = load_wT("gpsimd", wq_sb, wqT, D, D)
            wkve = load_wT("gpsimd", wkv_sb, wkvT, D, 2 * HD)
            P.dma("sync", cosl[:], costab)
            P.dma("sync", sinl[:], sintab)
            P.dma("sync", mc_sb[:], mconst)
            tabev = P.ev("ds")
            for c in range(DC):
                P.emit("vector", lambda e, c=c: e.tensor_scalar_mul(
                    wq_sb[:, c * D:(c + 1) * D], wq_sb[:, c * D:(c + 1) * D],
                    gn_sb[:, 2 * DC + c:2 * DC + c + 1]), waits=[wqe], inc="qv")
                P.emit("vector", lambda e, c=c: e.tensor_scalar_mul(
                    wkv_sb[:, c * 256:(c + 1) * 256], wkv_sb[:, c * 256:(c + 1) * 256],
                    gn_sb[:, 2 * DC + c:2 * DC + c + 1]), waits=[wkve], inc="qv")
            w_ready = P.ev("qv")
            P.emit("gpsimd", lambda e: e.memset(kfull[:, 3072:3200], 0.0), inc="qg")
            P.emit("gpsimd", lambda e: e.memset(vfull[:, 3072:3200], 0.0), inc="qg")
            pad_ev = P.ev("qg")

            shmask = []
            for i in range(0, 32, 2):
                shmask += [i + 1, i]

            def rms_u_chunk(w0):
              with contextlib.ExitStack() as psn:
                psc_ptk = PS(psn, "ptk_n", [128, 1024], F32)
                psc_ptb = PS(psn, "ptb_n", [128, 1024], BF16)
                for i in range(8):
                    wp = w0 + i * 128
                    sls, own = z_slices(wp)
                    pin = psc_ptk if own else psc_ptb
                    tev = transpose_fm_to_tokmaj(sls, pin, own,
                                                 extra_waits=[P.ev("qv"), P.ev("qs")])
                    agg = stats_on(pin[:], waits=[tev])
                    e1 = P.emit("vector", lambda e: e.scalar_tensor_tensor(
                        stat[:, 2:3], stat[:, 0:1], 1.0, stat[:, 0:1], ALU.mult, ALU.mult),
                        waits=[agg], inc="qv")
                    e2 = P.emit("vector", lambda e: e.tensor_add(
                        stat[:, 2:3], stat[:, 2:3], stat[:, 1:2]), waits=[e1], inc="qv")
                    e4 = mk_scale(2, 3, 4, None, e2)
                    aev = P.emit("scalar", lambda e, pin=pin: e.activation(
                        utk[:], pin[:], AF.Identity, scale=stat[:, 4:5]), waits=[e4], inc="qs")
                    tev = transpose_tokmaj_to_fm(utk, psc_ptb, False,
                                                 extra_waits=[aev, P.ev("qv")])
                    P.emit("vector", lambda e, i=i: e.tensor_copy(
                        _r(u_c[:], "p (c n) -> p c n", c=DC)[:, :, i * 128:(i + 1) * 128],
                        _r(psc_ptb[:], "p (c n) -> p c n", c=DC)),
                        waits=[tev], inc="qv")
                return P.ev("qv")

            with contextlib.ExitStack() as psd:
                pkv = PS(psd, "pkv", [128, 1024], F32)
                for ci in range(3):
                    w0 = KV0 + ci * 1024
                    uev = rms_u_chunk(w0)
                    for hm, dst in ((0, kfull), (1, vfull)):
                        for nb in range(2):
                            for kc in range(DC):
                                P.emit("tensor", lambda e, kc=kc, nb=nb, hm=hm: e.matmul(
                                    pkv[:, nb * 512:(nb + 1) * 512],
                                    wkv_sb[:, kc * 256 + hm * 128: kc * 256 + hm * 128 + 128],
                                    u_c[:, kc * 1024 + nb * 512: kc * 1024 + (nb + 1) * 512],
                                    start=(kc == 0), stop=(kc == DC - 1)),
                                    waits=([uev, w_ready, P.ev("qv")]
                                           if (kc == 0 and nb == 0) else []),
                                    inc="qt" if (kc == DC - 1 and nb == 1) else None)
                        mmev = P.ev("qt")
                        cols = slice(ci * 1024, (ci + 1) * 1024)
                        if hm == 1:
                            P.emit("vector", lambda e, cols=cols: e.tensor_copy(
                                vfull[:, cols], pkv[:]), waits=[mmev], inc="qv")
                        else:
                            e1 = P.emit("vector", lambda e: e.tensor_copy(tmix[:], pkv[:]),
                                        waits=[mmev, tabev], inc="qv")
                            e2 = P.emit("vector", lambda e: e.stream_shuffle(
                                tsh[:], tmix[:], shmask), waits=[e1], inc="qv")
                            e3 = P.emit("vector", lambda e, cols=cols: e.tensor_mul(
                                kfull[:, cols], tmix[:], cosl[:, cols]), waits=[e2], inc="qv")
                            e4 = P.emit("vector", lambda e, cols=cols: e.tensor_mul(
                                tsh[:], tsh[:], sinl[:, cols]), waits=[e3], inc="qv")
                            P.emit("vector", lambda e, cols=cols: e.tensor_add(
                                kfull[:, cols], kfull[:, cols], tsh[:]), waits=[e4], inc="qv")
                kv_ready = P.ev("qv")
                with contextlib.ExitStack() as psv:
                    pvt = PS(psv, "pvt", [128, 128], BF16)
                    for i in range(25):
                        P.emit("tensor", lambda e, i=i: e.matmul(
                            pvt[:], vfull[:, i * 128:(i + 1) * 128], idb_sb[:],
                            is_transpose=True, start=True, stop=True),
                            waits=[kv_ready, pad_ev, P.ev("qv")], inc="qt")
                        P.emit("vector", lambda e, i=i: e.tensor_copy(
                            vtok[:, i * 128:(i + 1) * 128], pvt[:]),
                            waits=[P.ev("qt")], inc="qv")
                vtok_ready = P.ev("qv")
                if debug_phase == "Dkv":
                    P.dma("sync", taps["t_k"], kfull[:], waits=[kv_ready, vtok_ready])
                    P.dma("sync", taps["t_v"], vfull[:], waits=[kv_ready])
                    _finish(P, nc, es)
                    return nc

            for qc in range(2):
                w0 = PRE + qc * 1024
                uev = rms_u_chunk(w0)
                with contextlib.ExitStack() as psd:
                    pq = PS(psd, "pq", [128, 512], F32)
                    psS = PS(psd, "psS", [128, 1536], F32)
                    psPT = PS(psd, "psPT", [128, 1536], BF16)
                    psO = PS(psd, "psO", [128, 128], F32)
                    pop = PS(psd, "pop", [128, 128], F32)
                    for hq in range(QH):
                        for nb in range(2):
                            for kc in range(DC):
                                P.emit("tensor", lambda e, kc=kc, nb=nb, hq=hq: e.matmul(
                                    pq[:],
                                    wq_sb[:, kc * D + hq * 128: kc * D + hq * 128 + 128],
                                    u_c[:, kc * 1024 + nb * 512: kc * 1024 + (nb + 1) * 512],
                                    start=(kc == 0), stop=(kc == DC - 1)),
                                    waits=([uev, P.ev("qv")] if kc == 0 else []),
                                    inc="qt" if kc == DC - 1 else None)
                            P.emit("vector", lambda e, nb=nb: e.tensor_copy(
                                tmix[:, nb * 512:(nb + 1) * 512], pq[:]),
                                waits=[P.ev("qt")], inc="qv")
                        mmev = P.ev("qv")
                        qcols = slice(hq * 1024, (hq + 1) * 1024)
                        tcols = slice(1024 + qc * 1024, 1024 + (qc + 1) * 1024)
                        e1 = mmev
                        e2 = P.emit("vector", lambda e: e.stream_shuffle(tsh[:], tmix[:], shmask),
                                    waits=[e1], inc="qv")
                        e3 = P.emit("vector", lambda e, qcols=qcols, tcols=tcols: e.tensor_mul(
                            qrot[:, qcols], tmix[:], cosl[:, tcols]), waits=[e2], inc="qv")
                        e4 = P.emit("vector", lambda e, tcols=tcols: e.tensor_mul(
                            tsh[:], tsh[:], sinl[:, tcols]), waits=[e3], inc="qv")
                        P.emit("vector", lambda e, qcols=qcols: e.tensor_add(
                            qrot[:, qcols], qrot[:, qcols], tsh[:]), waits=[e4], inc="qv")
                    q_ready = P.ev("qv")

                    for qi in range(8):
                        qt = qc * 8 + qi
                        use_bias = qt < 8
                        bev = None
                        if use_bias:
                            bev = P.dma("sync", mstage[:], mbias[qt:qt + 1, :],
                                        waits=[P.ev("qt")])
                        for hq in range(QH):
                            segs = ((0, 512), (512, 512), (1024, 256))
                            for c0, cn in segs:
                                P.emit("tensor", lambda e, qt=qt, hq=hq, c0=c0, cn=cn, qi=qi: e.matmul(
                                    psS[:, c0:c0 + cn],
                                    qrot[:, hq * 1024 + qi * 128: hq * 1024 + qi * 128 + 128],
                                    kfull[:, qt * 128 + c0: qt * 128 + c0 + cn],
                                    start=True, stop=False),
                                    waits=([q_ready, vtok_ready, bev, P.ev("qs"), P.ev("qv")]
                                           if c0 == 0 else []))
                                P.emit("tensor", lambda e, c0=c0, cn=cn: e.matmul(
                                    psS[:, c0:c0 + cn], idb_sb[:], mc_sb[:, c0:c0 + cn],
                                    start=False, stop=(not use_bias)),
                                    inc=(None if use_bias else ("qt" if c0 == 1024 else None)))
                                if use_bias:
                                    P.emit("tensor", lambda e, c0=c0, cn=cn: e.matmul(
                                        psS[:, c0:c0 + cn], ones_sb[:],
                                        mstage[:, c0:c0 + cn], start=False, stop=True),
                                        inc="qt" if c0 == 1024 else None)
                            mmev = P.ev("qt")
                            e1 = P.emit("vector", lambda e: e.reduce_max(
                                stat[:, 0:1], psS[:, 0:BAND], axis=AX.X, negate=True),
                                waits=[mmev], inc="qv")
                            e2 = P.emit("vector", lambda e: e.tensor_scalar_mul(
                                stat[:, 1:2], stat[:, 0:1], SCL), waits=[e1], inc="qv")
                            e3 = P.emit("scalar", lambda e: e.activation(
                                p_sb[:], psS[:, 0:BAND], AF.Exp, bias=stat[:, 1:2], scale=SCL),
                                waits=[e2], inc="qs")
                            e4 = P.emit("vector", lambda e: e.reduce_sum(
                                stat[:, 2:3], p_sb[:], axis=AX.X), waits=[e3], inc="qv")
                            e5 = P.emit("vector", lambda e: e.reciprocal(
                                stat[:, 3:4], stat[:, 2:3]), waits=[e4], inc="qv")
                            e6 = P.emit("vector", lambda e: e.tensor_scalar_mul(
                                p_sb[:], p_sb[:], stat[:, 3:4]), waits=[e5], inc="qv")
                            for kt in range(10):
                                P.emit("tensor", lambda e, kt=kt: e.matmul(
                                    psPT[:, kt * 128:(kt + 1) * 128],
                                    p_sb[:, kt * 128:(kt + 1) * 128], idb_sb[:],
                                    is_transpose=True, start=(kt in (0, 8)), stop=(kt == 9)),
                                    waits=([e6] if kt == 0 else []),
                                    inc="qt" if kt == 9 else None)
                            tev = P.ev("qt")
                            e7 = P.emit("vector", lambda e: e.tensor_copy(
                                pt_sb[:], psPT[:, 0:BAND]), waits=[tev], inc="qv")
                            for kt in range(10):
                                P.emit("tensor", lambda e, kt=kt, qt=qt: e.matmul(
                                    psO[:], vtok[:, (qt + kt) * 128:(qt + kt + 1) * 128],
                                    pt_sb[:, kt * 128:(kt + 1) * 128],
                                    start=(kt == 0), stop=(kt == 9)),
                                    waits=([e7] if kt == 0 else []),
                                    inc="qt" if kt == 9 else None)
                            pvev = P.ev("qt")
                            P.emit("vector", lambda e, hq=hq: e.tensor_copy(
                                otile[:, hq * 128:(hq + 1) * 128], psO[:]),
                                waits=[pvev], inc="qv")
                        o_ready = P.ev("qv")
                        for dm in range(DC):
                            evw = P.dma("gpsimd",
                                        _r(wod[:, (dm % 2) * QH * 128:(dm % 2) * QH * 128 + QH * 128],
                                           "p (c n) -> p c n", c=QH),
                                        _r(woT[:, dm * 128:(dm + 1) * 128],
                                           "(c p) n -> p c n", p=128),
                                        waits=[P.ev("qt")])
                            for kc in range(QH):
                                P.emit("tensor", lambda e, kc=kc, dm=dm: e.matmul(
                                    pop[:],
                                    wod[:, (dm % 2) * QH * 128 + kc * 128:
                                        (dm % 2) * QH * 128 + kc * 128 + 128],
                                    otile[:, kc * 128:(kc + 1) * 128],
                                    start=(kc == 0), stop=(kc == QH - 1)),
                                    waits=([o_ready, evw, P.ev("qv")] if kc == 0 else []),
                                    inc="qt" if kc == QH - 1 else None)
                            mmev = P.ev("qt")
                            zt = z[:, dm * OWN + qt * 128: dm * OWN + qt * 128 + 128]
                            P.emit("vector", lambda e, zt=zt: e.tensor_add(zt, zt, pop[:]),
                                   waits=[mmev], inc="qv")
            P.barrier()

        if debug_phase == "D":
            dump_z()
            return nc

        gmlp(g2gT, g2sT, 3, PRE, OWN)
        if debug_phase == "E":
            dump_z()
            return nc

        with contextlib.ExitStack() as phf, contextlib.ExitStack() as psf:
            xb3 = SB(phf, "xb3", [128, 2 * D], F32)
            xmt = SB(phf, "xmt", [128, D], F32)
            uft = SB(phf, "uft", [128, D], BF16)
            fwg_sb = SB(phf, "fwg_sb", [128, DC * NE], F32)
            bgr_sb = SB(phf, "bgr_sb", [128, NE], F32)
            io8_sb = SB(phf, "io8_sb", [128, NE], F32)
            lgt = SB(phf, "lgt", [128, NE], F32)
            oh_t = SB(phf, "oh_t", [8, OWN], F32)
            onest = SB(phf, "onest", [8, OWN], F32)
            post = SB(phf, "post", [8, OWN], F32)
            mx8 = SB(phf, "mx8", [128, 8], F32)
            ix8 = SB(phf, "ix8", [128, 8], U32)
            ohm = SB(phf, "ohm", [128, 8], F32)
            posm = SB(phf, "posm", [128, 8], F32)
            selc = SB(phf, "selc", [128, 8], F32)
            csr = SB(phf, "csr", [1, 8], F32)
            mrow = SB(phf, "mrow", [1, 128], F32)
            onesf = SB(phf, "onesf", [128, 1], F32)
            ptk = PS(psf, "ptkF", [128, 1024], F32)
            pl = PS(psf, "plF", [128, 512], F32)
            pcs = PS(psf, "pcsF", [128, 256], F32)

            P.dma("sync", _r(fwg_sb[:], "p (c n) -> p c n", c=DC),
                  _r(fwg, "(c p) n -> p c n", p=128))
            P.dma("sync", bgr_sb[:], bgrep)
            P.dma("sync", io8_sb[:], iota8)
            fev = P.ev("ds")
            P.emit("gpsimd", lambda e: e.memset(onest[:], 1.0), inc="qg")
            P.emit("gpsimd", lambda e: e.memset(onesf[:], 1.0), inc="qg")
            gz = P.ev("qg")
            for kc in range(DC):
                P.emit("tensor", lambda e, kc=kc: e.matmul(
                    pcs[0:8, 0:1], fwg_sb[:, kc * NE:(kc + 1) * NE], onesf[:],
                    start=(kc == 0), stop=(kc == DC - 1)),
                    waits=([fev, gz] if kc == 0 else []),
                    inc="qt" if kc == DC - 1 else None)
            P.emit("vector", lambda e: e.tensor_copy(stat[0:8, 6:7], pcs[0:8, 0:1]),
                   waits=[P.ev("qt")], inc="qv")
            P.emit("tensor", lambda e: e.matmul(
                pcs[0:1, 8:16], stat[0:8, 6:7], idf_sb[0:8, 0:8],
                is_transpose=True, start=True, stop=True),
                waits=[P.ev("qv")], inc="qt")
            P.emit("vector", lambda e: e.tensor_copy(csr[:], pcs[0:1, 8:16]),
                   waits=[P.ev("qt")], inc="qv")
            P.emit("vector", lambda e: e.tensor_scalar_mul(csr[:], csr[:], -1.0),
                   waits=[P.ev("qv")], inc="qv")
            cs_ready = P.ev("qv")

            for i in range(16):
                buf = xb3[:, (i % 2) * D:(i % 2) * D + D]
                xev = P.dma("sync", buf, xwin[PRE + i * 128: PRE + (i + 1) * 128, :],
                            waits=[P.ev("qt"), P.ev("qv")])
                tev = transpose_fm_to_tokmaj(
                    [z[:, c * OWN + i * 128: c * OWN + i * 128 + 128] for c in range(DC)],
                    ptk, True, extra_waits=[P.ev("qv"), P.ev("qs")])
                xmev = P.emit("vector", lambda e, buf=buf: e.tensor_add(xmt[:], buf, ptk[:]),
                              waits=[tev, xev], inc="qv")
                P.dma("sync", xmid_hbm[i * 128:(i + 1) * 128, :], xmt[:], waits=[xmev])
                agg = stats_on(xmt[:], waits=[xmev])
                rse = mk_scale(1, 3, 5, True, agg)
                nme = neg_m_rs(5, 7, rse)
                tev = transpose_tokmaj_to_fm(buf, ptk, True, extra_waits=[xmev, nme])
                zfm = P.emit("vector", lambda e, i=i: e.tensor_add(
                    _r(z[:], "p (c n) -> p c n", c=DC)[:, :, i * 128:(i + 1) * 128],
                    _r(z[:], "p (c n) -> p c n", c=DC)[:, :, i * 128:(i + 1) * 128],
                    _r(ptk[:], "p (c n) -> p c n", c=DC)),
                    waits=[tev], inc="qv")
                P.emit("tensor", lambda e: e.matmul(
                    pcs[0:1, 128:256], stat[:, 0:1], idf_sb[:],
                    is_transpose=True, start=True, stop=True),
                    waits=[nme, zfm], inc="qt")
                mre = P.emit("vector", lambda e: e.tensor_copy(mrow[:], pcs[0:1, 128:256]),
                             waits=[P.ev("qt")], inc="qv")
                for kc in range(DC):
                    P.emit("tensor", lambda e, kc=kc, i=i: e.matmul(
                        pl[:, 0:NE], z[:, kc * OWN + i * 128: kc * OWN + i * 128 + 128],
                        fwg_sb[:, kc * NE:(kc + 1) * NE],
                        start=(kc == 0), stop=False),
                        waits=([zfm, cs_ready, mre] if kc == 0 else []))
                P.emit("tensor", lambda e: e.matmul(
                    pl[:, 0:NE], mrow[:], csr[:], start=False, stop=True), inc="qt")
                lgev = P.ev("qt")
                e1 = P.emit("vector", lambda e: e.tensor_scalar_mul(
                    lgt[:], pl[:, 0:NE], stat[:, 5:6]), waits=[lgev], inc="qv")
                e2 = P.emit("vector", lambda e: e.tensor_add(lgt[:], lgt[:], bgr_sb[:]),
                            waits=[e1, fev], inc="qv")
                if debug_phase == "F":
                    P.dma("sync", taps["t_logits"][:, i * NE:(i + 1) * NE], lgt[:], waits=[e2])
                e3 = P.emit("vector", lambda e: e.max(mx8[:], lgt[:]), waits=[e2], inc="qv")
                e4 = P.emit("vector", lambda e: e.max_index(ix8[:], mx8[:], lgt[:]),
                            waits=[e3], inc="qv")
                e5 = P.emit("vector", lambda e, i=i: e.tensor_copy(idxf[:, i:i + 1], ix8[:, 0:1]),
                            waits=[e4], inc="qv")
                e6 = P.emit("vector", lambda e, i=i: e.tensor_single_scalar(
                    ohm[:], io8_sb[:], idxf[:, i:i + 1], ALU.is_equal),
                    waits=[e5], inc="qv")
                P.emit("tensor", lambda e: e.matmul(
                    pcs[0:8, 0:128], ohm[:], idf_sb[:], is_transpose=True,
                    start=True, stop=True), waits=[e6], inc="qt")
                P.emit("vector", lambda e, i=i: e.tensor_copy(
                    oh_t[:, i * 128:(i + 1) * 128], pcs[0:8, 0:128]),
                    waits=[P.ev("qt")], inc="qv")
            P.barrier()

            scev = P.emit("vector", lambda e: e.tensor_tensor_scan(
                post[:], onest[:], oh_t[:], 0.0, ALU.mult, ALU.add),
                waits=[P.ev("qv"), gz], inc="qv")
            for i in range(16):
                P.emit("tensor", lambda e, i=i: e.matmul(
                    pcs[0:128, 0:8], post[:, i * 128:(i + 1) * 128], idf_sb[0:8, 0:8],
                    is_transpose=True, start=True, stop=True),
                    waits=[scev, P.ev("qv")], inc="qt")
                e0 = P.emit("vector", lambda e: e.tensor_copy(posm[:], pcs[0:128, 0:8]),
                            waits=[P.ev("qt")], inc="qv")
                e1 = P.emit("vector", lambda e, i=i: e.tensor_single_scalar(
                    ohm[:], io8_sb[:], idxf[:, i:i + 1], ALU.is_equal), waits=[e0], inc="qv")
                e2 = P.emit("vector", lambda e: e.tensor_mul(posm[:], posm[:], ohm[:]),
                            waits=[e1], inc="qv")
                e3 = P.emit("vector", lambda e: e.reduce_sum(selc[:, 0:1], posm[:], axis=AX.X),
                            waits=[e2], inc="qv")
                e4 = P.emit("vector", lambda e, i=i: e.scalar_tensor_tensor(
                    selc[:, 1:2], idxf[:, i:i + 1], float(CAP), selc[:, 0:1],
                    ALU.mult, ALU.add), waits=[e3], inc="qv")
                e5 = P.emit("vector", lambda e: e.tensor_scalar_add(
                    selc[:, 1:2], selc[:, 1:2], -1.0), waits=[e4], inc="qv")
                e6 = P.emit("vector", lambda e: e.tensor_single_scalar(
                    selc[:, 2:3], selc[:, 0:1], float(CAP) + 0.5, ALU.is_gt),
                    waits=[e5], inc="qv")
                e7 = P.emit("vector", lambda e: e.scalar_tensor_tensor(
                    selc[:, 3:4], selc[:, 1:2], -1.0, selc[:, 2:3], ALU.mult, ALU.mult),
                    waits=[e6], inc="qv")
                e8 = P.emit("vector", lambda e: e.scalar_tensor_tensor(
                    selc[:, 4:5], selc[:, 2:3], float(TRASH), selc[:, 3:4],
                    ALU.mult, ALU.add), waits=[e7], inc="qv")
                e9 = P.emit("vector", lambda e: e.tensor_add(
                    selc[:, 1:2], selc[:, 1:2], selc[:, 4:5]), waits=[e8], inc="qv")
                e10 = P.emit("vector", lambda e, i=i: e.tensor_copy(
                    slot_i[:, i:i + 1], selc[:, 1:2]), waits=[e9], inc="qv")
                if debug_phase == "F":
                    P.dma("sync", taps["t_slot"][:, i:i + 1], slot_i[:, i:i + 1], waits=[e10],
                          allow_slow_non_contiguous=True)
            slots_ready = P.ev("qv")

            for i in range(16):
                buf = xb3[:, (i % 2) * D:(i % 2) * D + D]
                xev = P.dma("sync", buf, xmid_hbm[i * 128:(i + 1) * 128, :],
                            waits=[slots_ready, P.ev("qs")])
                agg = stats_on(buf, waits=[xev])
                rse = mk_scale(1, 3, 5, True, agg)
                nme = neg_m_rs(5, 7, rse)
                aev = P.emit("scalar", lambda e, buf=buf: e.activation(
                    uft[:], buf, AF.Identity, bias=stat[:, 7:8], scale=stat[:, 5:6]),
                    waits=[nme], inc="qs")
                P.wait("gpsimd", [aev, slots_ready, P.ev("dg")])
                P.emit("gpsimd", lambda e, i=i: e.indirect_dma_start(
                    out=xg_hbm, out_offset=IndirectOffsetOnAxis(slot_i[:, i:i + 1], 0),
                    in_=uft[:], in_offset=None), inc="dg", n=16)
            P.barrier()

            if debug_phase == "F":
                tmp = SB(phf, "dbgf", [128, D], F32)
                for i in range(16):
                    ev = P.dma("sync", tmp[:], xmid_hbm[i * 128:(i + 1) * 128, :])
                    P.dma("sync", taps["t_xmid"][:, i * D:(i + 1) * D], tmp[:], waits=[ev])
                _finish(P, nc, es)
                return nc
        zstack.close()

        with contextlib.ExitStack() as phx, contextlib.ExitStack() as psx:
            xg_t = SB(phx, "xg_t", [128, 2 * D], BF16)
            xg_f = SB(phx, "xg_f", [128, DC * CAP], BF16)
            h_sb = SB(phx, "h_sb", [128, 32 * CAP], BF16)
            w1s_ = SB(phx, "w1s_", [128, 2 * DC * 512], BF16)
            w2c = SB(phx, "w2c", [128, 2 * 32 * 128], BF16)
            y_f = SB(phx, "y_f", [128, DC * CAP], BF16)
            y_t = SB(phx, "y_t", [128, D], BF16)
            ph1 = PS(psx, "ph1", [128, CAP], F32)
            ph2 = PS(psx, "ph2", [128, CAP], F32)
            py = PS(psx, "py", [128, CAP], F32)
            ptf = PS(psx, "ptfF", [128, 1024], BF16)

            for ex in range(NE):
                for i in range(CAP // 128):
                    gev = P.dma("sync", xg_t[:, (i % 2) * D:(i % 2) * D + D],
                                xg_hbm[ex * CAP + i * 128: ex * CAP + (i + 1) * 128, :],
                                waits=[P.ev("qt"), P.ev("qv")])
                    tev = transpose_tokmaj_to_fm(
                        xg_t[:, (i % 2) * D:(i % 2) * D + D], ptf, False, extra_waits=[gev])
                    P.emit("vector", lambda e, i=i: e.tensor_copy(
                        _r(xg_f[:], "p (c n) -> p c n", c=DC)[:, :, i * 128:(i + 1) * 128],
                        _r(ptf[:], "p (c n) -> p c n", c=DC)),
                        waits=[tev], inc="qv")
                xg_ready = P.ev("qv")

                wflip = 0
                wfree = {0: None, 1: None}
                for hb in range(FH // 512):
                    wsl = w1s_[:, wflip * DC * 512:(wflip + 1) * DC * 512]
                    wev = P.dma("gpsimd", _r(wsl, "p (c n) -> p c n", c=DC),
                                _r(fw1[ex, :, hb * 512:(hb + 1) * 512], "(c p) n -> p c n", p=128),
                                waits=[wfree[wflip]])
                    last_mm = None
                    for sub in range(4):
                        hp = ph1 if sub % 2 == 0 else ph2
                        m0 = hb * 4 + sub
                        for nb in range(CAP // 512):
                            for kc in range(DC):
                                P.emit("tensor", lambda e, kc=kc, nb=nb, sub=sub, wsl=wsl,
                                       hp=hp: e.matmul(
                                    hp[:, nb * 512:(nb + 1) * 512],
                                    wsl[:, kc * 512 + sub * 128: kc * 512 + sub * 128 + 128],
                                    xg_f[:, kc * CAP + nb * 512: kc * CAP + (nb + 1) * 512],
                                    start=(kc == 0), stop=(kc == DC - 1)),
                                    waits=([xg_ready, wev, P.ev("qs")]
                                           if (kc == 0 and nb == 0) else []),
                                    inc="qt" if (kc == DC - 1 and nb == CAP // 512 - 1) else None)
                        mmev = P.ev("qt")
                        last_mm = mmev
                        P.emit("scalar", lambda e, m0=m0, hp=hp: e.activation(
                            h_sb[:, m0 * CAP:(m0 + 1) * CAP], hp[:], AF.Gelu_apprx_tanh),
                            waits=[mmev], inc="qs")
                    wfree[wflip] = last_mm
                    wflip ^= 1
                h_ready = P.ev("qs")

                w2flip = 0
                w2free = {0: None, 1: None}
                for dm in range(DC):
                    wsl = w2c[:, w2flip * 32 * 128:(w2flip + 1) * 32 * 128]
                    wev = P.dma("gpsimd", _r(wsl, "p (c n) -> p c n", c=32),
                                _r(fw2[ex, :, dm * 128:(dm + 1) * 128], "(c p) n -> p c n", p=128),
                                waits=[w2free[w2flip]])
                    for nb in range(CAP // 512):
                        for kc in range(32):
                            P.emit("tensor", lambda e, kc=kc, nb=nb, wsl=wsl: e.matmul(
                                py[:, nb * 512:(nb + 1) * 512],
                                wsl[:, kc * 128:(kc + 1) * 128],
                                h_sb[:, kc * CAP + nb * 512: kc * CAP + (nb + 1) * 512],
                                start=(kc == 0), stop=(kc == 31)),
                                waits=([h_ready, wev, P.ev("qs")]
                                       if (kc == 0 and nb == 0) else []),
                                inc="qt" if (kc == 31 and nb == CAP // 512 - 1) else None)
                    mmev = P.ev("qt")
                    w2free[w2flip] = mmev
                    w2flip ^= 1
                    P.emit("scalar", lambda e, dm=dm: e.activation(
                        y_f[:, dm * CAP:(dm + 1) * CAP], py[:], AF.Identity),
                        waits=[mmev], inc="qs")
                yf_ready = P.ev("qs")
                for i in range(CAP // 128):
                    tev = transpose_fm_to_tokmaj(
                        [y_f[:, c * CAP + i * 128: c * CAP + i * 128 + 128] for c in range(DC)],
                        ptf, False, extra_waits=[yf_ready, P.ev("ds")])
                    yev = P.emit("vector", lambda e: e.tensor_copy(y_t[:], ptf[:]),
                                 waits=[tev, P.ev("ds")], inc="qv")
                    P.dma("sync", yg_hbm[ex * CAP + i * 128: ex * CAP + (i + 1) * 128, :],
                          y_t[:], waits=[yev])
            P.barrier()

        with contextlib.ExitStack() as pho:
            xmo = SB(pho, "xmo", [128, D], F32)
            ygo = SB(pho, "ygo", [128, D], BF16)
            oto = SB(pho, "oto", [128, D], F32)
            for i in range(16):
                xev = P.dma("sync", xmo[:], xmid_hbm[i * 128:(i + 1) * 128, :],
                            waits=[P.ev("ds"), P.ev("qv")])
                gev = P.emit("gpsimd", lambda e, i=i: e.indirect_dma_start(
                    out=ygo[:], out_offset=None, in_=yg_hbm,
                    in_offset=IndirectOffsetOnAxis(slot_i[:, i:i + 1], 0)),
                    waits=[P.ev("ds"), P.ev("qv")], inc="dg", n=16)
                aev = P.emit("vector", lambda e: e.tensor_add(oto[:], xmo[:], ygo[:]),
                             waits=[xev, gev], inc="qv")
                P.dma("sync", yout[i * 128:(i + 1) * 128, :], oto[:], waits=[aev])
        _finish(P, nc, es)
    return nc


def _finish(P, nc, es):
    P.barrier()
    blk = es.enter_context(nc.Block())
    for eng in ENGINES:
        fns = P.streams[eng]
        getattr(blk, eng)(lambda e, fns=fns: [f(e) for f in fns])



_NC_CACHE = {}


def _get_nc():
    key = (DEBUG_PHASE, STRICT)
    if key not in _NC_CACHE:
        _NC_CACHE[key] = build_program(DEBUG_PHASE, STRICT)
    return _NC_CACHE[key]


def _host_tables(h):
    own0 = h * 2048
    g = own0 - 1024 + np.arange(ZLEN, dtype=np.float64)
    inv = (1.0 / 10000.0) ** (np.arange(0, HD, 2, dtype=np.float64) / HD)
    invrep = np.repeat(inv, 2)
    ang = invrep[:, None] * g[None, :]
    cost = np.cos(ang).astype(np.float32)
    sint = np.sin(ang).astype(np.float32)
    sint[0::2, :] *= -1.0
    mb = np.zeros((16, BAND), np.float32)
    for qt in range(16):
        gk = own0 - 1024 + qt * 128 + np.arange(BAND)
        mb[qt, gk < 0] = NEG
    return (cost.astype(BFD), sint.astype(BFD), np.ascontiguousarray(mb).astype(BFD))


def kernel(**inputs):
    x = np.asarray(inputs["L0_attn_x"], np.float32)
    T = lambda a: np.ascontiguousarray(np.asarray(a, np.float32).T)
    col = lambda a: np.asarray(a, np.float32).reshape(-1, 1)

    shared = {
        "winT": T(inputs["L0_attn_hawk_Win"]),
        "wgT": T(inputs["L0_attn_hawk_Wg"]),
        "woutT": T(inputs["L0_attn_hawk_Wout"]),
        "cwi": np.ascontiguousarray(np.asarray(inputs["L0_attn_hawk_cw"], np.float32)
                                    .reshape(HID, 4)),
        "cbi": col(inputs["L0_attn_hawk_cb"]),
        "hbg": col(inputs["L0_attn_hawk_bg"]),
        "hfb": col(inputs["L0_attn_hawk_fb"]),
        "g1gT": T(inputs["L0_attn_g1_Wgrow"]),
        "g1sT": T(inputs["L0_attn_g1_Wshrink"]),
        "g2gT": T(inputs["L0_attn_g2_Wgrow"]),
        "g2sT": T(inputs["L0_attn_g2_Wshrink"]),
        "wqT": T(inputs["L0_attn_smqa_Wq"]),
        "wkvT": T(inputs["L0_attn_smqa_Wkv"]),
        "woT": T(inputs["L0_attn_smqa_Wout"]),
        "fw1": np.ascontiguousarray(np.asarray(inputs["L0_ffn_W1"], np.float32)),
        "fw2": np.ascontiguousarray(np.asarray(inputs["L0_ffn_W2"], np.float32)),
        "fwg": np.ascontiguousarray(np.asarray(inputs["L0_ffn_Wg"], np.float32)),
        "gnh": col(inputs["L0_attn_gn_hawk"]),
        "gn1": col(inputs["L0_attn_gn_hgmlp"]),
        "gns": col(inputs["L0_attn_gn_smqa"]),
        "gn2": col(inputs["L0_attn_gn_sgmlp"]),
        "bgrep": np.ascontiguousarray(np.broadcast_to(
            np.asarray(inputs["L0_ffn_bg"], np.float32), (128, NE))),
        "identf": np.eye(128, dtype=np.float32),
        "identb": np.eye(128, dtype=np.float32).astype(BFD),
        "onesb": np.ones((1, 128), np.float32).astype(BFD),
        "iota8": np.ascontiguousarray(np.broadcast_to(
            np.arange(NE, dtype=np.float32), (128, NE))),
    }
    mconst = np.zeros((128, BAND), np.float32)
    r = np.arange(128)[:, None]
    j = np.arange(BAND)[None, :]
    mconst[~((j >= r) & (j <= r + WSZ))] = NEG
    shared["mconst"] = mconst.astype(BFD)

    tabs = {h: _host_tables(h) for h in (0, 1)}
    in_maps = []
    for core in range(8):
        b, h = core // 2, core % 2
        if h == 1:
            xw = x[b]
        else:
            xw = np.concatenate([x[b, 0:2048], x[b, 0:2048]], 0)
        cost, sint, mb = tabs[h]
        m = dict(shared)
        m["xwin"] = np.ascontiguousarray(xw)
        m["flagc"] = np.full((128, 1), float(h), np.float32)
        m["epsc"] = np.full((128, 1), EPS, np.float32)
        m["costab"] = cost
        m["sintab"] = sint
        m["mbias"] = mb
        in_maps.append(m)

    nc = _get_nc()
    res = run_bass_kernel_spmd(nc, in_maps, core_ids=list(range(8)))
    out = np.empty((4, 4096, D), np.float32)
    for core in range(8):
        b, h = core // 2, core % 2
        out[b, h * 2048:(h + 1) * 2048] = np.asarray(res.results[core]["y"], np.float32)
    return out


# version 02
# speedup vs baseline: 1.6348x; 1.6348x over previous
import contextlib
import numpy as np
import ml_dtypes
import concourse.bass as bass
import concourse.mybir as mybir
from concourse.bass_utils import run_bass_kernel_spmd
from concourse.bass_core_rust import IndirectOffsetOnAxis

F32 = mybir.dt.float32
BF16 = mybir.dt.bfloat16
I32 = mybir.dt.int32
U32 = mybir.dt.uint32
AF = mybir.ActivationFunctionType
ALU = mybir.AluOpType
AX = mybir.AxisListType
BFD = ml_dtypes.bfloat16

D = 1024
DC = 8
HID = 1536
HC = 12
GH = 2048
HD = 128
QH = 8
NE = 8
FH = 4096
WIN = 4096
OWN = 2048
PRE = 2048
KV0 = 1024
ZLEN = 3072
BAND = 1280
WSZ = 1024
CAP = 1024
TRASH = NE * CAP
NEG = -1.0e30
EPS = 1e-5
SCL = float(1.0 / np.sqrt(128.0))

STRICT = False
DEBUG_PHASE = None

ENGINES = ("sync", "gpsimd", "vector", "scalar", "tensor")
SEMS = ("ds", "dg", "qv", "qs", "qt", "qg")


class Plan:
    def __init__(self, nc, strict):
        self.nc = nc
        self.strict = strict
        self.streams = {e: [] for e in ENGINES}
        self.sem = {}
        self.cnt = {s: 0 for s in SEMS}
        self.seen = {e: {s: 0 for s in SEMS} for e in ENGINES}

    def ev(self, name):
        return (name, self.cnt[name])

    def all_ev(self):
        return [self.ev(s) for s in SEMS]

    def emit(self, eng, f, waits=(), inc=None, n=1):
        if self.strict:
            waits = self.all_ev()
        pruned = []
        seen = self.seen[eng]
        for w in waits:
            if w is None:
                continue
            s, v = w
            if v <= 0 or seen[s] >= v:
                continue
            seen[s] = v
            pruned.append((s, v))
        if inc is not None:
            self.cnt[inc] += n
        sems = self.sem

        def runner(e, f=f, pruned=tuple(pruned), inc=inc, n=n):
            for s, v in pruned:
                e.wait_ge(sems[s], v)
            ins = f(e)
            if inc is not None:
                ins.then_inc(sems[inc], n)
        self.streams[eng].append(runner)
        return self.ev(inc) if inc is not None else None

    def wait(self, eng, waits):
        pruned = []
        seen = self.seen[eng]
        for w in waits:
            if w is None:
                continue
            s, v = w
            if v <= 0 or seen[s] >= v:
                continue
            seen[s] = v
            pruned.append((s, v))
        if not pruned:
            return
        sems = self.sem

        def runner(e, pruned=tuple(pruned)):
            for s, v in pruned:
                e.wait_ge(sems[s], v)
        self.streams[eng].append(runner)

    def dma(self, eng, out, in_, waits=(), **kw):
        sem = "ds" if eng in ("sync", "scalar") else "dg"
        return self.emit(eng, lambda e, o=out, i=in_: e.dma_start(out=o, in_=i, **kw),
                         waits=waits, inc=sem, n=16)

    def barrier(self):
        evs = self.all_ev()
        for eng in ENGINES:
            self.wait(eng, evs)


def _r(ap, pat, **kw):
    return ap.rearrange(pat, **kw)


def build_program(debug_phase=None, strict=STRICT):
    nc = bass.Bass("TRN2", target_bir_lowering=False, debug=False)

    def din(name, shape, dt=F32):
        return nc.dram_tensor(name, list(shape), dt, kind="ExternalInput").ap()

    xwin = din("xwin", [WIN, D])
    flagc = din("flagc", [128, 1])
    costab = din("costab", [128, ZLEN], BF16)
    sintab = din("sintab", [128, ZLEN], BF16)
    mconst = din("mconst", [128, BAND], BF16)
    mbias = din("mbias", [16, BAND], BF16)
    identf = din("identf", [128, 128])
    identb = din("identb", [128, 128], BF16)
    onesb = din("onesb", [1, 128], BF16)
    iota8 = din("iota8", [128, NE])
    bgrep = din("bgrep", [128, NE])
    winT = din("winT", [D, 2 * HID])
    wgT = din("wgT", [HID, 2 * HID])
    woutT = din("woutT", [HID, D])
    cwi = din("cwi", [HID, 4])
    cbi = din("cbi", [HID, 1])
    hbg = din("hbg", [2 * HID, 1])
    hfb = din("hfb", [HID, 1])
    g1gT = din("g1gT", [D, 2 * GH])
    g1sT = din("g1sT", [GH, D])
    g2gT = din("g2gT", [D, 2 * GH])
    g2sT = din("g2sT", [GH, D])
    wqT = din("wqT", [D, D])
    wkvT = din("wkvT", [D, 2 * HD])
    woT = din("woT", [D, D])
    fw1 = din("fw1", [NE, D, FH])
    fw2 = din("fw2", [NE, FH, D])
    fwg = din("fwg", [D, NE])
    gnh = din("gnh", [D, 1])
    gn1 = din("gn1", [D, 1])
    gns = din("gns", [D, 1])
    gn2 = din("gn2", [D, 1])
    epsc = din("epsc", [128, 1])

    yout = nc.dram_tensor("y", [OWN, D], F32, kind="ExternalOutput").ap()

    gate_hbm = nc.dram_tensor("gate_hbm", [HID, ZLEN], BF16).ap()
    hconv_hbm = nc.dram_tensor("hconv_hbm", [HID, WIN], BF16).ap()
    hst_hbm = nc.dram_tensor("hst_hbm", [HID, ZLEN], BF16).ap()
    xg_hbm = nc.dram_tensor("xg_hbm", [TRASH + 1, D], BF16).ap()
    yg_hbm = nc.dram_tensor("yg_hbm", [TRASH + 1, D], BF16).ap()
    xmid_hbm = nc.dram_tensor("xmid_hbm", [OWN, D], F32).ap()

    taps = {}
    if debug_phase is not None:
        for nm, shape, dt in [
            ("t_u", [128, DC * WIN], BF16), ("t_hconv", [128, HC * WIN], BF16),
            ("t_gate", [128, HC * ZLEN], BF16), ("t_hst", [128, HC * ZLEN], BF16),
            ("t_z", [128, DC * OWN], F32), ("t_zh", [128, DC * KV0], BF16),
            ("t_k", [128, 3200], BF16), ("t_v", [128, 3200], BF16),
            ("t_logits", [128, 16 * NE], F32), ("t_slot", [128, 16], I32),
            ("t_xmid", [128, 16 * D], F32),
        ]:
            taps[nm] = nc.dram_tensor(nm, shape, dt, kind="ExternalOutput").ap()

    es = contextlib.ExitStack()
    with es:
        P = Plan(nc, strict)
        for s in SEMS:
            P.sem[s] = es.enter_context(nc.semaphore(s))

        _uid = [0]

        def SB(st, name, shape, dt):
            _uid[0] += 1
            return st.enter_context(nc.sbuf_tensor(f"{name}_{_uid[0]}", list(shape), dt))

        def PS(st, name, shape, dt):
            _uid[0] += 1
            return st.enter_context(nc.psum_tensor(f"{name}_{_uid[0]}", list(shape), dt))

        idf_sb = SB(es, "idf", [128, 128], F32)
        idb_sb = SB(es, "idb", [128, 128], BF16)
        ones_sb = SB(es, "ones1", [1, 128], BF16)
        flag_sb = SB(es, "flag", [128, 1], F32)
        gn_sb = SB(es, "gn", [128, DC * 4], F32)
        small = SB(es, "small", [128, 64], F32)
        cw_sb = SB(es, "cw4", [128, HC * 4], F32)
        stat = SB(es, "stat", [128, 8], F32)
        s6 = SB(es, "s6", [128, 12], F32)
        slot_i = SB(es, "slot_i", [128, 16], I32)
        eps_sb = SB(es, "eps_sb", [128, 1], F32)
        idxf = SB(es, "idxf", [128, 16], F32)

        P.dma("sync", idf_sb[:], identf)
        P.dma("sync", idb_sb[:], identb)
        P.dma("sync", ones_sb[:], onesb)
        P.dma("sync", flag_sb[:], flagc)
        P.dma("sync", eps_sb[:], epsc)
        for i, g in enumerate((gnh, gn1, gns, gn2)):
            P.dma("sync", _r(gn_sb[:, i * DC:(i + 1) * DC], "p (c one) -> p c one", c=DC),
                  _r(g, "(c p) one -> p c one", p=128), allow_slow_non_contiguous=True)
        P.dma("sync", _r(cw_sb[:], "p (c k) -> p c k", c=HC), _r(cwi, "(c p) k -> p c k", p=128), allow_slow_non_contiguous=True)
        P.dma("sync", _r(small[:, 36:48], "p (c o) -> p c o", c=HC),
              _r(cbi, "(c p) one -> p c one", p=128), allow_slow_non_contiguous=True)
        P.dma("sync", _r(small[:, 48:60], "p (c o) -> p c o", c=HC),
              _r(hfb, "(c p) one -> p c one", p=128), allow_slow_non_contiguous=True)
        P.dma("sync", _r(small[:, 12:24], "p (c o) -> p c o", c=HC),
              _r(hbg[0:HID], "(c p) one -> p c one", p=128), allow_slow_non_contiguous=True)
        P.dma("sync", _r(small[:, 24:36], "p (c o) -> p c o", c=HC),
              _r(hbg[HID:2 * HID], "(c p) one -> p c one", p=128), allow_slow_non_contiguous=True)
        ldev = P.ev("ds")
        fb = small[:, 48:60]
        P.emit("scalar", lambda e: e.activation(small[:, 0:12], fb, AF.Abs),
               waits=[ldev], inc="qs")
        P.emit("scalar", lambda e: e.activation(small[:, 0:12], small[:, 0:12], AF.Exp,
                                                scale=-1.0), waits=[P.ev("qs")], inc="qs")
        P.emit("scalar", lambda e: e.activation(small[:, 0:12], small[:, 0:12], AF.Ln,
                                                bias=1.0), waits=[P.ev("qs")], inc="qs")
        P.emit("vector", lambda e: e.tensor_scalar_max(stat[:, 0:6], fb[:, 0:6], 0.0),
               waits=[ldev], inc="qv")
        P.emit("vector", lambda e: e.tensor_scalar_max(s6[:, 0:6], fb[:, 6:12], 0.0),
               waits=[ldev], inc="qv")
        P.emit("vector", lambda e: e.tensor_add(small[:, 0:6], small[:, 0:6], stat[:, 0:6]),
               waits=[P.ev("qs"), P.ev("qv")], inc="qv")
        P.emit("vector", lambda e: e.tensor_add(small[:, 6:12], small[:, 6:12], s6[:, 0:6]),
               waits=[P.ev("qv")], inc="qv")
        P.emit("vector", lambda e: e.tensor_scalar_mul(small[:, 0:12], small[:, 0:12], -4.0),
               waits=[P.ev("qv")], inc="qv")
        P.emit("vector", lambda e: e.tensor_scalar_mul(small[:, 12:36], small[:, 12:36], 0.5),
               waits=[ldev], inc="qv")
        consts_ev = P.ev("qv")

        def load_wT(eng, dst_sb, src_ap, kdim, ncols, waits=()):
            kc = kdim // 128
            return P.dma(eng, _r(dst_sb[:, 0:kc * ncols], "p (c n) -> p c n", c=kc),
                         _r(src_ap, "(c p) n -> p c n", p=128), waits=waits)

        def stats_on(tok_tile, waits=()):
            P.emit("vector", lambda e: e.bn_stats(s6[:, 0:6], tok_tile[:, 0:512]),
                   waits=list(waits), inc="qv")
            P.emit("vector", lambda e: e.bn_stats(s6[:, 6:12], tok_tile[:, 512:1024]),
                   waits=[P.ev("qv")], inc="qv")
            return P.emit("vector", lambda e: e.bn_aggr(stat[:, 0:2], s6[:]),
                          waits=[P.ev("qv")], inc="qv")

        def mk_scale(col_in, col_sq, col_rs, eps, aggr_ev):
            bias = 0.0 if eps is None else eps_sb[:, 0:1]
            if eps is not None:
                aggr_ev = P.emit("vector", lambda e: e.tensor_scalar_add(
                    stat[:, col_in:col_in + 1], stat[:, col_in:col_in + 1], EPS),
                    waits=[aggr_ev], inc="qv")
            e1 = P.emit("scalar", lambda e: e.activation(
                stat[:, col_sq:col_sq + 1], stat[:, col_in:col_in + 1], AF.Sqrt),
                waits=[aggr_ev], inc="qs")
            e2 = P.emit("vector", lambda e: e.reciprocal(
                stat[:, col_rs:col_rs + 1], stat[:, col_sq:col_sq + 1]),
                waits=[e1], inc="qv")
            e3 = P.emit("vector", lambda e: e.tensor_mul(
                stat[:, col_sq:col_sq + 1], stat[:, col_rs:col_rs + 1],
                stat[:, col_rs:col_rs + 1]), waits=[e2], inc="qv")
            e4 = P.emit("vector", lambda e: e.tensor_mul(
                stat[:, col_sq:col_sq + 1], stat[:, col_sq:col_sq + 1],
                stat[:, col_in:col_in + 1]), waits=[e3], inc="qv")
            e5 = P.emit("vector", lambda e: e.tensor_scalar(
                stat[:, col_sq:col_sq + 1], stat[:, col_sq:col_sq + 1], -0.5, 1.5,
                ALU.mult, ALU.add), waits=[e4], inc="qv")
            return P.emit("vector", lambda e: e.tensor_mul(
                stat[:, col_rs:col_rs + 1], stat[:, col_rs:col_rs + 1],
                stat[:, col_sq:col_sq + 1]), waits=[e5], inc="qv")

        def neg_m_rs(col_rs, col_out, rs_ev):
            return P.emit("vector", lambda e: e.scalar_tensor_tensor(
                stat[:, col_out:col_out + 1], stat[:, 0:1], -1.0,
                stat[:, col_rs:col_rs + 1], ALU.mult, ALU.mult),
                waits=[rs_ev], inc="qv")

        def transpose_tokmaj_to_fm(src_tok, pst, dtype_f32, extra_waits=()):
            bank_cols = 4 if dtype_f32 else 8
            idm = idf_sb if dtype_f32 else idb_sb
            for c in range(DC):
                P.emit("tensor", lambda e, c=c: e.matmul(
                    pst[:, c * 128:(c + 1) * 128], src_tok[:, c * 128:(c + 1) * 128],
                    idm[:], is_transpose=True,
                    start=(c % bank_cols == 0), stop=(c == DC - 1)),
                    waits=(list(extra_waits) if c == 0 else []),
                    inc="qt" if c == DC - 1 else None)
            return P.ev("qt")

        def transpose_fm_to_tokmaj(src_fm_slices, pst, dtype_f32, extra_waits=()):
            bank_cols = 4 if dtype_f32 else 8
            idm = idf_sb if dtype_f32 else idb_sb
            for c, sl in enumerate(src_fm_slices):
                P.emit("tensor", lambda e, c=c, sl=sl: e.matmul(
                    pst[:, c * 128:(c + 1) * 128], sl, idm[:], is_transpose=True,
                    start=(c % bank_cols == 0), stop=(c == DC - 1)),
                    waits=(list(extra_waits) if c == 0 else []),
                    inc="qt" if c == DC - 1 else None)
            return P.ev("qt")

        def z_slices(wp, width=128):
            own = wp >= PRE
            src = z if own else zh
            base = (wp - PRE) if own else (wp - KV0)
            w = OWN if own else KV0
            return [src[:, c * w + base: c * w + base + width] for c in range(DC)], own

        zstack = es.enter_context(contextlib.ExitStack())

        with contextlib.ExitStack() as pha:
            wg_sb = SB(pha, "wg_sb", [128, HC * 2 * HID], BF16)

            with contextlib.ExitStack() as psa:
                win_sb = SB(psa, "win_sb", [128, DC * 2 * HID], BF16)
                xa = SB(psa, "xa", [128, 2 * D], F32)
                u_fm = SB(psa, "u_fm", [128, DC * 1024], BF16)
                utok = SB(psa, "utok", [128, D], BF16)
                hpre = SB(psa, "hpre", [128, 1024 + 3], BF16)
                hcv = SB(psa, "hcv", [128, 1024], BF16)
                gtmp = SB(psa, "gtmp", [128, 1024], BF16)
                carry3 = SB(psa, "carry3", [128, HC * 3], BF16)
                psT = PS(psa, "psT", [128, 1024], BF16)
                pm0 = PS(psa, "pm0", [128, 1024], F32)
                pm1 = PS(psa, "pm1", [128, 1024], F32)

                wev = load_wT("gpsimd", win_sb, winT, D, 2 * HID)
                for c in range(DC):
                    P.emit("vector", lambda e, c=c: e.tensor_scalar_mul(
                        win_sb[:, c * 2 * HID:(c + 1) * 2 * HID],
                        win_sb[:, c * 2 * HID:(c + 1) * 2 * HID], gn_sb[:, c:c + 1]),
                        waits=[wev, consts_ev], inc="qv")
                win_ready = P.ev("qv")
                wg_ev = load_wT("gpsimd", wg_sb, wgT, HID, 2 * HID)
                P.emit("gpsimd", lambda e: e.memset(carry3[:], 0.0), inc="qg")
                cz_ev = P.ev("qg")

                pm_free = {0: None, 1: None}
                TCH = 1024
                for ch in range(WIN // TCH):
                    t0 = ch * TCH
                    utile_evs = []
                    for i in range(TCH // 128):
                        buf = xa[:, (i % 2) * D:(i % 2) * D + D]
                        xev = P.dma("sync", buf, xwin[t0 + i * 128: t0 + (i + 1) * 128, :],
                                    waits=[P.ev("qs")])
                        agg = stats_on(buf, waits=[xev])
                        rse = mk_scale(1, 2, 4, None, agg)
                        nme = neg_m_rs(4, 6, rse)
                        aev = P.emit("scalar", lambda e, buf=buf: e.activation(
                            utok[:], buf, AF.Identity, bias=stat[:, 6:7], scale=stat[:, 4:5]),
                            waits=[nme], inc="qs")
                        tev = transpose_tokmaj_to_fm(utok, psT, False,
                                                     extra_waits=[aev, win_ready, P.ev("qv")])
                        cev = P.emit("vector", lambda e, i=i: e.tensor_copy(
                            _r(u_fm[:], "p (c n) -> p c n", c=DC)[:, :, i * 128:(i + 1) * 128],
                            _r(psT[:], "p (c n) -> p c n", c=DC)),
                            waits=[tev], inc="qv")
                        utile_evs.append(cev)
                    u_ready = P.ev("qv")
                    if debug_phase == "A1" and taps:
                        P.dma("sync", _r(taps["t_u"], "p (c w) -> p c w", c=DC)[:, :, t0:t0 + TCH],
                              _r(u_fm[:], "p (c n) -> p c n", c=DC), waits=[u_ready])

                    do_gate = t0 >= KV0
                    mlist = [("h", m) for m in range(HC)] + \
                            ([("g", m) for m in range(HC)] if do_gate else [])
                    for kind, m in mlist:
                        col0 = (HID if kind == "h" else 0) + m * 128
                        pmi = m % 2
                        pm = pm0 if pmi == 0 else pm1
                        for nb in range(2):
                            for kc in range(DC):
                                P.emit("tensor", lambda e, kc=kc, nb=nb, col0=col0, pm=pm: e.matmul(
                                    pm[:, nb * 512:(nb + 1) * 512],
                                    win_sb[:, kc * 2 * HID + col0: kc * 2 * HID + col0 + 128],
                                    u_fm[:, kc * 1024 + nb * 512: kc * 1024 + (nb + 1) * 512],
                                    start=(kc == 0), stop=(kc == DC - 1)),
                                    waits=([u_ready, pm_free[pmi], wg_ev if kind == "h" else None]
                                           if (kc == 0 and nb == 0) else []),
                                    inc="qt" if (kc == DC - 1 and nb == 1) else None)
                        mmev = P.ev("qt")
                        if kind == "h":
                            e1 = P.emit("vector", lambda e, pm=pm: e.tensor_copy(
                                hpre[:, 3:1027], pm[:]), waits=[mmev, P.ev("dg"), P.ev("qv")],
                                inc="qv")
                            pm_free[pmi] = e1
                            e2 = P.emit("vector", lambda e, m=m: e.tensor_copy(
                                hpre[:, 0:3], carry3[:, m * 3:m * 3 + 3]),
                                waits=[e1, cz_ev], inc="qv")
                            if t0 == PRE:
                                e2 = P.emit("vector", lambda e: e.tensor_scalar_mul(
                                    hpre[:, 0:3], hpre[:, 0:3], flag_sb[:, 0:1]),
                                    waits=[e2], inc="qv")
                            e3 = P.emit("vector", lambda e, m=m: e.tensor_copy(
                                carry3[:, m * 3:m * 3 + 3], hpre[:, 1024:1027]),
                                waits=[e1], inc="qv")
                            c0 = P.emit("vector", lambda e, m=m: e.tensor_scalar(
                                hcv[:], hpre[:, 0:1024], cw_sb[:, m * 4:m * 4 + 1],
                                small[:, 36 + m:37 + m], ALU.mult, ALU.add),
                                waits=[e2, e3], inc="qv")
                            for k in range(1, 4):
                                c0 = P.emit("vector", lambda e, k=k, m=m: e.scalar_tensor_tensor(
                                    hcv[:], hpre[:, k:k + 1024], cw_sb[:, m * 4 + k:m * 4 + k + 1],
                                    hcv[:], ALU.mult, ALU.add), waits=[c0], inc="qv")
                            P.dma("gpsimd", hconv_hbm[m * 128:(m + 1) * 128, t0:t0 + TCH],
                                  hcv[:], waits=[c0])
                            P.wait("vector", [P.ev("dg")])
                        else:
                            gev = P.emit("scalar", lambda e, pm=pm: e.activation(
                                gtmp[:], pm[:], AF.Identity), waits=[mmev, P.ev("dg")], inc="qs")
                            pm_free[pmi] = gev
                            P.dma("gpsimd",
                                  gate_hbm[m * 128:(m + 1) * 128, t0 - KV0:t0 - KV0 + TCH],
                                  gtmp[:], waits=[gev])
                            P.wait("scalar", [P.ev("dg")])
                P.barrier()

                if debug_phase == "A1":
                    tmp = SB(psa, "dbg1", [128, WIN], BF16)
                    for m in range(HC):
                        ev = P.dma("sync", tmp[:], hconv_hbm[m * 128:(m + 1) * 128, :])
                        P.dma("sync", taps["t_hconv"][:, m * WIN:(m + 1) * WIN], tmp[:],
                              waits=[ev])
                    _finish(P, nc, es)
                    return nc

            with contextlib.ExitStack() as psa:
                hcb = SB(psa, "hcb", [128, 2 * HC * 512], BF16)
                tf = SB(psa, "tf", [128, 512], F32)
                al = SB(psa, "al", [128, 512], F32)
                ti = SB(psa, "ti", [128, 512], F32)
                sg = SB(psa, "sg", [128, 512], F32)
                w1s = SB(psa, "w1s", [128, 512], F32)
                xs_ = SB(psa, "xs_", [128, 512], F32)
                hsc = SB(psa, "hsc", [128, 512], F32)
                hsb = SB(psa, "hsb", [128, 512], BF16)
                stcol = SB(psa, "stcol", [128, HC], F32)
                pf = PS(psa, "pf", [128, 512], F32)
                pi_ = PS(psa, "pi", [128, 512], F32)

                P.emit("vector", lambda e: e.memset(stcol[:], 0.0), waits=[P.ev("qt")], inc="qv")
                TB = 512
                hc_free = {0: None, 1: None}
                for tb in range(WIN // TB):
                    t0 = tb * TB
                    half = tb % 2
                    dst = hcb[:, half * HC * 512:(half + 1) * HC * 512]
                    hev = P.dma("sync", _r(dst, "p (c n) -> p c n", c=HC),
                                _r(hconv_hbm[:, t0:t0 + TB], "(c p) n -> p c n", p=128),
                                waits=[hc_free[half]])
                    for c in range(HC):
                        for pp, colbase in ((pf, 0), (pi_, HID)):
                            for kc in range(HC):
                                P.emit("tensor", lambda e, kc=kc, pp=pp, colbase=colbase, c=c,
                                       dst=dst: e.matmul(
                                    pp[:],
                                    wg_sb[:, kc * 2 * HID + colbase + c * 128:
                                          kc * 2 * HID + colbase + c * 128 + 128],
                                    dst[:, kc * 512:(kc + 1) * 512],
                                    start=(kc == 0), stop=(kc == HC - 1)),
                                    waits=([hev, P.ev("qs"), P.ev("qg")] if kc == 0 else []),
                                    inc="qt" if kc == HC - 1 else None)
                        mmev = P.ev("qt")
                        e1 = P.emit("scalar", lambda e, c=c: e.activation(
                            tf[:], pf[:], AF.Tanh, bias=small[:, 12 + c:13 + c], scale=0.5),
                            waits=[mmev, P.ev("qg"), P.ev("qv")], inc="qs")
                        e2 = P.emit("scalar", lambda e, c=c: e.activation(
                            al[:], tf[:], AF.Exp, bias=small[:, c:c + 1], scale=small[:, c:c + 1]),
                            waits=[e1], inc="qs")
                        e3 = P.emit("scalar", lambda e, c=c: e.activation(
                            ti[:], pi_[:], AF.Tanh, bias=small[:, 24 + c:25 + c], scale=0.5),
                            waits=[e2], inc="qs")
                        g1 = P.emit("gpsimd", lambda e: e.tensor_scalar(
                            sg[:], ti[:], 0.5, 0.5, ALU.mult, ALU.add), waits=[e3], inc="qg")
                        g2 = P.emit("gpsimd", lambda e: e.tensor_mul(w1s[:], al[:], al[:]),
                                    waits=[e2], inc="qg")
                        g3 = P.emit("gpsimd", lambda e: e.tensor_scalar(
                            w1s[:], w1s[:], -1.0, 1.0 + 1e-6, ALU.mult, ALU.add),
                            waits=[g2], inc="qg")
                        e4 = P.emit("scalar", lambda e: e.activation(w1s[:], w1s[:], AF.Ln),
                                    waits=[g3], inc="qs")
                        e5 = P.emit("scalar", lambda e: e.activation(w1s[:], w1s[:], AF.Exp,
                                                                     scale=0.5),
                                    waits=[e4], inc="qs")
                        g4 = P.emit("gpsimd", lambda e: e.tensor_mul(xs_[:], sg[:], w1s[:]),
                                    waits=[e5, g1], inc="qg")
                        g5 = P.emit("gpsimd", lambda e, c=c, dst=dst: e.tensor_mul(
                            xs_[:], xs_[:], dst[:, c * 512:(c + 1) * 512]),
                            waits=[g4], inc="qg")
                        init_ev = None
                        if tb == PRE // TB:
                            init_ev = P.emit("vector", lambda e, c=c: e.tensor_scalar_mul(
                                stcol[:, c:c + 1], stcol[:, c:c + 1], flag_sb[:, 0:1]),
                                waits=[P.ev("qv")], inc="qv")
                        sev = P.emit("vector", lambda e, c=c: e.tensor_tensor_scan(
                            hsc[:], al[:], xs_[:], stcol[:, c:c + 1], ALU.mult, ALU.add),
                            waits=[g5, init_ev, P.ev("qs")], inc="qv")
                        P.emit("vector", lambda e, c=c: e.tensor_copy(
                            stcol[:, c:c + 1], hsc[:, 511:512]), waits=[sev], inc="qv")
                        if t0 >= KV0:
                            bev = P.emit("vector", lambda e: e.tensor_copy(hsb[:], hsc[:]),
                                         waits=[sev, P.ev("dg")], inc="qv")
                            P.dma("gpsimd",
                                  hst_hbm[c * 128:(c + 1) * 128, t0 - KV0:t0 - KV0 + TB],
                                  hsb[:], waits=[bev])
                    hc_free[half] = P.ev("qt")
                P.barrier()

            if debug_phase == "A2":
                tmp = SB(pha, "dbg2", [128, ZLEN], BF16)
                for m in range(HC):
                    ev = P.dma("sync", tmp[:], hst_hbm[m * 128:(m + 1) * 128, :])
                    P.dma("sync", taps["t_hst"][:, m * ZLEN:(m + 1) * ZLEN], tmp[:], waits=[ev])
                    ev = P.dma("sync", tmp[:], gate_hbm[m * 128:(m + 1) * 128, :])
                    P.dma("sync", taps["t_gate"][:, m * ZLEN:(m + 1) * ZLEN], tmp[:], waits=[ev])
                _finish(P, nc, es)
                return nc

        z = SB(zstack, "z", [128, DC * OWN], F32)
        zh = SB(zstack, "zh", [128, DC * KV0], BF16)

        def dump_z():
            P.dma("sync", taps["t_z"], z[:], waits=P.all_ev())
            P.dma("sync", taps["t_zh"], zh[:], waits=P.all_ev())
            _finish(P, nc, es)

        with contextlib.ExitStack() as phb, contextlib.ExitStack() as psb:
            wo_sb = SB(phb, "wo_sb", [128, HC * D], BF16)
            gh_sb = SB(phb, "gh_sb", [128, HC * 1024], BF16)
            gl_sb = SB(phb, "gl_sb", [128, 1024], BF16)
            gld = SB(phb, "gld", [128, 2 * 1024], BF16)
            hld = SB(phb, "hld", [128, 2 * 1024], BF16)
            xb2 = SB(phb, "xb2", [128, 2 * D], F32)
            ztok = SB(phb, "ztok", [128, D], F32)
            psz = PS(psb, "psz", [128, 1024], F32)
            pw0 = PS(psb, "pw0", [128, 1024], F32)

            woev = load_wT("gpsimd", wo_sb, woutT, HID, D)
            TCH = 1024
            for ci in range(ZLEN // TCH):
                w0 = KV0 + ci * TCH
                for c in range(HC):
                    half = c % 2
                    gev = P.dma("sync", gld[:, half * 1024:(half + 1) * 1024],
                                gate_hbm[c * 128:(c + 1) * 128, ci * TCH:(ci + 1) * TCH],
                                waits=[P.ev("qs")])
                    hev = P.dma("sync", hld[:, half * 1024:(half + 1) * 1024],
                                hst_hbm[c * 128:(c + 1) * 128, ci * TCH:(ci + 1) * TCH],
                                waits=[P.ev("qv")])
                    sev = P.emit("scalar", lambda e, half=half: e.activation(
                        gl_sb[:], gld[:, half * 1024:(half + 1) * 1024], AF.Gelu),
                        waits=[gev, P.ev("qv")], inc="qs")
                    P.emit("vector", lambda e, c=c, half=half: e.tensor_mul(
                        gh_sb[:, c * 1024:(c + 1) * 1024], gl_sb[:],
                        hld[:, half * 1024:(half + 1) * 1024]),
                        waits=[sev, hev], inc="qv")
                gh_ready = P.ev("qv")
                for i in range(TCH // 128):
                    wp = w0 + i * 128
                    buf = xb2[:, (i % 2) * D:(i % 2) * D + D]
                    xev = P.dma("sync", buf, xwin[wp:wp + 128, :], waits=[P.ev("qs")])
                    agg = stats_on(buf, waits=[xev])
                    rse = mk_scale(1, 3, 5, True, agg)
                    nme = neg_m_rs(5, 7, rse)
                    aev = P.emit("scalar", lambda e, buf=buf: e.activation(
                        ztok[:], buf, AF.Identity, bias=stat[:, 7:8], scale=stat[:, 5:6]),
                        waits=[nme], inc="qs")
                    tev = transpose_tokmaj_to_fm(ztok, psz, True,
                                                 extra_waits=[aev, woev, P.ev("qv")])
                    sls, own = z_slices(wp)
                    tgt = (z if own else zh)
                    w = OWN if own else KV0
                    base = (wp - PRE) if own else (wp - KV0)
                    P.emit("vector", lambda e, tgt=tgt, w=w, base=base: e.tensor_copy(
                        _r(tgt[:], "p (c n) -> p c n", c=DC)[:, :, base:base + 128],
                        _r(psz[:], "p (c n) -> p c n", c=DC)),
                        waits=[tev], inc="qv")
                z0_ready = P.ev("qv")
                for dm in range(DC):
                    for nb in range(2):
                        for kc in range(HC):
                            P.emit("tensor", lambda e, dm=dm, nb=nb, kc=kc: e.matmul(
                                pw0[:, nb * 512:(nb + 1) * 512],
                                wo_sb[:, kc * D + dm * 128: kc * D + dm * 128 + 128],
                                gh_sb[:, kc * 1024 + nb * 512: kc * 1024 + (nb + 1) * 512],
                                start=(kc == 0), stop=(kc == HC - 1)),
                                waits=([gh_ready, z0_ready, P.ev("qv")]
                                       if (kc == 0 and nb == 0) else []),
                                inc="qt" if (kc == HC - 1 and nb == 1) else None)
                    mmev = P.ev("qt")
                    own = w0 >= PRE
                    zt = (z[:, dm * OWN + (w0 - PRE): dm * OWN + (w0 - PRE) + TCH] if own
                          else zh[:, dm * KV0 + (w0 - KV0): dm * KV0 + (w0 - KV0) + TCH])
                    P.emit("vector", lambda e, zt=zt: e.tensor_add(zt, zt, pw0[:]),
                           waits=[mmev], inc="qv")
            P.barrier()

        if debug_phase == "B":
            dump_z()
            return nc

        def gmlp(gT, sT, gcol, rng_start, rng_len):
            with contextlib.ExitStack() as phc, contextlib.ExitStack() as psc:
                u_c = SB(phc, "u_c", [128, DC * 1024], BF16)
                utk = SB(phc, "utk", [128, D], BF16)
                gel = SB(phc, "gel", [128, 1024], BF16)
                ghm = SB(phc, "ghm", [128, 16 * 1024], BF16)
                wmA = SB(phc, "wmA", [128, 2 * DC * 128], BF16)
                wmB = SB(phc, "wmB", [128, 2 * DC * 128], BF16)
                wsA = SB(phc, "wsA", [128, 2 * 16 * 128], BF16)
                ptk = PS(psc, "ptk", [128, 1024], F32)
                ptb = PS(psc, "ptb", [128, 1024], BF16)
                pg = PS(psc, "pg", [128, 1024], F32)
                ph_ = PS(psc, "ph_", [128, 1024], F32)

                for ci in range(rng_len // 1024):
                    w0 = rng_start + ci * 1024
                    for i in range(8):
                        wp = w0 + i * 128
                        sls, own = z_slices(wp)
                        pin = ptk if own else ptb
                        tev = transpose_fm_to_tokmaj(sls, pin, own,
                                                     extra_waits=[P.ev("qv"), P.ev("qs")])
                        agg = stats_on(pin[:], waits=[tev])
                        e1 = P.emit("vector", lambda e: e.scalar_tensor_tensor(
                            stat[:, 2:3], stat[:, 0:1], 1.0, stat[:, 0:1], ALU.mult, ALU.mult),
                            waits=[agg], inc="qv")
                        e2 = P.emit("vector", lambda e: e.tensor_add(
                            stat[:, 2:3], stat[:, 2:3], stat[:, 1:2]), waits=[e1], inc="qv")
                        e4 = mk_scale(2, 3, 4, None, e2)
                        aev = P.emit("scalar", lambda e, pin=pin: e.activation(
                            utk[:], pin[:], AF.Identity, scale=stat[:, 4:5]),
                            waits=[e4], inc="qs")
                        tev = transpose_tokmaj_to_fm(utk, ptb, False,
                                                     extra_waits=[aev, P.ev("qv")])
                        P.emit("vector", lambda e, i=i: e.tensor_copy(
                            _r(u_c[:], "p (c n) -> p c n", c=DC)[:, :, i * 128:(i + 1) * 128],
                            _r(ptb[:], "p (c n) -> p c n", c=DC)),
                            waits=[tev], inc="qv")
                    for c in range(DC):
                        P.emit("vector", lambda e, c=c: e.tensor_scalar_mul(
                            u_c[:, c * 1024:(c + 1) * 1024], u_c[:, c * 1024:(c + 1) * 1024],
                            gn_sb[:, gcol * DC + c:gcol * DC + c + 1]),
                            waits=[P.ev("qv")], inc="qv")
                    u_ready = P.ev("qv")

                    wflip = 0
                    wfree = {0: None, 1: None}
                    for m in range(16):
                        wA = wmA[:, wflip * DC * 128:(wflip + 1) * DC * 128]
                        wB = wmB[:, wflip * DC * 128:(wflip + 1) * DC * 128]
                        evA = P.dma("gpsimd", _r(wA, "p (c n) -> p c n", c=DC),
                                    _r(gT[:, m * 128:(m + 1) * 128], "(c p) n -> p c n", p=128),
                                    waits=[wfree[wflip]])
                        evB = P.dma("gpsimd", _r(wB, "p (c n) -> p c n", c=DC),
                                    _r(gT[:, GH + m * 128: GH + (m + 1) * 128],
                                       "(c p) n -> p c n", p=128))
                        for nb in range(2):
                            for kc in range(DC):
                                P.emit("tensor", lambda e, kc=kc, nb=nb, wA=wA: e.matmul(
                                    pg[:, nb * 512:(nb + 1) * 512],
                                    wA[:, kc * 128:(kc + 1) * 128],
                                    u_c[:, kc * 1024 + nb * 512: kc * 1024 + (nb + 1) * 512],
                                    start=(kc == 0), stop=(kc == DC - 1)),
                                    waits=([u_ready, evA, P.ev("qs")]
                                           if (kc == 0 and nb == 0) else []),
                                    inc="qt" if (kc == DC - 1 and nb == 1) else None)
                        for nb in range(2):
                            for kc in range(DC):
                                P.emit("tensor", lambda e, kc=kc, nb=nb, wB=wB: e.matmul(
                                    ph_[:, nb * 512:(nb + 1) * 512],
                                    wB[:, kc * 128:(kc + 1) * 128],
                                    u_c[:, kc * 1024 + nb * 512: kc * 1024 + (nb + 1) * 512],
                                    start=(kc == 0), stop=(kc == DC - 1)),
                                    waits=([evB, P.ev("qv")] if (kc == 0 and nb == 0) else []),
                                    inc="qt" if (kc == DC - 1 and nb == 1) else None)
                        mmev = P.ev("qt")
                        gev = P.emit("scalar", lambda e: e.activation(gel[:], pg[:], AF.Gelu),
                                     waits=[mmev], inc="qs")
                        mev = P.emit("vector", lambda e, m=m: e.tensor_mul(
                            ghm[:, m * 1024:(m + 1) * 1024], gel[:], ph_[:]),
                            waits=[gev], inc="qv")
                        wfree[wflip] = mev
                        wflip ^= 1
                    gh_ready = P.ev("qv")

                    sflip = 0
                    sfree = {0: None, 1: None}
                    for dm in range(DC):
                        wS = wsA[:, sflip * 16 * 128:(sflip + 1) * 16 * 128]
                        evS = P.dma("gpsimd", _r(wS, "p (c n) -> p c n", c=16),
                                    _r(sT[:, dm * 128:(dm + 1) * 128], "(c p) n -> p c n", p=128),
                                    waits=[sfree[sflip]])
                        for nb in range(2):
                            for kc in range(16):
                                P.emit("tensor", lambda e, kc=kc, nb=nb, wS=wS: e.matmul(
                                    pg[:, nb * 512:(nb + 1) * 512],
                                    wS[:, kc * 128:(kc + 1) * 128],
                                    ghm[:, kc * 1024 + nb * 512: kc * 1024 + (nb + 1) * 512],
                                    start=(kc == 0), stop=(kc == 15)),
                                    waits=([gh_ready, evS, P.ev("qv"), P.ev("qs")]
                                           if (kc == 0 and nb == 0) else []),
                                    inc="qt" if (kc == 15 and nb == 1) else None)
                        mmev = P.ev("qt")
                        own = w0 >= PRE
                        zt = (z[:, dm * OWN + (w0 - PRE): dm * OWN + (w0 - PRE) + 1024] if own
                              else zh[:, dm * KV0 + (w0 - KV0): dm * KV0 + (w0 - KV0) + 1024])
                        sfree[sflip] = P.emit("vector", lambda e, zt=zt: e.tensor_add(
                            zt, zt, pg[:]), waits=[mmev], inc="qv")
                        sflip ^= 1
                P.barrier()

        gmlp(g1gT, g1sT, 1, KV0, ZLEN)
        if debug_phase == "C":
            dump_z()
            return nc

        with contextlib.ExitStack() as phd:
            wq_sb = SB(phd, "wq_sb", [128, DC * D], BF16)
            wkv_sb = SB(phd, "wkv_sb", [128, DC * 256], BF16)
            kfull = SB(phd, "kfull", [128, 3200], BF16)
            vfull = SB(phd, "vfull", [128, 3200], BF16)
            vtok = SB(phd, "vtok", [128, 25 * 128], BF16)
            qrot = SB(phd, "qrot", [128, QH * 1024], BF16)
            u_c = SB(phd, "u_cD", [128, DC * 1024], BF16)
            utk = SB(phd, "utkD", [128, D], BF16)
            cosl = SB(phd, "cosl", [128, ZLEN], BF16)
            sinl = SB(phd, "sinl", [128, ZLEN], BF16)
            tsh = SB(phd, "tsh", [128, 1024], BF16)
            tmix = SB(phd, "tmix", [128, 1024], BF16)
            p_sb = SB(phd, "p_sb", [128, BAND], BF16)
            pt_sb = SB(phd, "pt_sb", [128, BAND], BF16)
            mstage = SB(phd, "mstage", [1, BAND], BF16)
            mc_sb = SB(phd, "mc_sb", [128, BAND], BF16)
            otile = SB(phd, "otile", [128, QH * 128], BF16)
            wod = SB(phd, "wod", [128, 2 * QH * 128], BF16)

            wqe = load_wT("gpsimd", wq_sb, wqT, D, D)
            wkve = load_wT("gpsimd", wkv_sb, wkvT, D, 2 * HD)
            P.dma("sync", cosl[:], costab)
            P.dma("sync", sinl[:], sintab)
            P.dma("sync", mc_sb[:], mconst)
            tabev = P.ev("ds")
            for c in range(DC):
                P.emit("vector", lambda e, c=c: e.tensor_scalar_mul(
                    wq_sb[:, c * D:(c + 1) * D], wq_sb[:, c * D:(c + 1) * D],
                    gn_sb[:, 2 * DC + c:2 * DC + c + 1]), waits=[wqe], inc="qv")
                P.emit("vector", lambda e, c=c: e.tensor_scalar_mul(
                    wkv_sb[:, c * 256:(c + 1) * 256], wkv_sb[:, c * 256:(c + 1) * 256],
                    gn_sb[:, 2 * DC + c:2 * DC + c + 1]), waits=[wkve], inc="qv")
            w_ready = P.ev("qv")
            P.emit("gpsimd", lambda e: e.memset(kfull[:, 3072:3200], 0.0), inc="qg")
            P.emit("gpsimd", lambda e: e.memset(vfull[:, 3072:3200], 0.0), inc="qg")
            pad_ev = P.ev("qg")

            shmask = []
            for i in range(0, 32, 2):
                shmask += [i + 1, i]

            def rms_u_chunk(w0):
              with contextlib.ExitStack() as psn:
                psc_ptk = PS(psn, "ptk_n", [128, 1024], F32)
                psc_ptb = PS(psn, "ptb_n", [128, 1024], BF16)
                for i in range(8):
                    wp = w0 + i * 128
                    sls, own = z_slices(wp)
                    pin = psc_ptk if own else psc_ptb
                    tev = transpose_fm_to_tokmaj(sls, pin, own,
                                                 extra_waits=[P.ev("qv"), P.ev("qs")])
                    agg = stats_on(pin[:], waits=[tev])
                    e1 = P.emit("vector", lambda e: e.scalar_tensor_tensor(
                        stat[:, 2:3], stat[:, 0:1], 1.0, stat[:, 0:1], ALU.mult, ALU.mult),
                        waits=[agg], inc="qv")
                    e2 = P.emit("vector", lambda e: e.tensor_add(
                        stat[:, 2:3], stat[:, 2:3], stat[:, 1:2]), waits=[e1], inc="qv")
                    e4 = mk_scale(2, 3, 4, None, e2)
                    aev = P.emit("scalar", lambda e, pin=pin: e.activation(
                        utk[:], pin[:], AF.Identity, scale=stat[:, 4:5]), waits=[e4], inc="qs")
                    tev = transpose_tokmaj_to_fm(utk, psc_ptb, False,
                                                 extra_waits=[aev, P.ev("qv")])
                    P.emit("vector", lambda e, i=i: e.tensor_copy(
                        _r(u_c[:], "p (c n) -> p c n", c=DC)[:, :, i * 128:(i + 1) * 128],
                        _r(psc_ptb[:], "p (c n) -> p c n", c=DC)),
                        waits=[tev], inc="qv")
                return P.ev("qv")

            with contextlib.ExitStack() as psd:
                pkv = PS(psd, "pkv", [128, 1024], F32)
                for ci in range(3):
                    w0 = KV0 + ci * 1024
                    uev = rms_u_chunk(w0)
                    for hm, dst in ((0, kfull), (1, vfull)):
                        for nb in range(2):
                            for kc in range(DC):
                                P.emit("tensor", lambda e, kc=kc, nb=nb, hm=hm: e.matmul(
                                    pkv[:, nb * 512:(nb + 1) * 512],
                                    wkv_sb[:, kc * 256 + hm * 128: kc * 256 + hm * 128 + 128],
                                    u_c[:, kc * 1024 + nb * 512: kc * 1024 + (nb + 1) * 512],
                                    start=(kc == 0), stop=(kc == DC - 1)),
                                    waits=([uev, w_ready, P.ev("qv")]
                                           if (kc == 0 and nb == 0) else []),
                                    inc="qt" if (kc == DC - 1 and nb == 1) else None)
                        mmev = P.ev("qt")
                        cols = slice(ci * 1024, (ci + 1) * 1024)
                        if hm == 1:
                            P.emit("vector", lambda e, cols=cols: e.tensor_copy(
                                vfull[:, cols], pkv[:]), waits=[mmev], inc="qv")
                        else:
                            e1 = P.emit("vector", lambda e: e.tensor_copy(tmix[:], pkv[:]),
                                        waits=[mmev, tabev], inc="qv")
                            e2 = P.emit("vector", lambda e: e.stream_shuffle(
                                tsh[:], tmix[:], shmask), waits=[e1], inc="qv")
                            e3 = P.emit("vector", lambda e, cols=cols: e.tensor_mul(
                                kfull[:, cols], tmix[:], cosl[:, cols]), waits=[e2], inc="qv")
                            e4 = P.emit("vector", lambda e, cols=cols: e.tensor_mul(
                                tsh[:], tsh[:], sinl[:, cols]), waits=[e3], inc="qv")
                            P.emit("vector", lambda e, cols=cols: e.tensor_add(
                                kfull[:, cols], kfull[:, cols], tsh[:]), waits=[e4], inc="qv")
                kv_ready = P.ev("qv")
                with contextlib.ExitStack() as psv:
                    pvt = PS(psv, "pvt", [128, 128], BF16)
                    for i in range(25):
                        P.emit("tensor", lambda e, i=i: e.matmul(
                            pvt[:], vfull[:, i * 128:(i + 1) * 128], idb_sb[:],
                            is_transpose=True, start=True, stop=True),
                            waits=[kv_ready, pad_ev, P.ev("qv")], inc="qt")
                        P.emit("vector", lambda e, i=i: e.tensor_copy(
                            vtok[:, i * 128:(i + 1) * 128], pvt[:]),
                            waits=[P.ev("qt")], inc="qv")
                vtok_ready = P.ev("qv")
                if debug_phase == "Dkv":
                    P.dma("sync", taps["t_k"], kfull[:], waits=[kv_ready, vtok_ready])
                    P.dma("sync", taps["t_v"], vfull[:], waits=[kv_ready])
                    _finish(P, nc, es)
                    return nc

            for qc in range(2):
                w0 = PRE + qc * 1024
                uev = rms_u_chunk(w0)
                with contextlib.ExitStack() as psd:
                    pq = PS(psd, "pq", [128, 512], F32)
                    psS = PS(psd, "psS", [128, 1536], F32)
                    psPT = PS(psd, "psPT", [128, 1536], BF16)
                    psO = PS(psd, "psO", [128, 128], F32)
                    pop = PS(psd, "pop", [128, 128], F32)
                    for hq in range(QH):
                        for nb in range(2):
                            for kc in range(DC):
                                P.emit("tensor", lambda e, kc=kc, nb=nb, hq=hq: e.matmul(
                                    pq[:],
                                    wq_sb[:, kc * D + hq * 128: kc * D + hq * 128 + 128],
                                    u_c[:, kc * 1024 + nb * 512: kc * 1024 + (nb + 1) * 512],
                                    start=(kc == 0), stop=(kc == DC - 1)),
                                    waits=([uev, P.ev("qv")] if kc == 0 else []),
                                    inc="qt" if kc == DC - 1 else None)
                            P.emit("vector", lambda e, nb=nb: e.tensor_copy(
                                tmix[:, nb * 512:(nb + 1) * 512], pq[:]),
                                waits=[P.ev("qt")], inc="qv")
                        mmev = P.ev("qv")
                        qcols = slice(hq * 1024, (hq + 1) * 1024)
                        tcols = slice(1024 + qc * 1024, 1024 + (qc + 1) * 1024)
                        e1 = mmev
                        e2 = P.emit("vector", lambda e: e.stream_shuffle(tsh[:], tmix[:], shmask),
                                    waits=[e1], inc="qv")
                        e3 = P.emit("vector", lambda e, qcols=qcols, tcols=tcols: e.tensor_mul(
                            qrot[:, qcols], tmix[:], cosl[:, tcols]), waits=[e2], inc="qv")
                        e4 = P.emit("vector", lambda e, tcols=tcols: e.tensor_mul(
                            tsh[:], tsh[:], sinl[:, tcols]), waits=[e3], inc="qv")
                        P.emit("vector", lambda e, qcols=qcols: e.tensor_add(
                            qrot[:, qcols], qrot[:, qcols], tsh[:]), waits=[e4], inc="qv")
                    q_ready = P.ev("qv")

                    for qi in range(8):
                        qt = qc * 8 + qi
                        use_bias = qt < 8
                        bev = None
                        if use_bias:
                            bev = P.dma("sync", mstage[:], mbias[qt:qt + 1, :],
                                        waits=[P.ev("qt")])
                        for hq in range(QH):
                            segs = ((0, 512), (512, 512), (1024, 256))
                            for c0, cn in segs:
                                P.emit("tensor", lambda e, qt=qt, hq=hq, c0=c0, cn=cn, qi=qi: e.matmul(
                                    psS[:, c0:c0 + cn],
                                    qrot[:, hq * 1024 + qi * 128: hq * 1024 + qi * 128 + 128],
                                    kfull[:, qt * 128 + c0: qt * 128 + c0 + cn],
                                    start=True, stop=False),
                                    waits=([q_ready, vtok_ready, bev, P.ev("qs"), P.ev("qv")]
                                           if c0 == 0 else []))
                                P.emit("tensor", lambda e, c0=c0, cn=cn: e.matmul(
                                    psS[:, c0:c0 + cn], idb_sb[:], mc_sb[:, c0:c0 + cn],
                                    start=False, stop=(not use_bias)),
                                    inc=(None if use_bias else ("qt" if c0 == 1024 else None)))
                                if use_bias:
                                    P.emit("tensor", lambda e, c0=c0, cn=cn: e.matmul(
                                        psS[:, c0:c0 + cn], ones_sb[:],
                                        mstage[:, c0:c0 + cn], start=False, stop=True),
                                        inc="qt" if c0 == 1024 else None)
                            mmev = P.ev("qt")
                            e1 = P.emit("vector", lambda e: e.reduce_max(
                                stat[:, 0:1], psS[:, 0:BAND], axis=AX.X, negate=True),
                                waits=[mmev], inc="qv")
                            e2 = P.emit("vector", lambda e: e.tensor_scalar_mul(
                                stat[:, 1:2], stat[:, 0:1], SCL), waits=[e1], inc="qv")
                            e3 = P.emit("scalar", lambda e: e.activation(
                                p_sb[:], psS[:, 0:BAND], AF.Exp, bias=stat[:, 1:2], scale=SCL),
                                waits=[e2], inc="qs")
                            e4 = P.emit("vector", lambda e: e.reduce_sum(
                                stat[:, 2:3], p_sb[:], axis=AX.X), waits=[e3], inc="qv")
                            e5 = P.emit("vector", lambda e: e.reciprocal(
                                stat[:, 3:4], stat[:, 2:3]), waits=[e4], inc="qv")
                            e6 = P.emit("vector", lambda e: e.tensor_scalar_mul(
                                p_sb[:], p_sb[:], stat[:, 3:4]), waits=[e5], inc="qv")
                            for kt in range(10):
                                P.emit("tensor", lambda e, kt=kt: e.matmul(
                                    psPT[:, kt * 128:(kt + 1) * 128],
                                    p_sb[:, kt * 128:(kt + 1) * 128], idb_sb[:],
                                    is_transpose=True, start=(kt in (0, 8)), stop=(kt == 9)),
                                    waits=([e6] if kt == 0 else []),
                                    inc="qt" if kt == 9 else None)
                            tev = P.ev("qt")
                            e7 = P.emit("vector", lambda e: e.tensor_copy(
                                pt_sb[:], psPT[:, 0:BAND]), waits=[tev], inc="qv")
                            for kt in range(10):
                                P.emit("tensor", lambda e, kt=kt, qt=qt: e.matmul(
                                    psO[:], vtok[:, (qt + kt) * 128:(qt + kt + 1) * 128],
                                    pt_sb[:, kt * 128:(kt + 1) * 128],
                                    start=(kt == 0), stop=(kt == 9)),
                                    waits=([e7] if kt == 0 else []),
                                    inc="qt" if kt == 9 else None)
                            pvev = P.ev("qt")
                            P.emit("vector", lambda e, hq=hq: e.tensor_copy(
                                otile[:, hq * 128:(hq + 1) * 128], psO[:]),
                                waits=[pvev], inc="qv")
                        o_ready = P.ev("qv")
                        for dm in range(DC):
                            evw = P.dma("gpsimd",
                                        _r(wod[:, (dm % 2) * QH * 128:(dm % 2) * QH * 128 + QH * 128],
                                           "p (c n) -> p c n", c=QH),
                                        _r(woT[:, dm * 128:(dm + 1) * 128],
                                           "(c p) n -> p c n", p=128),
                                        waits=[P.ev("qt")])
                            for kc in range(QH):
                                P.emit("tensor", lambda e, kc=kc, dm=dm: e.matmul(
                                    pop[:],
                                    wod[:, (dm % 2) * QH * 128 + kc * 128:
                                        (dm % 2) * QH * 128 + kc * 128 + 128],
                                    otile[:, kc * 128:(kc + 1) * 128],
                                    start=(kc == 0), stop=(kc == QH - 1)),
                                    waits=([o_ready, evw, P.ev("qv")] if kc == 0 else []),
                                    inc="qt" if kc == QH - 1 else None)
                            mmev = P.ev("qt")
                            zt = z[:, dm * OWN + qt * 128: dm * OWN + qt * 128 + 128]
                            P.emit("vector", lambda e, zt=zt: e.tensor_add(zt, zt, pop[:]),
                                   waits=[mmev], inc="qv")
            P.barrier()

        if debug_phase == "D":
            dump_z()
            return nc

        gmlp(g2gT, g2sT, 3, PRE, OWN)
        if debug_phase == "E":
            dump_z()
            return nc

        with contextlib.ExitStack() as phf, contextlib.ExitStack() as psf:
            xb3 = SB(phf, "xb3", [128, 2 * D], F32)
            xmt = SB(phf, "xmt", [128, D], F32)
            uft = SB(phf, "uft", [128, D], BF16)
            fwg_sb = SB(phf, "fwg_sb", [128, DC * NE], F32)
            bgr_sb = SB(phf, "bgr_sb", [128, NE], F32)
            io8_sb = SB(phf, "io8_sb", [128, NE], F32)
            lgt = SB(phf, "lgt", [128, NE], F32)
            oh_t = SB(phf, "oh_t", [8, OWN], F32)
            onest = SB(phf, "onest", [8, OWN], F32)
            post = SB(phf, "post", [8, OWN], F32)
            mx8 = SB(phf, "mx8", [128, 8], F32)
            ix8 = SB(phf, "ix8", [128, 8], U32)
            ohm = SB(phf, "ohm", [128, 8], F32)
            posm = SB(phf, "posm", [128, 8], F32)
            selc = SB(phf, "selc", [128, 8], F32)
            csr = SB(phf, "csr", [1, 8], F32)
            mrow = SB(phf, "mrow", [1, 128], F32)
            onesf = SB(phf, "onesf", [128, 1], F32)
            ptk = PS(psf, "ptkF", [128, 1024], F32)
            pl = PS(psf, "plF", [128, 512], F32)
            pcs = PS(psf, "pcsF", [128, 256], F32)

            P.dma("sync", _r(fwg_sb[:], "p (c n) -> p c n", c=DC),
                  _r(fwg, "(c p) n -> p c n", p=128))
            P.dma("sync", bgr_sb[:], bgrep)
            P.dma("sync", io8_sb[:], iota8)
            fev = P.ev("ds")
            P.emit("gpsimd", lambda e: e.memset(onest[:], 1.0), inc="qg")
            P.emit("gpsimd", lambda e: e.memset(onesf[:], 1.0), inc="qg")
            gz = P.ev("qg")
            for kc in range(DC):
                P.emit("tensor", lambda e, kc=kc: e.matmul(
                    pcs[0:8, 0:1], fwg_sb[:, kc * NE:(kc + 1) * NE], onesf[:],
                    start=(kc == 0), stop=(kc == DC - 1)),
                    waits=([fev, gz] if kc == 0 else []),
                    inc="qt" if kc == DC - 1 else None)
            P.emit("vector", lambda e: e.tensor_copy(stat[0:8, 6:7], pcs[0:8, 0:1]),
                   waits=[P.ev("qt")], inc="qv")
            P.emit("tensor", lambda e: e.matmul(
                pcs[0:1, 8:16], stat[0:8, 6:7], idf_sb[0:8, 0:8],
                is_transpose=True, start=True, stop=True),
                waits=[P.ev("qv")], inc="qt")
            P.emit("vector", lambda e: e.tensor_copy(csr[:], pcs[0:1, 8:16]),
                   waits=[P.ev("qt")], inc="qv")
            P.emit("vector", lambda e: e.tensor_scalar_mul(csr[:], csr[:], -1.0),
                   waits=[P.ev("qv")], inc="qv")
            cs_ready = P.ev("qv")

            for i in range(16):
                buf = xb3[:, (i % 2) * D:(i % 2) * D + D]
                xev = P.dma("sync", buf, xwin[PRE + i * 128: PRE + (i + 1) * 128, :],
                            waits=[P.ev("qt"), P.ev("qv")])
                tev = transpose_fm_to_tokmaj(
                    [z[:, c * OWN + i * 128: c * OWN + i * 128 + 128] for c in range(DC)],
                    ptk, True, extra_waits=[P.ev("qv"), P.ev("qs")])
                xmev = P.emit("vector", lambda e, buf=buf: e.tensor_add(xmt[:], buf, ptk[:]),
                              waits=[tev, xev], inc="qv")
                P.dma("sync", xmid_hbm[i * 128:(i + 1) * 128, :], xmt[:], waits=[xmev])
                agg = stats_on(xmt[:], waits=[xmev])
                rse = mk_scale(1, 3, 5, True, agg)
                nme = neg_m_rs(5, 7, rse)
                tev = transpose_tokmaj_to_fm(buf, ptk, True, extra_waits=[xmev, nme])
                zfm = P.emit("vector", lambda e, i=i: e.tensor_add(
                    _r(z[:], "p (c n) -> p c n", c=DC)[:, :, i * 128:(i + 1) * 128],
                    _r(z[:], "p (c n) -> p c n", c=DC)[:, :, i * 128:(i + 1) * 128],
                    _r(ptk[:], "p (c n) -> p c n", c=DC)),
                    waits=[tev], inc="qv")
                P.emit("tensor", lambda e: e.matmul(
                    pcs[0:1, 128:256], stat[:, 0:1], idf_sb[:],
                    is_transpose=True, start=True, stop=True),
                    waits=[nme, zfm], inc="qt")
                mre = P.emit("vector", lambda e: e.tensor_copy(mrow[:], pcs[0:1, 128:256]),
                             waits=[P.ev("qt")], inc="qv")
                for kc in range(DC):
                    P.emit("tensor", lambda e, kc=kc, i=i: e.matmul(
                        pl[:, 0:NE], z[:, kc * OWN + i * 128: kc * OWN + i * 128 + 128],
                        fwg_sb[:, kc * NE:(kc + 1) * NE],
                        start=(kc == 0), stop=False),
                        waits=([zfm, cs_ready, mre] if kc == 0 else []))
                P.emit("tensor", lambda e: e.matmul(
                    pl[:, 0:NE], mrow[:], csr[:], start=False, stop=True), inc="qt")
                lgev = P.ev("qt")
                e1 = P.emit("vector", lambda e: e.tensor_scalar_mul(
                    lgt[:], pl[:, 0:NE], stat[:, 5:6]), waits=[lgev], inc="qv")
                e2 = P.emit("vector", lambda e: e.tensor_add(lgt[:], lgt[:], bgr_sb[:]),
                            waits=[e1, fev], inc="qv")
                if debug_phase == "F":
                    P.dma("sync", taps["t_logits"][:, i * NE:(i + 1) * NE], lgt[:], waits=[e2])
                e3 = P.emit("vector", lambda e: e.max(mx8[:], lgt[:]), waits=[e2], inc="qv")
                e4 = P.emit("vector", lambda e: e.max_index(ix8[:], mx8[:], lgt[:]),
                            waits=[e3], inc="qv")
                e5 = P.emit("vector", lambda e, i=i: e.tensor_copy(idxf[:, i:i + 1], ix8[:, 0:1]),
                            waits=[e4], inc="qv")
                e6 = P.emit("vector", lambda e, i=i: e.tensor_single_scalar(
                    ohm[:], io8_sb[:], idxf[:, i:i + 1], ALU.is_equal),
                    waits=[e5], inc="qv")
                P.emit("tensor", lambda e: e.matmul(
                    pcs[0:8, 0:128], ohm[:], idf_sb[:], is_transpose=True,
                    start=True, stop=True), waits=[e6], inc="qt")
                P.emit("vector", lambda e, i=i: e.tensor_copy(
                    oh_t[:, i * 128:(i + 1) * 128], pcs[0:8, 0:128]),
                    waits=[P.ev("qt")], inc="qv")
            P.barrier()

            scev = P.emit("vector", lambda e: e.tensor_tensor_scan(
                post[:], onest[:], oh_t[:], 0.0, ALU.mult, ALU.add),
                waits=[P.ev("qv"), gz], inc="qv")
            for i in range(16):
                P.emit("tensor", lambda e, i=i: e.matmul(
                    pcs[0:128, 0:8], post[:, i * 128:(i + 1) * 128], idf_sb[0:8, 0:8],
                    is_transpose=True, start=True, stop=True),
                    waits=[scev, P.ev("qv")], inc="qt")
                e0 = P.emit("vector", lambda e: e.tensor_copy(posm[:], pcs[0:128, 0:8]),
                            waits=[P.ev("qt")], inc="qv")
                e1 = P.emit("vector", lambda e, i=i: e.tensor_single_scalar(
                    ohm[:], io8_sb[:], idxf[:, i:i + 1], ALU.is_equal), waits=[e0], inc="qv")
                e2 = P.emit("vector", lambda e: e.tensor_mul(posm[:], posm[:], ohm[:]),
                            waits=[e1], inc="qv")
                e3 = P.emit("vector", lambda e: e.reduce_sum(selc[:, 0:1], posm[:], axis=AX.X),
                            waits=[e2], inc="qv")
                e4 = P.emit("vector", lambda e, i=i: e.scalar_tensor_tensor(
                    selc[:, 1:2], idxf[:, i:i + 1], float(CAP), selc[:, 0:1],
                    ALU.mult, ALU.add), waits=[e3], inc="qv")
                e5 = P.emit("vector", lambda e: e.tensor_scalar_add(
                    selc[:, 1:2], selc[:, 1:2], -1.0), waits=[e4], inc="qv")
                e6 = P.emit("vector", lambda e: e.tensor_single_scalar(
                    selc[:, 2:3], selc[:, 0:1], float(CAP) + 0.5, ALU.is_gt),
                    waits=[e5], inc="qv")
                e7 = P.emit("vector", lambda e: e.scalar_tensor_tensor(
                    selc[:, 3:4], selc[:, 1:2], -1.0, selc[:, 2:3], ALU.mult, ALU.mult),
                    waits=[e6], inc="qv")
                e8 = P.emit("vector", lambda e: e.scalar_tensor_tensor(
                    selc[:, 4:5], selc[:, 2:3], float(TRASH), selc[:, 3:4],
                    ALU.mult, ALU.add), waits=[e7], inc="qv")
                e9 = P.emit("vector", lambda e: e.tensor_add(
                    selc[:, 1:2], selc[:, 1:2], selc[:, 4:5]), waits=[e8], inc="qv")
                e10 = P.emit("vector", lambda e, i=i: e.tensor_copy(
                    slot_i[:, i:i + 1], selc[:, 1:2]), waits=[e9], inc="qv")
                if debug_phase == "F":
                    P.dma("sync", taps["t_slot"][:, i:i + 1], slot_i[:, i:i + 1], waits=[e10],
                          allow_slow_non_contiguous=True)
            slots_ready = P.ev("qv")

            for i in range(16):
                buf = xb3[:, (i % 2) * D:(i % 2) * D + D]
                xev = P.dma("sync", buf, xmid_hbm[i * 128:(i + 1) * 128, :],
                            waits=[slots_ready, P.ev("qs")])
                agg = stats_on(buf, waits=[xev])
                rse = mk_scale(1, 3, 5, True, agg)
                nme = neg_m_rs(5, 7, rse)
                aev = P.emit("scalar", lambda e, buf=buf: e.activation(
                    uft[:], buf, AF.Identity, bias=stat[:, 7:8], scale=stat[:, 5:6]),
                    waits=[nme], inc="qs")
                P.wait("gpsimd", [aev, slots_ready, P.ev("dg")])
                P.emit("gpsimd", lambda e, i=i: e.indirect_dma_start(
                    out=xg_hbm, out_offset=IndirectOffsetOnAxis(slot_i[:, i:i + 1], 0),
                    in_=uft[:], in_offset=None), inc="dg", n=16)
            P.barrier()

            if debug_phase == "F":
                tmp = SB(phf, "dbgf", [128, D], F32)
                for i in range(16):
                    ev = P.dma("sync", tmp[:], xmid_hbm[i * 128:(i + 1) * 128, :])
                    P.dma("sync", taps["t_xmid"][:, i * D:(i + 1) * D], tmp[:], waits=[ev])
                _finish(P, nc, es)
                return nc
        zstack.close()

        with contextlib.ExitStack() as phx, contextlib.ExitStack() as psx:
            xg_t = SB(phx, "xg_t", [128, 2 * D], BF16)
            xg_f = SB(phx, "xg_f", [128, DC * CAP], BF16)
            h_sb = SB(phx, "h_sb", [128, 32 * CAP], BF16)
            w1s_ = SB(phx, "w1s_", [128, 2 * DC * 512], BF16)
            w2c = SB(phx, "w2c", [128, 2 * 32 * 128], BF16)
            y_f = SB(phx, "y_f", [128, DC * CAP], BF16)
            y_t = SB(phx, "y_t", [128, D], BF16)
            ph1 = PS(psx, "ph1", [128, CAP], F32)
            ph2 = PS(psx, "ph2", [128, CAP], F32)
            py = PS(psx, "py", [128, CAP], F32)
            ptf = PS(psx, "ptfF", [128, 1024], BF16)

            for ex in range(NE):
                for i in range(CAP // 128):
                    gev = P.dma("sync", xg_t[:, (i % 2) * D:(i % 2) * D + D],
                                xg_hbm[ex * CAP + i * 128: ex * CAP + (i + 1) * 128, :],
                                waits=[P.ev("qt"), P.ev("qv")])
                    tev = transpose_tokmaj_to_fm(
                        xg_t[:, (i % 2) * D:(i % 2) * D + D], ptf, False, extra_waits=[gev])
                    P.emit("vector", lambda e, i=i: e.tensor_copy(
                        _r(xg_f[:], "p (c n) -> p c n", c=DC)[:, :, i * 128:(i + 1) * 128],
                        _r(ptf[:], "p (c n) -> p c n", c=DC)),
                        waits=[tev], inc="qv")
                xg_ready = P.ev("qv")

                wflip = 0
                wfree = {0: None, 1: None}
                for hb in range(FH // 512):
                    wsl = w1s_[:, wflip * DC * 512:(wflip + 1) * DC * 512]
                    wev = P.dma("gpsimd", _r(wsl, "p (c n) -> p c n", c=DC),
                                _r(fw1[ex, :, hb * 512:(hb + 1) * 512], "(c p) n -> p c n", p=128),
                                waits=[wfree[wflip]])
                    last_mm = None
                    for sub in range(4):
                        hp = ph1 if sub % 2 == 0 else ph2
                        m0 = hb * 4 + sub
                        for nb in range(CAP // 512):
                            for kc in range(DC):
                                P.emit("tensor", lambda e, kc=kc, nb=nb, sub=sub, wsl=wsl,
                                       hp=hp: e.matmul(
                                    hp[:, nb * 512:(nb + 1) * 512],
                                    wsl[:, kc * 512 + sub * 128: kc * 512 + sub * 128 + 128],
                                    xg_f[:, kc * CAP + nb * 512: kc * CAP + (nb + 1) * 512],
                                    start=(kc == 0), stop=(kc == DC - 1)),
                                    waits=([xg_ready, wev, P.ev("qs")]
                                           if (kc == 0 and nb == 0) else []),
                                    inc="qt" if (kc == DC - 1 and nb == CAP // 512 - 1) else None)
                        mmev = P.ev("qt")
                        last_mm = mmev
                        P.emit("scalar", lambda e, m0=m0, hp=hp: e.activation(
                            h_sb[:, m0 * CAP:(m0 + 1) * CAP], hp[:], AF.Gelu_apprx_tanh),
                            waits=[mmev], inc="qs")
                    wfree[wflip] = last_mm
                    wflip ^= 1
                h_ready = P.ev("qs")

                w2flip = 0
                w2free = {0: None, 1: None}
                for dm in range(DC):
                    wsl = w2c[:, w2flip * 32 * 128:(w2flip + 1) * 32 * 128]
                    wev = P.dma("gpsimd", _r(wsl, "p (c n) -> p c n", c=32),
                                _r(fw2[ex, :, dm * 128:(dm + 1) * 128], "(c p) n -> p c n", p=128),
                                waits=[w2free[w2flip]])
                    for nb in range(CAP // 512):
                        for kc in range(32):
                            P.emit("tensor", lambda e, kc=kc, nb=nb, wsl=wsl: e.matmul(
                                py[:, nb * 512:(nb + 1) * 512],
                                wsl[:, kc * 128:(kc + 1) * 128],
                                h_sb[:, kc * CAP + nb * 512: kc * CAP + (nb + 1) * 512],
                                start=(kc == 0), stop=(kc == 31)),
                                waits=([h_ready, wev, P.ev("qs")]
                                       if (kc == 0 and nb == 0) else []),
                                inc="qt" if (kc == 31 and nb == CAP // 512 - 1) else None)
                    mmev = P.ev("qt")
                    w2free[w2flip] = mmev
                    w2flip ^= 1
                    P.emit("scalar", lambda e, dm=dm: e.activation(
                        y_f[:, dm * CAP:(dm + 1) * CAP], py[:], AF.Identity),
                        waits=[mmev], inc="qs")
                yf_ready = P.ev("qs")
                for i in range(CAP // 128):
                    tev = transpose_fm_to_tokmaj(
                        [y_f[:, c * CAP + i * 128: c * CAP + i * 128 + 128] for c in range(DC)],
                        ptf, False, extra_waits=[yf_ready, P.ev("ds")])
                    yev = P.emit("vector", lambda e: e.tensor_copy(y_t[:], ptf[:]),
                                 waits=[tev, P.ev("ds")], inc="qv")
                    P.dma("sync", yg_hbm[ex * CAP + i * 128: ex * CAP + (i + 1) * 128, :],
                          y_t[:], waits=[yev])
            P.barrier()

        with contextlib.ExitStack() as pho:
            xmo = SB(pho, "xmo", [128, D], F32)
            ygo = SB(pho, "ygo", [128, D], BF16)
            oto = SB(pho, "oto", [128, D], F32)
            for i in range(16):
                xev = P.dma("sync", xmo[:], xmid_hbm[i * 128:(i + 1) * 128, :],
                            waits=[P.ev("ds"), P.ev("qv")])
                gev = P.emit("gpsimd", lambda e, i=i: e.indirect_dma_start(
                    out=ygo[:], out_offset=None, in_=yg_hbm,
                    in_offset=IndirectOffsetOnAxis(slot_i[:, i:i + 1], 0)),
                    waits=[P.ev("ds"), P.ev("qv")], inc="dg", n=16)
                aev = P.emit("vector", lambda e: e.tensor_add(oto[:], xmo[:], ygo[:]),
                             waits=[xev, gev], inc="qv")
                P.dma("sync", yout[i * 128:(i + 1) * 128, :], oto[:], waits=[aev])
        _finish(P, nc, es)
    return nc


def _finish(P, nc, es):
    P.barrier()
    blk = es.enter_context(nc.Block())
    for eng in ENGINES:
        fns = P.streams[eng]
        getattr(blk, eng)(lambda e, fns=fns: [f(e) for f in fns])



_NC_CACHE = {}


def _get_nc():
    key = (DEBUG_PHASE, STRICT)
    if key not in _NC_CACHE:
        _NC_CACHE[key] = build_program(DEBUG_PHASE, STRICT)
    return _NC_CACHE[key]


def _host_tables(h):
    own0 = h * 2048
    g = own0 - 1024 + np.arange(ZLEN, dtype=np.float64)
    inv = (1.0 / 10000.0) ** (np.arange(0, HD, 2, dtype=np.float64) / HD)
    invrep = np.repeat(inv, 2)
    ang = invrep[:, None] * g[None, :]
    cost = np.cos(ang).astype(np.float32)
    sint = np.sin(ang).astype(np.float32)
    sint[0::2, :] *= -1.0
    mb = np.zeros((16, BAND), np.float32)
    for qt in range(16):
        gk = own0 - 1024 + qt * 128 + np.arange(BAND)
        mb[qt, gk < 0] = NEG
    return (cost.astype(BFD), sint.astype(BFD), np.ascontiguousarray(mb).astype(BFD))


def kernel(**inputs):
    x = np.asarray(inputs["L0_attn_x"], np.float32)
    T = lambda a: np.ascontiguousarray(np.asarray(a, np.float32).T)
    col = lambda a: np.asarray(a, np.float32).reshape(-1, 1)

    shared = {
        "winT": T(inputs["L0_attn_hawk_Win"]),
        "wgT": T(inputs["L0_attn_hawk_Wg"]),
        "woutT": T(inputs["L0_attn_hawk_Wout"]),
        "cwi": np.ascontiguousarray(np.asarray(inputs["L0_attn_hawk_cw"], np.float32)
                                    .reshape(HID, 4)),
        "cbi": col(inputs["L0_attn_hawk_cb"]),
        "hbg": col(inputs["L0_attn_hawk_bg"]),
        "hfb": col(inputs["L0_attn_hawk_fb"]),
        "g1gT": T(inputs["L0_attn_g1_Wgrow"]),
        "g1sT": T(inputs["L0_attn_g1_Wshrink"]),
        "g2gT": T(inputs["L0_attn_g2_Wgrow"]),
        "g2sT": T(inputs["L0_attn_g2_Wshrink"]),
        "wqT": T(inputs["L0_attn_smqa_Wq"]),
        "wkvT": T(inputs["L0_attn_smqa_Wkv"]),
        "woT": T(inputs["L0_attn_smqa_Wout"]),
        "fw1": np.ascontiguousarray(np.asarray(inputs["L0_ffn_W1"], np.float32)),
        "fw2": np.ascontiguousarray(np.asarray(inputs["L0_ffn_W2"], np.float32)),
        "fwg": np.ascontiguousarray(np.asarray(inputs["L0_ffn_Wg"], np.float32)),
        "gnh": col(inputs["L0_attn_gn_hawk"]),
        "gn1": col(inputs["L0_attn_gn_hgmlp"]),
        "gns": col(inputs["L0_attn_gn_smqa"]),
        "gn2": col(inputs["L0_attn_gn_sgmlp"]),
        "bgrep": np.ascontiguousarray(np.broadcast_to(
            np.asarray(inputs["L0_ffn_bg"], np.float32), (128, NE))),
        "identf": np.eye(128, dtype=np.float32),
        "identb": np.eye(128, dtype=np.float32).astype(BFD),
        "onesb": np.ones((1, 128), np.float32).astype(BFD),
        "iota8": np.ascontiguousarray(np.broadcast_to(
            np.arange(NE, dtype=np.float32), (128, NE))),
    }
    mconst = np.zeros((128, BAND), np.float32)
    r = np.arange(128)[:, None]
    j = np.arange(BAND)[None, :]
    mconst[~((j >= r) & (j <= r + WSZ))] = NEG
    shared["mconst"] = mconst.astype(BFD)

    tabs = {h: _host_tables(h) for h in (0, 1)}
    in_maps = []
    for core in range(8):
        b, h = core // 2, core % 2
        if h == 1:
            xw = x[b]
        else:
            xw = np.concatenate([x[b, 0:2048], x[b, 0:2048]], 0)
        cost, sint, mb = tabs[h]
        m = dict(shared)
        m["xwin"] = np.ascontiguousarray(xw)
        m["flagc"] = np.full((128, 1), float(h), np.float32)
        m["epsc"] = np.full((128, 1), EPS, np.float32)
        m["costab"] = cost
        m["sintab"] = sint
        m["mbias"] = mb
        in_maps.append(m)

    nc = _get_nc()
    res = run_bass_kernel_spmd(nc, in_maps, core_ids=list(range(8)))
    out = np.empty((4, 4096, D), np.float32)
    for core in range(8):
        b, h = core // 2, core % 2
        out[b, h * 2048:(h + 1) * 2048] = np.asarray(res.results[core]["y"], np.float32)
    return out


# version 03
# speedup vs baseline: 1.8382x; 1.1244x over previous
import contextlib
import numpy as np
import ml_dtypes
import concourse.bass as bass
import concourse.mybir as mybir
from concourse.bass_utils import run_bass_kernel_spmd
from concourse.bass_core_rust import IndirectOffsetOnAxis

F32 = mybir.dt.float32
BF16 = mybir.dt.bfloat16
I32 = mybir.dt.int32
U32 = mybir.dt.uint32
AF = mybir.ActivationFunctionType
ALU = mybir.AluOpType
AX = mybir.AxisListType
BFD = ml_dtypes.bfloat16

D = 1024
DC = 8
HID = 1536
HC = 12
GH = 2048
HD = 128
QH = 8
NE = 8
FH = 4096
WIN = 4096
OWN = 2048
PRE = 2048
KV0 = 1024
ZLEN = 3072
BAND = 1280
WSZ = 1024
CAP = 1024
TRASH = NE * CAP
NEG = -1.0e30
EPS = 1e-5
SCL = float(1.0 / np.sqrt(128.0))

STRICT = False
DEBUG_PHASE = None

ENGINES = ("sync", "gpsimd", "vector", "scalar", "tensor")
SEMS = ("ds", "dg", "qv", "qs", "qt", "qg")


class Plan:
    def __init__(self, nc, strict):
        self.nc = nc
        self.strict = strict
        self.streams = {e: [] for e in ENGINES}
        self.sem = {}
        self.cnt = {s: 0 for s in SEMS}
        self.seen = {e: {s: 0 for s in SEMS} for e in ENGINES}

    def ev(self, name):
        return (name, self.cnt[name])

    def all_ev(self):
        return [self.ev(s) for s in SEMS]

    def emit(self, eng, f, waits=(), inc=None, n=1):
        if self.strict:
            waits = self.all_ev()
        pruned = []
        seen = self.seen[eng]
        for w in waits:
            if w is None:
                continue
            s, v = w
            if v <= 0 or seen[s] >= v:
                continue
            seen[s] = v
            pruned.append((s, v))
        if inc is not None:
            self.cnt[inc] += n
        sems = self.sem

        def runner(e, f=f, pruned=tuple(pruned), inc=inc, n=n):
            for s, v in pruned:
                e.wait_ge(sems[s], v)
            ins = f(e)
            if inc is not None:
                ins.then_inc(sems[inc], n)
        self.streams[eng].append(runner)
        return self.ev(inc) if inc is not None else None

    def wait(self, eng, waits):
        pruned = []
        seen = self.seen[eng]
        for w in waits:
            if w is None:
                continue
            s, v = w
            if v <= 0 or seen[s] >= v:
                continue
            seen[s] = v
            pruned.append((s, v))
        if not pruned:
            return
        sems = self.sem

        def runner(e, pruned=tuple(pruned)):
            for s, v in pruned:
                e.wait_ge(sems[s], v)
        self.streams[eng].append(runner)

    def dma(self, eng, out, in_, waits=(), **kw):
        sem = "ds" if eng in ("sync", "scalar") else "dg"
        return self.emit(eng, lambda e, o=out, i=in_: e.dma_start(out=o, in_=i, **kw),
                         waits=waits, inc=sem, n=16)

    def barrier(self):
        evs = self.all_ev()
        for eng in ENGINES:
            self.wait(eng, evs)


def _r(ap, pat, **kw):
    return ap.rearrange(pat, **kw)


def build_program(debug_phase=None, strict=STRICT):
    nc = bass.Bass("TRN2", target_bir_lowering=False, debug=False)

    def din(name, shape, dt=F32):
        return nc.dram_tensor(name, list(shape), dt, kind="ExternalInput").ap()

    xwin = din("xwin", [WIN, D])
    flagc = din("flagc", [128, 1])
    costab = din("costab", [128, ZLEN], BF16)
    sintab = din("sintab", [128, ZLEN], BF16)
    mconst = din("mconst", [128, BAND], BF16)
    mbias = din("mbias", [16, BAND], BF16)
    identf = din("identf", [128, 128])
    identb = din("identb", [128, 128], BF16)
    onesb = din("onesb", [1, 128], BF16)
    iota8 = din("iota8", [128, NE])
    bgrep = din("bgrep", [128, NE])
    winT = din("winT", [D, 2 * HID])
    wgT = din("wgT", [HID, 2 * HID])
    woutT = din("woutT", [HID, D])
    cwi = din("cwi", [HID, 4])
    cbi = din("cbi", [HID, 1])
    hbg = din("hbg", [2 * HID, 1])
    hfb = din("hfb", [HID, 1])
    g1gT = din("g1gT", [D, 2 * GH])
    g1sT = din("g1sT", [GH, D])
    g2gT = din("g2gT", [D, 2 * GH])
    g2sT = din("g2sT", [GH, D])
    wqT = din("wqT", [D, D])
    wkvT = din("wkvT", [D, 2 * HD])
    woT = din("woT", [D, D])
    fw1 = din("fw1", [NE, D, FH])
    fw2 = din("fw2", [NE, FH, D])
    fwg = din("fwg", [D, NE])
    gnh = din("gnh", [D, 1])
    gn1 = din("gn1", [D, 1])
    gns = din("gns", [D, 1])
    gn2 = din("gn2", [D, 1])
    epsc = din("epsc", [128, 1])

    yout = nc.dram_tensor("y", [OWN, D], F32, kind="ExternalOutput").ap()

    gate_hbm = nc.dram_tensor("gate_hbm", [HID, ZLEN], BF16).ap()
    hconv_hbm = nc.dram_tensor("hconv_hbm", [HID, WIN], BF16).ap()
    hst_hbm = nc.dram_tensor("hst_hbm", [HID, ZLEN], BF16).ap()
    xg_hbm = nc.dram_tensor("xg_hbm", [TRASH + 1, D], BF16).ap()
    yg_hbm = nc.dram_tensor("yg_hbm", [TRASH + 1, D], BF16).ap()
    xmid_hbm = nc.dram_tensor("xmid_hbm", [OWN, D], F32).ap()

    taps = {}
    if debug_phase is not None:
        for nm, shape, dt in [
            ("t_u", [128, DC * WIN], BF16), ("t_hconv", [128, HC * WIN], BF16),
            ("t_gate", [128, HC * ZLEN], BF16), ("t_hst", [128, HC * ZLEN], BF16),
            ("t_z", [128, DC * OWN], F32), ("t_zh", [128, DC * KV0], BF16),
            ("t_k", [128, 3200], BF16), ("t_v", [128, 3200], BF16),
            ("t_logits", [128, 16 * NE], F32), ("t_slot", [128, 16], I32),
            ("t_xmid", [128, 16 * D], F32),
        ]:
            taps[nm] = nc.dram_tensor(nm, shape, dt, kind="ExternalOutput").ap()

    es = contextlib.ExitStack()
    with es:
        P = Plan(nc, strict)
        for s in SEMS:
            P.sem[s] = es.enter_context(nc.semaphore(s))

        _uid = [0]

        def SB(st, name, shape, dt):
            _uid[0] += 1
            return st.enter_context(nc.sbuf_tensor(f"{name}_{_uid[0]}", list(shape), dt))

        def PS(st, name, shape, dt):
            _uid[0] += 1
            return st.enter_context(nc.psum_tensor(f"{name}_{_uid[0]}", list(shape), dt))

        idf_sb = SB(es, "idf", [128, 128], F32)
        idb_sb = SB(es, "idb", [128, 128], BF16)
        ones_sb = SB(es, "ones1", [1, 128], BF16)
        flag_sb = SB(es, "flag", [128, 1], F32)
        gn_sb = SB(es, "gn", [128, DC * 4], F32)
        small = SB(es, "small", [128, 64], F32)
        cw_sb = SB(es, "cw4", [128, HC * 4], F32)
        stat = SB(es, "stat", [128, 8], F32)
        s6 = SB(es, "s6", [128, 12], F32)
        slot_i = SB(es, "slot_i", [128, 16], I32)
        eps_sb = SB(es, "eps_sb", [128, 1], F32)
        idxf = SB(es, "idxf", [128, 16], F32)

        P.dma("sync", idf_sb[:], identf)
        P.dma("sync", idb_sb[:], identb)
        P.dma("sync", ones_sb[:], onesb)
        P.dma("sync", flag_sb[:], flagc)
        P.dma("sync", eps_sb[:], epsc)
        for i, g in enumerate((gnh, gn1, gns, gn2)):
            P.dma("sync", _r(gn_sb[:, i * DC:(i + 1) * DC], "p (c one) -> p c one", c=DC),
                  _r(g, "(c p) one -> p c one", p=128), allow_slow_non_contiguous=True)
        P.dma("sync", _r(cw_sb[:], "p (c k) -> p c k", c=HC), _r(cwi, "(c p) k -> p c k", p=128), allow_slow_non_contiguous=True)
        P.dma("sync", _r(small[:, 36:48], "p (c o) -> p c o", c=HC),
              _r(cbi, "(c p) one -> p c one", p=128), allow_slow_non_contiguous=True)
        P.dma("sync", _r(small[:, 48:60], "p (c o) -> p c o", c=HC),
              _r(hfb, "(c p) one -> p c one", p=128), allow_slow_non_contiguous=True)
        P.dma("sync", _r(small[:, 12:24], "p (c o) -> p c o", c=HC),
              _r(hbg[0:HID], "(c p) one -> p c one", p=128), allow_slow_non_contiguous=True)
        P.dma("sync", _r(small[:, 24:36], "p (c o) -> p c o", c=HC),
              _r(hbg[HID:2 * HID], "(c p) one -> p c one", p=128), allow_slow_non_contiguous=True)
        ldev = P.ev("ds")
        fb = small[:, 48:60]
        P.emit("scalar", lambda e: e.activation(small[:, 0:12], fb, AF.Abs),
               waits=[ldev], inc="qs")
        P.emit("scalar", lambda e: e.activation(small[:, 0:12], small[:, 0:12], AF.Exp,
                                                scale=-1.0), waits=[P.ev("qs")], inc="qs")
        P.emit("scalar", lambda e: e.activation(small[:, 0:12], small[:, 0:12], AF.Ln,
                                                bias=1.0), waits=[P.ev("qs")], inc="qs")
        P.emit("vector", lambda e: e.tensor_scalar_max(stat[:, 0:6], fb[:, 0:6], 0.0),
               waits=[ldev], inc="qv")
        P.emit("vector", lambda e: e.tensor_scalar_max(s6[:, 0:6], fb[:, 6:12], 0.0),
               waits=[ldev], inc="qv")
        P.emit("vector", lambda e: e.tensor_add(small[:, 0:6], small[:, 0:6], stat[:, 0:6]),
               waits=[P.ev("qs"), P.ev("qv")], inc="qv")
        P.emit("vector", lambda e: e.tensor_add(small[:, 6:12], small[:, 6:12], s6[:, 0:6]),
               waits=[P.ev("qv")], inc="qv")
        P.emit("vector", lambda e: e.tensor_scalar_mul(small[:, 0:12], small[:, 0:12], -4.0),
               waits=[P.ev("qv")], inc="qv")
        P.emit("vector", lambda e: e.tensor_scalar_mul(small[:, 12:36], small[:, 12:36], 0.5),
               waits=[ldev], inc="qv")
        consts_ev = P.ev("qv")

        def load_wT(eng, dst_sb, src_ap, kdim, ncols, waits=()):
            kc = kdim // 128
            return P.dma(eng, _r(dst_sb[:, 0:kc * ncols], "p (c n) -> p c n", c=kc),
                         _r(src_ap, "(c p) n -> p c n", p=128), waits=waits)

        def stats_on(tok_tile, waits=()):
            P.emit("vector", lambda e: e.bn_stats(s6[:, 0:6], tok_tile[:, 0:512]),
                   waits=list(waits), inc="qv")
            P.emit("vector", lambda e: e.bn_stats(s6[:, 6:12], tok_tile[:, 512:1024]),
                   waits=[P.ev("qv")], inc="qv")
            return P.emit("vector", lambda e: e.bn_aggr(stat[:, 0:2], s6[:]),
                          waits=[P.ev("qv")], inc="qv")

        def mk_scale(col_in, col_sq, col_rs, eps, aggr_ev):
            bias = 0.0 if eps is None else eps_sb[:, 0:1]
            if eps is not None:
                aggr_ev = P.emit("vector", lambda e: e.tensor_scalar_add(
                    stat[:, col_in:col_in + 1], stat[:, col_in:col_in + 1], EPS),
                    waits=[aggr_ev], inc="qv")
            e1 = P.emit("scalar", lambda e: e.activation(
                stat[:, col_sq:col_sq + 1], stat[:, col_in:col_in + 1], AF.Sqrt),
                waits=[aggr_ev], inc="qs")
            e2 = P.emit("vector", lambda e: e.reciprocal(
                stat[:, col_rs:col_rs + 1], stat[:, col_sq:col_sq + 1]),
                waits=[e1], inc="qv")
            e3 = P.emit("vector", lambda e: e.tensor_mul(
                stat[:, col_sq:col_sq + 1], stat[:, col_rs:col_rs + 1],
                stat[:, col_rs:col_rs + 1]), waits=[e2], inc="qv")
            e4 = P.emit("vector", lambda e: e.tensor_mul(
                stat[:, col_sq:col_sq + 1], stat[:, col_sq:col_sq + 1],
                stat[:, col_in:col_in + 1]), waits=[e3], inc="qv")
            e5 = P.emit("vector", lambda e: e.tensor_scalar(
                stat[:, col_sq:col_sq + 1], stat[:, col_sq:col_sq + 1], -0.5, 1.5,
                ALU.mult, ALU.add), waits=[e4], inc="qv")
            return P.emit("vector", lambda e: e.tensor_mul(
                stat[:, col_rs:col_rs + 1], stat[:, col_rs:col_rs + 1],
                stat[:, col_sq:col_sq + 1]), waits=[e5], inc="qv")

        def neg_m_rs(col_rs, col_out, rs_ev):
            return P.emit("vector", lambda e: e.scalar_tensor_tensor(
                stat[:, col_out:col_out + 1], stat[:, 0:1], -1.0,
                stat[:, col_rs:col_rs + 1], ALU.mult, ALU.mult),
                waits=[rs_ev], inc="qv")

        def transpose_tokmaj_to_fm(src_tok, pst, dtype_f32, extra_waits=()):
            bank_cols = 4 if dtype_f32 else 8
            idm = idf_sb if dtype_f32 else idb_sb
            for c in range(DC):
                P.emit("tensor", lambda e, c=c: e.matmul(
                    pst[:, c * 128:(c + 1) * 128], src_tok[:, c * 128:(c + 1) * 128],
                    idm[:], is_transpose=True,
                    start=(c % bank_cols == 0), stop=(c == DC - 1)),
                    waits=(list(extra_waits) if c == 0 else []),
                    inc="qt" if c == DC - 1 else None)
            return P.ev("qt")

        def transpose_fm_to_tokmaj(src_fm_slices, pst, dtype_f32, extra_waits=()):
            bank_cols = 4 if dtype_f32 else 8
            idm = idf_sb if dtype_f32 else idb_sb
            for c, sl in enumerate(src_fm_slices):
                P.emit("tensor", lambda e, c=c, sl=sl: e.matmul(
                    pst[:, c * 128:(c + 1) * 128], sl, idm[:], is_transpose=True,
                    start=(c % bank_cols == 0), stop=(c == DC - 1)),
                    waits=(list(extra_waits) if c == 0 else []),
                    inc="qt" if c == DC - 1 else None)
            return P.ev("qt")

        def z_slices(wp, width=128):
            own = wp >= PRE
            src = z if own else zh
            base = (wp - PRE) if own else (wp - KV0)
            w = OWN if own else KV0
            return [src[:, c * w + base: c * w + base + width] for c in range(DC)], own

        zstack = es.enter_context(contextlib.ExitStack())

        with contextlib.ExitStack() as pha:
            wg_sb = SB(pha, "wg_sb", [128, HC * 2 * HID], BF16)

            with contextlib.ExitStack() as psa:
                win_sb = SB(psa, "win_sb", [128, DC * 2 * HID], BF16)
                xa = SB(psa, "xa", [128, 2 * D], F32)
                u_fm = SB(psa, "u_fm", [128, DC * 1024], BF16)
                utok = SB(psa, "utok", [128, D], BF16)
                hpre = SB(psa, "hpre", [128, 1024 + 3], BF16)
                hcv = SB(psa, "hcv", [128, 1024], BF16)
                gtmp = SB(psa, "gtmp", [128, 1024], BF16)
                carry3 = SB(psa, "carry3", [128, HC * 3], BF16)
                psT = PS(psa, "psT", [128, 1024], BF16)
                pm0 = PS(psa, "pm0", [128, 1024], F32)
                pm1 = PS(psa, "pm1", [128, 1024], F32)

                wev = load_wT("gpsimd", win_sb, winT, D, 2 * HID)
                for c in range(DC):
                    P.emit("vector", lambda e, c=c: e.tensor_scalar_mul(
                        win_sb[:, c * 2 * HID:(c + 1) * 2 * HID],
                        win_sb[:, c * 2 * HID:(c + 1) * 2 * HID], gn_sb[:, c:c + 1]),
                        waits=[wev, consts_ev], inc="qv")
                win_ready = P.ev("qv")
                wg_ev = load_wT("gpsimd", wg_sb, wgT, HID, 2 * HID)
                P.emit("gpsimd", lambda e: e.memset(carry3[:], 0.0), inc="qg")
                cz_ev = P.ev("qg")

                pm_free = {0: None, 1: None}
                TCH = 1024
                for ch in range(WIN // TCH):
                    t0 = ch * TCH
                    utile_evs = []
                    for i in range(TCH // 128):
                        buf = xa[:, (i % 2) * D:(i % 2) * D + D]
                        xev = P.dma("sync", buf, xwin[t0 + i * 128: t0 + (i + 1) * 128, :],
                                    waits=[P.ev("qs")])
                        agg = stats_on(buf, waits=[xev])
                        rse = mk_scale(1, 2, 4, None, agg)
                        nme = neg_m_rs(4, 6, rse)
                        aev = P.emit("scalar", lambda e, buf=buf: e.activation(
                            utok[:], buf, AF.Identity, bias=stat[:, 6:7], scale=stat[:, 4:5]),
                            waits=[nme], inc="qs")
                        tev = transpose_tokmaj_to_fm(utok, psT, False,
                                                     extra_waits=[aev, win_ready, P.ev("qv")])
                        cev = P.emit("vector", lambda e, i=i: e.tensor_copy(
                            _r(u_fm[:], "p (c n) -> p c n", c=DC)[:, :, i * 128:(i + 1) * 128],
                            _r(psT[:], "p (c n) -> p c n", c=DC)),
                            waits=[tev], inc="qv")
                        utile_evs.append(cev)
                    u_ready = P.ev("qv")
                    if debug_phase == "A1" and taps:
                        P.dma("sync", _r(taps["t_u"], "p (c w) -> p c w", c=DC)[:, :, t0:t0 + TCH],
                              _r(u_fm[:], "p (c n) -> p c n", c=DC), waits=[u_ready])

                    do_gate = t0 >= KV0
                    mlist = [("h", m) for m in range(HC)] + \
                            ([("g", m) for m in range(HC)] if do_gate else [])
                    for kind, m in mlist:
                        col0 = (HID if kind == "h" else 0) + m * 128
                        pmi = m % 2
                        pm = pm0 if pmi == 0 else pm1
                        for nb in range(2):
                            for kc in range(DC):
                                P.emit("tensor", lambda e, kc=kc, nb=nb, col0=col0, pm=pm: e.matmul(
                                    pm[:, nb * 512:(nb + 1) * 512],
                                    win_sb[:, kc * 2 * HID + col0: kc * 2 * HID + col0 + 128],
                                    u_fm[:, kc * 1024 + nb * 512: kc * 1024 + (nb + 1) * 512],
                                    start=(kc == 0), stop=(kc == DC - 1)),
                                    waits=([u_ready, pm_free[pmi], wg_ev if kind == "h" else None]
                                           if (kc == 0 and nb == 0) else []),
                                    inc="qt" if (kc == DC - 1 and nb == 1) else None)
                        mmev = P.ev("qt")
                        if kind == "h":
                            e1 = P.emit("vector", lambda e, pm=pm: e.tensor_copy(
                                hpre[:, 3:1027], pm[:]), waits=[mmev, P.ev("dg"), P.ev("qv")],
                                inc="qv")
                            pm_free[pmi] = e1
                            e2 = P.emit("vector", lambda e, m=m: e.tensor_copy(
                                hpre[:, 0:3], carry3[:, m * 3:m * 3 + 3]),
                                waits=[e1, cz_ev], inc="qv")
                            if t0 == PRE:
                                e2 = P.emit("vector", lambda e: e.tensor_scalar_mul(
                                    hpre[:, 0:3], hpre[:, 0:3], flag_sb[:, 0:1]),
                                    waits=[e2], inc="qv")
                            e3 = P.emit("vector", lambda e, m=m: e.tensor_copy(
                                carry3[:, m * 3:m * 3 + 3], hpre[:, 1024:1027]),
                                waits=[e1], inc="qv")
                            c0 = P.emit("vector", lambda e, m=m: e.tensor_scalar(
                                hcv[:], hpre[:, 0:1024], cw_sb[:, m * 4:m * 4 + 1],
                                small[:, 36 + m:37 + m], ALU.mult, ALU.add),
                                waits=[e2, e3], inc="qv")
                            for k in range(1, 4):
                                c0 = P.emit("vector", lambda e, k=k, m=m: e.scalar_tensor_tensor(
                                    hcv[:], hpre[:, k:k + 1024], cw_sb[:, m * 4 + k:m * 4 + k + 1],
                                    hcv[:], ALU.mult, ALU.add), waits=[c0], inc="qv")
                            P.dma("gpsimd", hconv_hbm[m * 128:(m + 1) * 128, t0:t0 + TCH],
                                  hcv[:], waits=[c0])
                            P.wait("vector", [P.ev("dg")])
                        else:
                            gev = P.emit("scalar", lambda e, pm=pm: e.activation(
                                gtmp[:], pm[:], AF.Identity), waits=[mmev, P.ev("dg")], inc="qs")
                            pm_free[pmi] = gev
                            P.dma("gpsimd",
                                  gate_hbm[m * 128:(m + 1) * 128, t0 - KV0:t0 - KV0 + TCH],
                                  gtmp[:], waits=[gev])
                            P.wait("scalar", [P.ev("dg")])
                P.barrier()

                if debug_phase == "A1":
                    tmp = SB(psa, "dbg1", [128, WIN], BF16)
                    for m in range(HC):
                        ev = P.dma("sync", tmp[:], hconv_hbm[m * 128:(m + 1) * 128, :])
                        P.dma("sync", taps["t_hconv"][:, m * WIN:(m + 1) * WIN], tmp[:],
                              waits=[ev])
                    _finish(P, nc, es)
                    return nc

            with contextlib.ExitStack() as psa:
                hcb = SB(psa, "hcb", [128, 2 * HC * 512], BF16)
                tf = SB(psa, "tf", [128, 512], F32)
                al = SB(psa, "al", [128, 512], F32)
                ti = SB(psa, "ti", [128, 512], F32)
                sg = SB(psa, "sg", [128, 512], F32)
                w1s = SB(psa, "w1s", [128, 512], F32)
                xs_ = SB(psa, "xs_", [128, 512], F32)
                hsc = SB(psa, "hsc", [128, 512], F32)
                hsb = SB(psa, "hsb", [128, 512], BF16)
                stcol = SB(psa, "stcol", [128, HC], F32)
                pf = PS(psa, "pf", [128, 512], F32)
                pi_ = PS(psa, "pi", [128, 512], F32)

                P.emit("vector", lambda e: e.memset(stcol[:], 0.0), waits=[P.ev("qt")], inc="qv")
                TB = 512
                hc_free = {0: None, 1: None}
                for tb in range(WIN // TB):
                    t0 = tb * TB
                    half = tb % 2
                    dst = hcb[:, half * HC * 512:(half + 1) * HC * 512]
                    hev = P.dma("sync", _r(dst, "p (c n) -> p c n", c=HC),
                                _r(hconv_hbm[:, t0:t0 + TB], "(c p) n -> p c n", p=128),
                                waits=[hc_free[half]])
                    for c in range(HC):
                        for pp, colbase in ((pf, 0), (pi_, HID)):
                            for kc in range(HC):
                                P.emit("tensor", lambda e, kc=kc, pp=pp, colbase=colbase, c=c,
                                       dst=dst: e.matmul(
                                    pp[:],
                                    wg_sb[:, kc * 2 * HID + colbase + c * 128:
                                          kc * 2 * HID + colbase + c * 128 + 128],
                                    dst[:, kc * 512:(kc + 1) * 512],
                                    start=(kc == 0), stop=(kc == HC - 1)),
                                    waits=([hev, P.ev("qs"), P.ev("qg")] if kc == 0 else []),
                                    inc="qt" if kc == HC - 1 else None)
                        mmev = P.ev("qt")
                        e1 = P.emit("scalar", lambda e, c=c: e.activation(
                            tf[:], pf[:], AF.Tanh, bias=small[:, 12 + c:13 + c], scale=0.5),
                            waits=[mmev, P.ev("qg"), P.ev("qv")], inc="qs")
                        e2 = P.emit("scalar", lambda e, c=c: e.activation(
                            al[:], tf[:], AF.Exp, bias=small[:, c:c + 1], scale=small[:, c:c + 1]),
                            waits=[e1], inc="qs")
                        e3 = P.emit("scalar", lambda e, c=c: e.activation(
                            ti[:], pi_[:], AF.Tanh, bias=small[:, 24 + c:25 + c], scale=0.5),
                            waits=[e2], inc="qs")
                        g1 = P.emit("gpsimd", lambda e: e.tensor_scalar(
                            sg[:], ti[:], 0.5, 0.5, ALU.mult, ALU.add), waits=[e3], inc="qg")
                        g2 = P.emit("gpsimd", lambda e: e.tensor_mul(w1s[:], al[:], al[:]),
                                    waits=[e2], inc="qg")
                        g3 = P.emit("gpsimd", lambda e: e.tensor_scalar(
                            w1s[:], w1s[:], -1.0, 1.0 + 1e-6, ALU.mult, ALU.add),
                            waits=[g2], inc="qg")
                        e4 = P.emit("scalar", lambda e: e.activation(w1s[:], w1s[:], AF.Ln),
                                    waits=[g3], inc="qs")
                        e5 = P.emit("scalar", lambda e: e.activation(w1s[:], w1s[:], AF.Exp,
                                                                     scale=0.5),
                                    waits=[e4], inc="qs")
                        g4 = P.emit("gpsimd", lambda e: e.tensor_mul(xs_[:], sg[:], w1s[:]),
                                    waits=[e5, g1], inc="qg")
                        g5 = P.emit("gpsimd", lambda e, c=c, dst=dst: e.tensor_mul(
                            xs_[:], xs_[:], dst[:, c * 512:(c + 1) * 512]),
                            waits=[g4], inc="qg")
                        init_ev = None
                        if tb == PRE // TB:
                            init_ev = P.emit("vector", lambda e, c=c: e.tensor_scalar_mul(
                                stcol[:, c:c + 1], stcol[:, c:c + 1], flag_sb[:, 0:1]),
                                waits=[P.ev("qv")], inc="qv")
                        sev = P.emit("vector", lambda e, c=c: e.tensor_tensor_scan(
                            hsc[:], al[:], xs_[:], stcol[:, c:c + 1], ALU.mult, ALU.add),
                            waits=[g5, init_ev, P.ev("qs")], inc="qv")
                        P.emit("vector", lambda e, c=c: e.tensor_copy(
                            stcol[:, c:c + 1], hsc[:, 511:512]), waits=[sev], inc="qv")
                        if t0 >= KV0:
                            bev = P.emit("vector", lambda e: e.tensor_copy(hsb[:], hsc[:]),
                                         waits=[sev, P.ev("dg")], inc="qv")
                            P.dma("gpsimd",
                                  hst_hbm[c * 128:(c + 1) * 128, t0 - KV0:t0 - KV0 + TB],
                                  hsb[:], waits=[bev])
                    hc_free[half] = P.ev("qt")
                P.barrier()

            if debug_phase == "A2":
                tmp = SB(pha, "dbg2", [128, ZLEN], BF16)
                for m in range(HC):
                    ev = P.dma("sync", tmp[:], hst_hbm[m * 128:(m + 1) * 128, :])
                    P.dma("sync", taps["t_hst"][:, m * ZLEN:(m + 1) * ZLEN], tmp[:], waits=[ev])
                    ev = P.dma("sync", tmp[:], gate_hbm[m * 128:(m + 1) * 128, :])
                    P.dma("sync", taps["t_gate"][:, m * ZLEN:(m + 1) * ZLEN], tmp[:], waits=[ev])
                _finish(P, nc, es)
                return nc

        z = SB(zstack, "z", [128, DC * OWN], F32)
        zh = SB(zstack, "zh", [128, DC * KV0], BF16)

        def dump_z():
            P.dma("sync", taps["t_z"], z[:], waits=P.all_ev())
            P.dma("sync", taps["t_zh"], zh[:], waits=P.all_ev())
            _finish(P, nc, es)

        with contextlib.ExitStack() as phb, contextlib.ExitStack() as psb:
            wo_sb = SB(phb, "wo_sb", [128, HC * D], BF16)
            gh_sb = SB(phb, "gh_sb", [128, HC * 1024], BF16)
            gl_sb = SB(phb, "gl_sb", [128, 1024], BF16)
            gld = SB(phb, "gld", [128, 2 * 1024], BF16)
            hld = SB(phb, "hld", [128, 2 * 1024], BF16)
            xb2 = SB(phb, "xb2", [128, 2 * D], F32)
            ztok = SB(phb, "ztok", [128, D], F32)
            psz = PS(psb, "psz", [128, 1024], F32)
            pw0 = PS(psb, "pw0", [128, 1024], F32)

            woev = load_wT("gpsimd", wo_sb, woutT, HID, D)
            TCH = 1024
            for ci in range(ZLEN // TCH):
                w0 = KV0 + ci * TCH
                for c in range(HC):
                    half = c % 2
                    gev = P.dma("sync", gld[:, half * 1024:(half + 1) * 1024],
                                gate_hbm[c * 128:(c + 1) * 128, ci * TCH:(ci + 1) * TCH],
                                waits=[P.ev("qs")])
                    hev = P.dma("sync", hld[:, half * 1024:(half + 1) * 1024],
                                hst_hbm[c * 128:(c + 1) * 128, ci * TCH:(ci + 1) * TCH],
                                waits=[P.ev("qv")])
                    sev = P.emit("scalar", lambda e, half=half: e.activation(
                        gl_sb[:], gld[:, half * 1024:(half + 1) * 1024], AF.Gelu),
                        waits=[gev, P.ev("qv")], inc="qs")
                    P.emit("vector", lambda e, c=c, half=half: e.tensor_mul(
                        gh_sb[:, c * 1024:(c + 1) * 1024], gl_sb[:],
                        hld[:, half * 1024:(half + 1) * 1024]),
                        waits=[sev, hev], inc="qv")
                gh_ready = P.ev("qv")
                for i in range(TCH // 128):
                    wp = w0 + i * 128
                    buf = xb2[:, (i % 2) * D:(i % 2) * D + D]
                    xev = P.dma("sync", buf, xwin[wp:wp + 128, :], waits=[P.ev("qs")])
                    agg = stats_on(buf, waits=[xev])
                    rse = mk_scale(1, 3, 5, True, agg)
                    nme = neg_m_rs(5, 7, rse)
                    aev = P.emit("scalar", lambda e, buf=buf: e.activation(
                        ztok[:], buf, AF.Identity, bias=stat[:, 7:8], scale=stat[:, 5:6]),
                        waits=[nme], inc="qs")
                    tev = transpose_tokmaj_to_fm(ztok, psz, True,
                                                 extra_waits=[aev, woev, P.ev("qv")])
                    sls, own = z_slices(wp)
                    tgt = (z if own else zh)
                    w = OWN if own else KV0
                    base = (wp - PRE) if own else (wp - KV0)
                    P.emit("vector", lambda e, tgt=tgt, w=w, base=base: e.tensor_copy(
                        _r(tgt[:], "p (c n) -> p c n", c=DC)[:, :, base:base + 128],
                        _r(psz[:], "p (c n) -> p c n", c=DC)),
                        waits=[tev], inc="qv")
                z0_ready = P.ev("qv")
                for dm in range(DC):
                    for nb in range(2):
                        for kc in range(HC):
                            P.emit("tensor", lambda e, dm=dm, nb=nb, kc=kc: e.matmul(
                                pw0[:, nb * 512:(nb + 1) * 512],
                                wo_sb[:, kc * D + dm * 128: kc * D + dm * 128 + 128],
                                gh_sb[:, kc * 1024 + nb * 512: kc * 1024 + (nb + 1) * 512],
                                start=(kc == 0), stop=(kc == HC - 1)),
                                waits=([gh_ready, z0_ready, P.ev("qv")]
                                       if (kc == 0 and nb == 0) else []),
                                inc="qt" if (kc == HC - 1 and nb == 1) else None)
                    mmev = P.ev("qt")
                    own = w0 >= PRE
                    zt = (z[:, dm * OWN + (w0 - PRE): dm * OWN + (w0 - PRE) + TCH] if own
                          else zh[:, dm * KV0 + (w0 - KV0): dm * KV0 + (w0 - KV0) + TCH])
                    P.emit("vector", lambda e, zt=zt: e.tensor_add(zt, zt, pw0[:]),
                           waits=[mmev], inc="qv")
            P.barrier()

        if debug_phase == "B":
            dump_z()
            return nc

        def gmlp(gT, sT, gcol, rng_start, rng_len):
            with contextlib.ExitStack() as phc, contextlib.ExitStack() as psc:
                u_c = SB(phc, "u_c", [128, DC * 1024], BF16)
                utk = SB(phc, "utk", [128, D], BF16)
                gel = SB(phc, "gel", [128, 1024], BF16)
                ghm = SB(phc, "ghm", [128, 16 * 1024], BF16)
                wmA = SB(phc, "wmA", [128, 2 * DC * 128], BF16)
                wmB = SB(phc, "wmB", [128, 2 * DC * 128], BF16)
                wsA = SB(phc, "wsA", [128, 2 * 16 * 128], BF16)
                ptk = PS(psc, "ptk", [128, 1024], F32)
                ptb = PS(psc, "ptb", [128, 1024], BF16)
                pg = PS(psc, "pg", [128, 1024], F32)
                ph_ = PS(psc, "ph_", [128, 1024], F32)

                for ci in range(rng_len // 1024):
                    w0 = rng_start + ci * 1024
                    for i in range(8):
                        wp = w0 + i * 128
                        sls, own = z_slices(wp)
                        pin = ptk if own else ptb
                        tev = transpose_fm_to_tokmaj(sls, pin, own,
                                                     extra_waits=[P.ev("qv"), P.ev("qs")])
                        agg = stats_on(pin[:], waits=[tev])
                        e1 = P.emit("vector", lambda e: e.scalar_tensor_tensor(
                            stat[:, 2:3], stat[:, 0:1], 1.0, stat[:, 0:1], ALU.mult, ALU.mult),
                            waits=[agg], inc="qv")
                        e2 = P.emit("vector", lambda e: e.tensor_add(
                            stat[:, 2:3], stat[:, 2:3], stat[:, 1:2]), waits=[e1], inc="qv")
                        e4 = mk_scale(2, 3, 4, None, e2)
                        aev = P.emit("scalar", lambda e, pin=pin: e.activation(
                            utk[:], pin[:], AF.Identity, scale=stat[:, 4:5]),
                            waits=[e4], inc="qs")
                        tev = transpose_tokmaj_to_fm(utk, ptb, False,
                                                     extra_waits=[aev, P.ev("qv")])
                        P.emit("vector", lambda e, i=i: e.tensor_copy(
                            _r(u_c[:], "p (c n) -> p c n", c=DC)[:, :, i * 128:(i + 1) * 128],
                            _r(ptb[:], "p (c n) -> p c n", c=DC)),
                            waits=[tev], inc="qv")
                    for c in range(DC):
                        P.emit("vector", lambda e, c=c: e.tensor_scalar_mul(
                            u_c[:, c * 1024:(c + 1) * 1024], u_c[:, c * 1024:(c + 1) * 1024],
                            gn_sb[:, gcol * DC + c:gcol * DC + c + 1]),
                            waits=[P.ev("qv")], inc="qv")
                    u_ready = P.ev("qv")

                    wflip = 0
                    wfree = {0: None, 1: None}
                    for m in range(16):
                        wA = wmA[:, wflip * DC * 128:(wflip + 1) * DC * 128]
                        wB = wmB[:, wflip * DC * 128:(wflip + 1) * DC * 128]
                        evA = P.dma("gpsimd", _r(wA, "p (c n) -> p c n", c=DC),
                                    _r(gT[:, m * 128:(m + 1) * 128], "(c p) n -> p c n", p=128),
                                    waits=[wfree[wflip]])
                        evB = P.dma("gpsimd", _r(wB, "p (c n) -> p c n", c=DC),
                                    _r(gT[:, GH + m * 128: GH + (m + 1) * 128],
                                       "(c p) n -> p c n", p=128))
                        for nb in range(2):
                            for kc in range(DC):
                                P.emit("tensor", lambda e, kc=kc, nb=nb, wA=wA: e.matmul(
                                    pg[:, nb * 512:(nb + 1) * 512],
                                    wA[:, kc * 128:(kc + 1) * 128],
                                    u_c[:, kc * 1024 + nb * 512: kc * 1024 + (nb + 1) * 512],
                                    start=(kc == 0), stop=(kc == DC - 1)),
                                    waits=([u_ready, evA, P.ev("qs")]
                                           if (kc == 0 and nb == 0) else []),
                                    inc="qt" if (kc == DC - 1 and nb == 1) else None)
                        for nb in range(2):
                            for kc in range(DC):
                                P.emit("tensor", lambda e, kc=kc, nb=nb, wB=wB: e.matmul(
                                    ph_[:, nb * 512:(nb + 1) * 512],
                                    wB[:, kc * 128:(kc + 1) * 128],
                                    u_c[:, kc * 1024 + nb * 512: kc * 1024 + (nb + 1) * 512],
                                    start=(kc == 0), stop=(kc == DC - 1)),
                                    waits=([evB, P.ev("qv")] if (kc == 0 and nb == 0) else []),
                                    inc="qt" if (kc == DC - 1 and nb == 1) else None)
                        mmev = P.ev("qt")
                        gev = P.emit("scalar", lambda e: e.activation(gel[:], pg[:], AF.Gelu),
                                     waits=[mmev], inc="qs")
                        mev = P.emit("vector", lambda e, m=m: e.tensor_mul(
                            ghm[:, m * 1024:(m + 1) * 1024], gel[:], ph_[:]),
                            waits=[gev], inc="qv")
                        wfree[wflip] = mev
                        wflip ^= 1
                    gh_ready = P.ev("qv")

                    sflip = 0
                    sfree = {0: None, 1: None}
                    for dm in range(DC):
                        wS = wsA[:, sflip * 16 * 128:(sflip + 1) * 16 * 128]
                        evS = P.dma("gpsimd", _r(wS, "p (c n) -> p c n", c=16),
                                    _r(sT[:, dm * 128:(dm + 1) * 128], "(c p) n -> p c n", p=128),
                                    waits=[sfree[sflip]])
                        for nb in range(2):
                            for kc in range(16):
                                P.emit("tensor", lambda e, kc=kc, nb=nb, wS=wS: e.matmul(
                                    pg[:, nb * 512:(nb + 1) * 512],
                                    wS[:, kc * 128:(kc + 1) * 128],
                                    ghm[:, kc * 1024 + nb * 512: kc * 1024 + (nb + 1) * 512],
                                    start=(kc == 0), stop=(kc == 15)),
                                    waits=([gh_ready, evS, P.ev("qv"), P.ev("qs")]
                                           if (kc == 0 and nb == 0) else []),
                                    inc="qt" if (kc == 15 and nb == 1) else None)
                        mmev = P.ev("qt")
                        own = w0 >= PRE
                        zt = (z[:, dm * OWN + (w0 - PRE): dm * OWN + (w0 - PRE) + 1024] if own
                              else zh[:, dm * KV0 + (w0 - KV0): dm * KV0 + (w0 - KV0) + 1024])
                        sfree[sflip] = P.emit("vector", lambda e, zt=zt: e.tensor_add(
                            zt, zt, pg[:]), waits=[mmev], inc="qv")
                        sflip ^= 1
                P.barrier()

        gmlp(g1gT, g1sT, 1, KV0, ZLEN)
        if debug_phase == "C":
            dump_z()
            return nc

        with contextlib.ExitStack() as phd:
            wq_sb = SB(phd, "wq_sb", [128, DC * D], BF16)
            wkv_sb = SB(phd, "wkv_sb", [128, DC * 256], BF16)
            kfull = SB(phd, "kfull", [128, 3200], BF16)
            vtok = SB(phd, "vtok", [128, 25 * 128], BF16)
            qrot = SB(phd, "qrot", [128, QH * 1024], BF16)
            u_c = SB(phd, "u_cD", [128, DC * 1024], BF16)
            utk = SB(phd, "utkD", [128, D], BF16)
            cosl = SB(phd, "cosl", [128, ZLEN], BF16)
            sinl = SB(phd, "sinl", [128, ZLEN], BF16)
            tsh = SB(phd, "tsh", [128, 1024], BF16)
            tmix = SB(phd, "tmix", [128, 1024], BF16)
            mc_sb = SB(phd, "mc_sb", [128, BAND], BF16)

            wqe = load_wT("gpsimd", wq_sb, wqT, D, D)
            wkve = load_wT("gpsimd", wkv_sb, wkvT, D, 2 * HD)
            P.dma("sync", cosl[:], costab)
            P.dma("sync", sinl[:], sintab)
            P.dma("sync", mc_sb[:], mconst)
            tabev = P.ev("ds")
            for c in range(DC):
                P.emit("vector", lambda e, c=c: e.tensor_scalar_mul(
                    wq_sb[:, c * D:(c + 1) * D], wq_sb[:, c * D:(c + 1) * D],
                    gn_sb[:, 2 * DC + c:2 * DC + c + 1]), waits=[wqe], inc="qv")
                P.emit("vector", lambda e, c=c: e.tensor_scalar_mul(
                    wkv_sb[:, c * 256:(c + 1) * 256], wkv_sb[:, c * 256:(c + 1) * 256],
                    gn_sb[:, 2 * DC + c:2 * DC + c + 1]), waits=[wkve], inc="qv")
            w_ready = P.ev("qv")
            P.emit("gpsimd", lambda e: e.memset(kfull[:, 3072:3200], 0.0), inc="qg")
            kpad_ev = P.ev("qg")

            shmask = []
            for i in range(0, 32, 2):
                shmask += [i + 1, i]

            def rms_u_chunk(w0):
              with contextlib.ExitStack() as psn:
                psc_ptk = PS(psn, "ptk_n", [128, 1024], F32)
                psc_ptb = PS(psn, "ptb_n", [128, 1024], BF16)
                for i in range(8):
                    wp = w0 + i * 128
                    sls, own = z_slices(wp)
                    pin = psc_ptk if own else psc_ptb
                    tev = transpose_fm_to_tokmaj(sls, pin, own,
                                                 extra_waits=[P.ev("qv"), P.ev("qs")])
                    agg = stats_on(pin[:], waits=[tev])
                    e1 = P.emit("vector", lambda e: e.scalar_tensor_tensor(
                        stat[:, 2:3], stat[:, 0:1], 1.0, stat[:, 0:1], ALU.mult, ALU.mult),
                        waits=[agg], inc="qv")
                    e2 = P.emit("vector", lambda e: e.tensor_add(
                        stat[:, 2:3], stat[:, 2:3], stat[:, 1:2]), waits=[e1], inc="qv")
                    e4 = mk_scale(2, 3, 4, None, e2)
                    aev = P.emit("scalar", lambda e, pin=pin: e.activation(
                        utk[:], pin[:], AF.Identity, scale=stat[:, 4:5]), waits=[e4], inc="qs")
                    tev = transpose_tokmaj_to_fm(utk, psc_ptb, False,
                                                 extra_waits=[aev, P.ev("qv")])
                    P.emit("vector", lambda e, i=i: e.tensor_copy(
                        _r(u_c[:], "p (c n) -> p c n", c=DC)[:, :, i * 128:(i + 1) * 128],
                        _r(psc_ptb[:], "p (c n) -> p c n", c=DC)),
                        waits=[tev], inc="qv")
                return P.ev("qv")

            with contextlib.ExitStack() as psd:
                vfull = SB(psd, "vfull", [128, 3200], BF16)
                pkv = PS(psd, "pkv", [128, 1024], F32)
                P.emit("gpsimd", lambda e: e.memset(vfull[:, 3072:3200], 0.0), inc="qg")
                pad_ev = P.ev("qg")
                for ci in range(3):
                    w0 = KV0 + ci * 1024
                    uev = rms_u_chunk(w0)
                    for hm, dst in ((0, kfull), (1, vfull)):
                        for nb in range(2):
                            for kc in range(DC):
                                P.emit("tensor", lambda e, kc=kc, nb=nb, hm=hm: e.matmul(
                                    pkv[:, nb * 512:(nb + 1) * 512],
                                    wkv_sb[:, kc * 256 + hm * 128: kc * 256 + hm * 128 + 128],
                                    u_c[:, kc * 1024 + nb * 512: kc * 1024 + (nb + 1) * 512],
                                    start=(kc == 0), stop=(kc == DC - 1)),
                                    waits=([uev, w_ready, P.ev("qv")]
                                           if (kc == 0 and nb == 0) else []),
                                    inc="qt" if (kc == DC - 1 and nb == 1) else None)
                        mmev = P.ev("qt")
                        cols = slice(ci * 1024, (ci + 1) * 1024)
                        if hm == 1:
                            P.emit("vector", lambda e, cols=cols: e.tensor_copy(
                                vfull[:, cols], pkv[:]), waits=[mmev], inc="qv")
                        else:
                            e1 = P.emit("vector", lambda e: e.tensor_copy(tmix[:], pkv[:]),
                                        waits=[mmev, tabev], inc="qv")
                            e2 = P.emit("vector", lambda e: e.stream_shuffle(
                                tsh[:], tmix[:], shmask), waits=[e1], inc="qv")
                            e3 = P.emit("vector", lambda e, cols=cols: e.tensor_mul(
                                kfull[:, cols], tmix[:], cosl[:, cols]), waits=[e2], inc="qv")
                            e4 = P.emit("vector", lambda e, cols=cols: e.tensor_mul(
                                tsh[:], tsh[:], sinl[:, cols]), waits=[e3], inc="qv")
                            P.emit("vector", lambda e, cols=cols: e.tensor_add(
                                kfull[:, cols], kfull[:, cols], tsh[:]), waits=[e4], inc="qv")
                kv_ready = P.ev("qv")
                with contextlib.ExitStack() as psv:
                    pvt = PS(psv, "pvt", [128, 128], BF16)
                    for i in range(25):
                        P.emit("tensor", lambda e, i=i: e.matmul(
                            pvt[:], vfull[:, i * 128:(i + 1) * 128], idb_sb[:],
                            is_transpose=True, start=True, stop=True),
                            waits=[kv_ready, pad_ev, P.ev("qv")], inc="qt")
                        P.emit("vector", lambda e, i=i: e.tensor_copy(
                            vtok[:, i * 128:(i + 1) * 128], pvt[:]),
                            waits=[P.ev("qt")], inc="qv")
                vtok_ready = P.ev("qv")
                if debug_phase == "Dkv":
                    P.dma("sync", taps["t_k"], kfull[:], waits=[kv_ready, vtok_ready])
                    P.dma("sync", taps["t_v"], vfull[:], waits=[kv_ready])
                    _finish(P, nc, es)
                    return nc

            segs = ((0, 512), (512, 512), (1024, 256))
            for qc in range(2):
                w0 = PRE + qc * 1024
                guard = P.ev("qt")
                uev = rms_u_chunk(w0)
                with contextlib.ExitStack() as psd, contextlib.ExitStack() as sbd:
                    psS = PS(psd, "psS", [128, 1536], F32)
                    psPT = PS(psd, "psPT", [128, 1536], BF16)
                    psO = PS(psd, "psO", [128, 128], F32)
                    pop8 = PS(psd, "pop8", [128, 1024], F32)
                    p_sb2 = SB(sbd, "p_sb2", [128, 2 * BAND], BF16)
                    pt_sb2 = SB(sbd, "pt_sb2", [128, 2 * BAND], BF16)
                    wodq = SB(sbd, "wodq", [128, 2 * QH * 128], BF16)
                    otile = SB(sbd, "otile", [128, QH * 128], BF16)
                    msa = SB(sbd, "msa", [1, BAND], BF16)
                    msb = SB(sbd, "msb", [1, BAND], BF16)

                    qrot_ev = {}
                    ev_qevac = None
                    for hq in range(QH):
                        for nb in range(2):
                            for kc in range(DC):
                                P.emit("tensor", lambda e, kc=kc, nb=nb, hq=hq: e.matmul(
                                    psS[:, nb * 512:(nb + 1) * 512],
                                    wq_sb[:, kc * D + hq * 128: kc * D + hq * 128 + 128],
                                    u_c[:, kc * 1024 + nb * 512: kc * 1024 + (nb + 1) * 512],
                                    start=(kc == 0), stop=(kc == DC - 1)),
                                    waits=([uev, w_ready, guard, ev_qevac]
                                           if (kc == 0 and nb == 0) else []),
                                    inc="qt" if kc == DC - 1 else None)
                            P.emit("vector", lambda e, nb=nb: e.tensor_copy(
                                tmix[:, nb * 512:(nb + 1) * 512],
                                psS[:, nb * 512:(nb + 1) * 512]),
                                waits=[P.ev("qt"), tabev], inc="qv")
                        ev_qevac = P.ev("qv")
                        qcols = slice(hq * 1024, (hq + 1) * 1024)
                        tcols = slice(1024 + qc * 1024, 1024 + (qc + 1) * 1024)
                        e2 = P.emit("vector", lambda e: e.stream_shuffle(
                            tsh[:], tmix[:], shmask), waits=[ev_qevac], inc="qv")
                        e3 = P.emit("vector", lambda e, qcols=qcols, tcols=tcols: e.tensor_mul(
                            qrot[:, qcols], tmix[:], cosl[:, tcols]),
                            waits=[e2, guard], inc="qv")
                        e4 = P.emit("vector", lambda e, tcols=tcols: e.tensor_mul(
                            tsh[:], tsh[:], sinl[:, tcols]), waits=[e3], inc="qv")
                        P.emit("vector", lambda e, qcols=qcols: e.tensor_add(
                            qrot[:, qcols], qrot[:, qcols], tsh[:]), waits=[e4], inc="qv")
                        qrot_ev[hq] = P.ev("qv")

                    use_bias = (qc == 0)
                    NT = 64
                    ev_sc = {}; ev_exp = {}; ev_scaled = {}; ev_pt = {}
                    ev_ptc = {}; ev_pv = {}; ev_ot = {}; ev_op = {}
                    ev_popev = {}; ev_mb = {}; ev_wfree = {}
                    for t in range(NT + 2):
                        if t < NT:
                            qi, hq = divmod(t, 8)
                            qt_ = qc * 8 + qi
                            par = t % 2
                            mst = msa if qi % 2 == 0 else msb
                            if use_bias and hq == 0:
                                ev_mb[qi] = P.dma(
                                    "sync", mst[:], mbias[qt_:qt_ + 1, :],
                                    waits=[ev_sc.get((qi - 2) * 8 + 7), vtok_ready, guard])
                            for si, (c0, cn) in enumerate(segs):
                                P.emit("tensor", lambda e, c0=c0, cn=cn, qt_=qt_, hq=hq,
                                       qi=qi: e.matmul(
                                    psS[:, c0:c0 + cn],
                                    qrot[:, hq * 1024 + qi * 128: hq * 1024 + qi * 128 + 128],
                                    kfull[:, qt_ * 128 + c0: qt_ * 128 + c0 + cn],
                                    start=True, stop=False),
                                    waits=([qrot_ev[hq], ev_exp.get(t - 1), kv_ready,
                                            vtok_ready, kpad_ev, ev_mb.get(qi), ev_qevac]
                                           if si == 0 else []))
                                P.emit("tensor", lambda e, c0=c0, cn=cn: e.matmul(
                                    psS[:, c0:c0 + cn], idb_sb[:], mc_sb[:, c0:c0 + cn],
                                    start=False, stop=(not use_bias)),
                                    inc=(None if use_bias else ("qt" if si == 2 else None)))
                                if use_bias:
                                    P.emit("tensor", lambda e, c0=c0, cn=cn, mst=mst: e.matmul(
                                        psS[:, c0:c0 + cn], ones_sb[:], mst[:, c0:c0 + cn],
                                        start=False, stop=True),
                                        inc="qt" if si == 2 else None)
                            ev_sc[t] = P.ev("qt")
                            emax = P.emit("vector", lambda e, par=par: e.reduce_max(
                                stat[:, 0 + par:1 + par], psS[:, 0:BAND], axis=AX.X,
                                negate=True), waits=[ev_sc[t]], inc="qv")
                            emsc = P.emit("vector", lambda e, par=par: e.tensor_scalar_mul(
                                stat[:, 2 + par:3 + par], stat[:, 0 + par:1 + par], SCL),
                                waits=[emax, ev_exp.get(t - 2)], inc="qv")
                            ev_exp[t] = P.emit("scalar", lambda e, par=par: e.activation(
                                p_sb2[:, par * BAND:(par + 1) * BAND], psS[:, 0:BAND],
                                AF.Exp, bias=stat[:, 2 + par:3 + par], scale=SCL,
                                accum_out=stat[:, 4 + par:5 + par]),
                                waits=[emsc, ev_pt.get(t - 2)], inc="qs")
                            erc = P.emit("vector", lambda e, par=par: e.reciprocal(
                                stat[:, 6 + par:7 + par], stat[:, 4 + par:5 + par]),
                                waits=[ev_exp[t]], inc="qv")
                            ev_scaled[t] = P.emit("vector", lambda e, par=par: e.tensor_scalar_mul(
                                p_sb2[:, par * BAND:(par + 1) * BAND],
                                p_sb2[:, par * BAND:(par + 1) * BAND],
                                stat[:, 6 + par:7 + par]), waits=[erc], inc="qv")
                        u = t - 1
                        if 0 <= u < NT:
                            paru = u % 2
                            for kt in range(10):
                                P.emit("tensor", lambda e, kt=kt, paru=paru: e.matmul(
                                    psPT[:, kt * 128:(kt + 1) * 128],
                                    p_sb2[:, paru * BAND + kt * 128:
                                          paru * BAND + (kt + 1) * 128],
                                    idb_sb[:], is_transpose=True,
                                    start=(kt in (0, 8)), stop=(kt == 9)),
                                    waits=([ev_scaled[u], ev_ptc.get(u - 1)]
                                           if kt == 0 else []),
                                    inc="qt" if kt == 9 else None)
                            ev_pt[u] = P.ev("qt")
                            ev_ptc[u] = P.emit("scalar", lambda e, paru=paru: e.activation(
                                pt_sb2[:, paru * BAND:(paru + 1) * BAND],
                                psPT[:, 0:BAND], AF.Identity),
                                waits=[ev_pt[u], ev_pv.get(u - 2)], inc="qs")
                        v = t - 2
                        if 0 <= v < NT:
                            qiv, hqv = divmod(v, 8)
                            qtv = qc * 8 + qiv
                            parv = v % 2
                            for kt in range(10):
                                P.emit("tensor", lambda e, kt=kt, qtv=qtv, parv=parv: e.matmul(
                                    psO[:], vtok[:, (qtv + kt) * 128:(qtv + kt + 1) * 128],
                                    pt_sb2[:, parv * BAND + kt * 128:
                                           parv * BAND + (kt + 1) * 128],
                                    start=(kt == 0), stop=(kt == 9)),
                                    waits=([ev_ptc[v], ev_ot.get(v - 1)] if kt == 0 else []),
                                    inc="qt" if kt == 9 else None)
                            ev_pv[v] = P.ev("qt")
                            ev_ot[v] = P.emit("vector", lambda e, hqv=hqv: e.tensor_copy(
                                otile[:, hqv * 128:(hqv + 1) * 128], psO[:]),
                                waits=[ev_pv[v], ev_op.get(qiv - 1)], inc="qv")
                            if hqv == 7:
                                for dm in range(DC):
                                    wpar = dm % 2
                                    wsl = wodq[:, wpar * QH * 128:(wpar + 1) * QH * 128]
                                    evw = P.dma("gpsimd", _r(wsl, "p (c n) -> p c n", c=QH),
                                                _r(woT[:, dm * 128:(dm + 1) * 128],
                                                   "(c p) n -> p c n", p=128),
                                                waits=[ev_wfree.get((qiv, dm - 2)),
                                                       ev_wfree.get((qiv - 1, DC - 2 + wpar)),
                                                       guard, vtok_ready])
                                    for kc in range(QH):
                                        P.emit("tensor", lambda e, kc=kc, dm=dm, wsl=wsl: e.matmul(
                                            pop8[:, dm * 128:(dm + 1) * 128],
                                            wsl[:, kc * 128:(kc + 1) * 128],
                                            otile[:, kc * 128:(kc + 1) * 128],
                                            start=(kc == 0), stop=(kc == QH - 1)),
                                            waits=([ev_ot[v], evw, ev_popev.get(qiv - 1)]
                                                   if kc == 0 else []),
                                            inc="qt" if kc == QH - 1 else None)
                                    ev_wfree[(qiv, dm)] = P.ev("qt")
                                ev_op[qiv] = P.ev("qt")
                                for dm in range(DC):
                                    P.emit("vector", lambda e, dm=dm, qtv=qtv: e.tensor_add(
                                        z[:, dm * OWN + qtv * 128: dm * OWN + qtv * 128 + 128],
                                        z[:, dm * OWN + qtv * 128: dm * OWN + qtv * 128 + 128],
                                        pop8[:, dm * 128:(dm + 1) * 128]),
                                        waits=([ev_op[qiv]] if dm == 0 else []),
                                        inc="qv" if dm == DC - 1 else None)
                                ev_popev[qiv] = P.ev("qv")
            P.barrier()

        if debug_phase == "D":
            dump_z()
            return nc

        gmlp(g2gT, g2sT, 3, PRE, OWN)
        if debug_phase == "E":
            dump_z()
            return nc

        with contextlib.ExitStack() as phf, contextlib.ExitStack() as psf:
            xb3 = SB(phf, "xb3", [128, 2 * D], F32)
            xmt = SB(phf, "xmt", [128, D], F32)
            uft = SB(phf, "uft", [128, D], BF16)
            fwg_sb = SB(phf, "fwg_sb", [128, DC * NE], F32)
            bgr_sb = SB(phf, "bgr_sb", [128, NE], F32)
            io8_sb = SB(phf, "io8_sb", [128, NE], F32)
            lgt = SB(phf, "lgt", [128, NE], F32)
            oh_t = SB(phf, "oh_t", [8, OWN], F32)
            onest = SB(phf, "onest", [8, OWN], F32)
            post = SB(phf, "post", [8, OWN], F32)
            mx8 = SB(phf, "mx8", [128, 8], F32)
            ix8 = SB(phf, "ix8", [128, 8], U32)
            ohm = SB(phf, "ohm", [128, 8], F32)
            posm = SB(phf, "posm", [128, 8], F32)
            selc = SB(phf, "selc", [128, 8], F32)
            csr = SB(phf, "csr", [1, 8], F32)
            mrow = SB(phf, "mrow", [1, 128], F32)
            onesf = SB(phf, "onesf", [128, 1], F32)
            ptk = PS(psf, "ptkF", [128, 1024], F32)
            pl = PS(psf, "plF", [128, 512], F32)
            pcs = PS(psf, "pcsF", [128, 256], F32)

            P.dma("sync", _r(fwg_sb[:], "p (c n) -> p c n", c=DC),
                  _r(fwg, "(c p) n -> p c n", p=128))
            P.dma("sync", bgr_sb[:], bgrep)
            P.dma("sync", io8_sb[:], iota8)
            fev = P.ev("ds")
            P.emit("gpsimd", lambda e: e.memset(onest[:], 1.0), inc="qg")
            P.emit("gpsimd", lambda e: e.memset(onesf[:], 1.0), inc="qg")
            gz = P.ev("qg")
            for kc in range(DC):
                P.emit("tensor", lambda e, kc=kc: e.matmul(
                    pcs[0:8, 0:1], fwg_sb[:, kc * NE:(kc + 1) * NE], onesf[:],
                    start=(kc == 0), stop=(kc == DC - 1)),
                    waits=([fev, gz] if kc == 0 else []),
                    inc="qt" if kc == DC - 1 else None)
            P.emit("vector", lambda e: e.tensor_copy(stat[0:8, 6:7], pcs[0:8, 0:1]),
                   waits=[P.ev("qt")], inc="qv")
            P.emit("tensor", lambda e: e.matmul(
                pcs[0:1, 8:16], stat[0:8, 6:7], idf_sb[0:8, 0:8],
                is_transpose=True, start=True, stop=True),
                waits=[P.ev("qv")], inc="qt")
            P.emit("vector", lambda e: e.tensor_copy(csr[:], pcs[0:1, 8:16]),
                   waits=[P.ev("qt")], inc="qv")
            P.emit("vector", lambda e: e.tensor_scalar_mul(csr[:], csr[:], -1.0),
                   waits=[P.ev("qv")], inc="qv")
            cs_ready = P.ev("qv")

            for i in range(16):
                buf = xb3[:, (i % 2) * D:(i % 2) * D + D]
                xev = P.dma("sync", buf, xwin[PRE + i * 128: PRE + (i + 1) * 128, :],
                            waits=[P.ev("qt"), P.ev("qv")])
                tev = transpose_fm_to_tokmaj(
                    [z[:, c * OWN + i * 128: c * OWN + i * 128 + 128] for c in range(DC)],
                    ptk, True, extra_waits=[P.ev("qv"), P.ev("qs")])
                xmev = P.emit("vector", lambda e, buf=buf: e.tensor_add(xmt[:], buf, ptk[:]),
                              waits=[tev, xev], inc="qv")
                P.dma("sync", xmid_hbm[i * 128:(i + 1) * 128, :], xmt[:], waits=[xmev])
                agg = stats_on(xmt[:], waits=[xmev])
                rse = mk_scale(1, 3, 5, True, agg)
                nme = neg_m_rs(5, 7, rse)
                tev = transpose_tokmaj_to_fm(buf, ptk, True, extra_waits=[xmev, nme])
                zfm = P.emit("vector", lambda e, i=i: e.tensor_add(
                    _r(z[:], "p (c n) -> p c n", c=DC)[:, :, i * 128:(i + 1) * 128],
                    _r(z[:], "p (c n) -> p c n", c=DC)[:, :, i * 128:(i + 1) * 128],
                    _r(ptk[:], "p (c n) -> p c n", c=DC)),
                    waits=[tev], inc="qv")
                P.emit("tensor", lambda e: e.matmul(
                    pcs[0:1, 128:256], stat[:, 0:1], idf_sb[:],
                    is_transpose=True, start=True, stop=True),
                    waits=[nme, zfm], inc="qt")
                mre = P.emit("vector", lambda e: e.tensor_copy(mrow[:], pcs[0:1, 128:256]),
                             waits=[P.ev("qt")], inc="qv")
                for kc in range(DC):
                    P.emit("tensor", lambda e, kc=kc, i=i: e.matmul(
                        pl[:, 0:NE], z[:, kc * OWN + i * 128: kc * OWN + i * 128 + 128],
                        fwg_sb[:, kc * NE:(kc + 1) * NE],
                        start=(kc == 0), stop=False),
                        waits=([zfm, cs_ready, mre] if kc == 0 else []))
                P.emit("tensor", lambda e: e.matmul(
                    pl[:, 0:NE], mrow[:], csr[:], start=False, stop=True), inc="qt")
                lgev = P.ev("qt")
                e1 = P.emit("vector", lambda e: e.tensor_scalar_mul(
                    lgt[:], pl[:, 0:NE], stat[:, 5:6]), waits=[lgev], inc="qv")
                e2 = P.emit("vector", lambda e: e.tensor_add(lgt[:], lgt[:], bgr_sb[:]),
                            waits=[e1, fev], inc="qv")
                if debug_phase == "F":
                    P.dma("sync", taps["t_logits"][:, i * NE:(i + 1) * NE], lgt[:], waits=[e2])
                e3 = P.emit("vector", lambda e: e.max(mx8[:], lgt[:]), waits=[e2], inc="qv")
                e4 = P.emit("vector", lambda e: e.max_index(ix8[:], mx8[:], lgt[:]),
                            waits=[e3], inc="qv")
                e5 = P.emit("vector", lambda e, i=i: e.tensor_copy(idxf[:, i:i + 1], ix8[:, 0:1]),
                            waits=[e4], inc="qv")
                e6 = P.emit("vector", lambda e, i=i: e.tensor_single_scalar(
                    ohm[:], io8_sb[:], idxf[:, i:i + 1], ALU.is_equal),
                    waits=[e5], inc="qv")
                P.emit("tensor", lambda e: e.matmul(
                    pcs[0:8, 0:128], ohm[:], idf_sb[:], is_transpose=True,
                    start=True, stop=True), waits=[e6], inc="qt")
                P.emit("vector", lambda e, i=i: e.tensor_copy(
                    oh_t[:, i * 128:(i + 1) * 128], pcs[0:8, 0:128]),
                    waits=[P.ev("qt")], inc="qv")
            P.barrier()

            scev = P.emit("vector", lambda e: e.tensor_tensor_scan(
                post[:], onest[:], oh_t[:], 0.0, ALU.mult, ALU.add),
                waits=[P.ev("qv"), gz], inc="qv")
            for i in range(16):
                P.emit("tensor", lambda e, i=i: e.matmul(
                    pcs[0:128, 0:8], post[:, i * 128:(i + 1) * 128], idf_sb[0:8, 0:8],
                    is_transpose=True, start=True, stop=True),
                    waits=[scev, P.ev("qv")], inc="qt")
                e0 = P.emit("vector", lambda e: e.tensor_copy(posm[:], pcs[0:128, 0:8]),
                            waits=[P.ev("qt")], inc="qv")
                e1 = P.emit("vector", lambda e, i=i: e.tensor_single_scalar(
                    ohm[:], io8_sb[:], idxf[:, i:i + 1], ALU.is_equal), waits=[e0], inc="qv")
                e2 = P.emit("vector", lambda e: e.tensor_mul(posm[:], posm[:], ohm[:]),
                            waits=[e1], inc="qv")
                e3 = P.emit("vector", lambda e: e.reduce_sum(selc[:, 0:1], posm[:], axis=AX.X),
                            waits=[e2], inc="qv")
                e4 = P.emit("vector", lambda e, i=i: e.scalar_tensor_tensor(
                    selc[:, 1:2], idxf[:, i:i + 1], float(CAP), selc[:, 0:1],
                    ALU.mult, ALU.add), waits=[e3], inc="qv")
                e5 = P.emit("vector", lambda e: e.tensor_scalar_add(
                    selc[:, 1:2], selc[:, 1:2], -1.0), waits=[e4], inc="qv")
                e6 = P.emit("vector", lambda e: e.tensor_single_scalar(
                    selc[:, 2:3], selc[:, 0:1], float(CAP) + 0.5, ALU.is_gt),
                    waits=[e5], inc="qv")
                e7 = P.emit("vector", lambda e: e.scalar_tensor_tensor(
                    selc[:, 3:4], selc[:, 1:2], -1.0, selc[:, 2:3], ALU.mult, ALU.mult),
                    waits=[e6], inc="qv")
                e8 = P.emit("vector", lambda e: e.scalar_tensor_tensor(
                    selc[:, 4:5], selc[:, 2:3], float(TRASH), selc[:, 3:4],
                    ALU.mult, ALU.add), waits=[e7], inc="qv")
                e9 = P.emit("vector", lambda e: e.tensor_add(
                    selc[:, 1:2], selc[:, 1:2], selc[:, 4:5]), waits=[e8], inc="qv")
                e10 = P.emit("vector", lambda e, i=i: e.tensor_copy(
                    slot_i[:, i:i + 1], selc[:, 1:2]), waits=[e9], inc="qv")
                if debug_phase == "F":
                    P.dma("sync", taps["t_slot"][:, i:i + 1], slot_i[:, i:i + 1], waits=[e10],
                          allow_slow_non_contiguous=True)
            slots_ready = P.ev("qv")

            for i in range(16):
                buf = xb3[:, (i % 2) * D:(i % 2) * D + D]
                xev = P.dma("sync", buf, xmid_hbm[i * 128:(i + 1) * 128, :],
                            waits=[slots_ready, P.ev("qs")])
                agg = stats_on(buf, waits=[xev])
                rse = mk_scale(1, 3, 5, True, agg)
                nme = neg_m_rs(5, 7, rse)
                aev = P.emit("scalar", lambda e, buf=buf: e.activation(
                    uft[:], buf, AF.Identity, bias=stat[:, 7:8], scale=stat[:, 5:6]),
                    waits=[nme], inc="qs")
                P.wait("gpsimd", [aev, slots_ready, P.ev("dg")])
                P.emit("gpsimd", lambda e, i=i: e.indirect_dma_start(
                    out=xg_hbm, out_offset=IndirectOffsetOnAxis(slot_i[:, i:i + 1], 0),
                    in_=uft[:], in_offset=None), inc="dg", n=16)
            P.barrier()

            if debug_phase == "F":
                tmp = SB(phf, "dbgf", [128, D], F32)
                for i in range(16):
                    ev = P.dma("sync", tmp[:], xmid_hbm[i * 128:(i + 1) * 128, :])
                    P.dma("sync", taps["t_xmid"][:, i * D:(i + 1) * D], tmp[:], waits=[ev])
                _finish(P, nc, es)
                return nc
        zstack.close()

        with contextlib.ExitStack() as phx, contextlib.ExitStack() as psx:
            xg_t = SB(phx, "xg_t", [128, 2 * D], BF16)
            xg_f = SB(phx, "xg_f", [128, DC * CAP], BF16)
            h_sb = SB(phx, "h_sb", [128, 32 * CAP], BF16)
            w1s_ = SB(phx, "w1s_", [128, 2 * DC * 512], BF16)
            w2c = SB(phx, "w2c", [128, 2 * 32 * 128], BF16)
            y_f = SB(phx, "y_f", [128, DC * CAP], BF16)
            y_t = SB(phx, "y_t", [128, D], BF16)
            ph1 = PS(psx, "ph1", [128, CAP], F32)
            ph2 = PS(psx, "ph2", [128, CAP], F32)
            py = PS(psx, "py", [128, CAP], F32)
            ptf = PS(psx, "ptfF", [128, 1024], BF16)

            for ex in range(NE):
                for i in range(CAP // 128):
                    gev = P.dma("sync", xg_t[:, (i % 2) * D:(i % 2) * D + D],
                                xg_hbm[ex * CAP + i * 128: ex * CAP + (i + 1) * 128, :],
                                waits=[P.ev("qt"), P.ev("qv")])
                    tev = transpose_tokmaj_to_fm(
                        xg_t[:, (i % 2) * D:(i % 2) * D + D], ptf, False, extra_waits=[gev])
                    P.emit("vector", lambda e, i=i: e.tensor_copy(
                        _r(xg_f[:], "p (c n) -> p c n", c=DC)[:, :, i * 128:(i + 1) * 128],
                        _r(ptf[:], "p (c n) -> p c n", c=DC)),
                        waits=[tev], inc="qv")
                xg_ready = P.ev("qv")

                wflip = 0
                wfree = {0: None, 1: None}
                for hb in range(FH // 512):
                    wsl = w1s_[:, wflip * DC * 512:(wflip + 1) * DC * 512]
                    wev = P.dma("gpsimd", _r(wsl, "p (c n) -> p c n", c=DC),
                                _r(fw1[ex, :, hb * 512:(hb + 1) * 512], "(c p) n -> p c n", p=128),
                                waits=[wfree[wflip]])
                    last_mm = None
                    for sub in range(4):
                        hp = ph1 if sub % 2 == 0 else ph2
                        m0 = hb * 4 + sub
                        for nb in range(CAP // 512):
                            for kc in range(DC):
                                P.emit("tensor", lambda e, kc=kc, nb=nb, sub=sub, wsl=wsl,
                                       hp=hp: e.matmul(
                                    hp[:, nb * 512:(nb + 1) * 512],
                                    wsl[:, kc * 512 + sub * 128: kc * 512 + sub * 128 + 128],
                                    xg_f[:, kc * CAP + nb * 512: kc * CAP + (nb + 1) * 512],
                                    start=(kc == 0), stop=(kc == DC - 1)),
                                    waits=([xg_ready, wev, P.ev("qs")]
                                           if (kc == 0 and nb == 0) else []),
                                    inc="qt" if (kc == DC - 1 and nb == CAP // 512 - 1) else None)
                        mmev = P.ev("qt")
                        last_mm = mmev
                        P.emit("scalar", lambda e, m0=m0, hp=hp: e.activation(
                            h_sb[:, m0 * CAP:(m0 + 1) * CAP], hp[:], AF.Gelu_apprx_tanh),
                            waits=[mmev], inc="qs")
                    wfree[wflip] = last_mm
                    wflip ^= 1
                h_ready = P.ev("qs")

                w2flip = 0
                w2free = {0: None, 1: None}
                for dm in range(DC):
                    wsl = w2c[:, w2flip * 32 * 128:(w2flip + 1) * 32 * 128]
                    wev = P.dma("gpsimd", _r(wsl, "p (c n) -> p c n", c=32),
                                _r(fw2[ex, :, dm * 128:(dm + 1) * 128], "(c p) n -> p c n", p=128),
                                waits=[w2free[w2flip]])
                    for nb in range(CAP // 512):
                        for kc in range(32):
                            P.emit("tensor", lambda e, kc=kc, nb=nb, wsl=wsl: e.matmul(
                                py[:, nb * 512:(nb + 1) * 512],
                                wsl[:, kc * 128:(kc + 1) * 128],
                                h_sb[:, kc * CAP + nb * 512: kc * CAP + (nb + 1) * 512],
                                start=(kc == 0), stop=(kc == 31)),
                                waits=([h_ready, wev, P.ev("qs")]
                                       if (kc == 0 and nb == 0) else []),
                                inc="qt" if (kc == 31 and nb == CAP // 512 - 1) else None)
                    mmev = P.ev("qt")
                    w2free[w2flip] = mmev
                    w2flip ^= 1
                    P.emit("scalar", lambda e, dm=dm: e.activation(
                        y_f[:, dm * CAP:(dm + 1) * CAP], py[:], AF.Identity),
                        waits=[mmev], inc="qs")
                yf_ready = P.ev("qs")
                for i in range(CAP // 128):
                    tev = transpose_fm_to_tokmaj(
                        [y_f[:, c * CAP + i * 128: c * CAP + i * 128 + 128] for c in range(DC)],
                        ptf, False, extra_waits=[yf_ready, P.ev("ds")])
                    yev = P.emit("vector", lambda e: e.tensor_copy(y_t[:], ptf[:]),
                                 waits=[tev, P.ev("ds")], inc="qv")
                    P.dma("sync", yg_hbm[ex * CAP + i * 128: ex * CAP + (i + 1) * 128, :],
                          y_t[:], waits=[yev])
            P.barrier()

        with contextlib.ExitStack() as pho:
            xmo = SB(pho, "xmo", [128, D], F32)
            ygo = SB(pho, "ygo", [128, D], BF16)
            oto = SB(pho, "oto", [128, D], F32)
            for i in range(16):
                xev = P.dma("sync", xmo[:], xmid_hbm[i * 128:(i + 1) * 128, :],
                            waits=[P.ev("ds"), P.ev("qv")])
                gev = P.emit("gpsimd", lambda e, i=i: e.indirect_dma_start(
                    out=ygo[:], out_offset=None, in_=yg_hbm,
                    in_offset=IndirectOffsetOnAxis(slot_i[:, i:i + 1], 0)),
                    waits=[P.ev("ds"), P.ev("qv")], inc="dg", n=16)
                aev = P.emit("vector", lambda e: e.tensor_add(oto[:], xmo[:], ygo[:]),
                             waits=[xev, gev], inc="qv")
                P.dma("sync", yout[i * 128:(i + 1) * 128, :], oto[:], waits=[aev])
        _finish(P, nc, es)
    return nc


def _finish(P, nc, es):
    P.barrier()
    blk = es.enter_context(nc.Block())
    for eng in ENGINES:
        fns = P.streams[eng]
        getattr(blk, eng)(lambda e, fns=fns: [f(e) for f in fns])



_NC_CACHE = {}


def _get_nc():
    key = (DEBUG_PHASE, STRICT)
    if key not in _NC_CACHE:
        _NC_CACHE[key] = build_program(DEBUG_PHASE, STRICT)
    return _NC_CACHE[key]


def _host_tables(h):
    own0 = h * 2048
    g = own0 - 1024 + np.arange(ZLEN, dtype=np.float64)
    inv = (1.0 / 10000.0) ** (np.arange(0, HD, 2, dtype=np.float64) / HD)
    invrep = np.repeat(inv, 2)
    ang = invrep[:, None] * g[None, :]
    cost = np.cos(ang).astype(np.float32)
    sint = np.sin(ang).astype(np.float32)
    sint[0::2, :] *= -1.0
    mb = np.zeros((16, BAND), np.float32)
    for qt in range(16):
        gk = own0 - 1024 + qt * 128 + np.arange(BAND)
        mb[qt, gk < 0] = NEG
    return (cost.astype(BFD), sint.astype(BFD), np.ascontiguousarray(mb).astype(BFD))


def kernel(**inputs):
    x = np.asarray(inputs["L0_attn_x"], np.float32)
    T = lambda a: np.ascontiguousarray(np.asarray(a, np.float32).T)
    col = lambda a: np.asarray(a, np.float32).reshape(-1, 1)

    shared = {
        "winT": T(inputs["L0_attn_hawk_Win"]),
        "wgT": T(inputs["L0_attn_hawk_Wg"]),
        "woutT": T(inputs["L0_attn_hawk_Wout"]),
        "cwi": np.ascontiguousarray(np.asarray(inputs["L0_attn_hawk_cw"], np.float32)
                                    .reshape(HID, 4)),
        "cbi": col(inputs["L0_attn_hawk_cb"]),
        "hbg": col(inputs["L0_attn_hawk_bg"]),
        "hfb": col(inputs["L0_attn_hawk_fb"]),
        "g1gT": T(inputs["L0_attn_g1_Wgrow"]),
        "g1sT": T(inputs["L0_attn_g1_Wshrink"]),
        "g2gT": T(inputs["L0_attn_g2_Wgrow"]),
        "g2sT": T(inputs["L0_attn_g2_Wshrink"]),
        "wqT": T(inputs["L0_attn_smqa_Wq"]),
        "wkvT": T(inputs["L0_attn_smqa_Wkv"]),
        "woT": T(inputs["L0_attn_smqa_Wout"]),
        "fw1": np.ascontiguousarray(np.asarray(inputs["L0_ffn_W1"], np.float32)),
        "fw2": np.ascontiguousarray(np.asarray(inputs["L0_ffn_W2"], np.float32)),
        "fwg": np.ascontiguousarray(np.asarray(inputs["L0_ffn_Wg"], np.float32)),
        "gnh": col(inputs["L0_attn_gn_hawk"]),
        "gn1": col(inputs["L0_attn_gn_hgmlp"]),
        "gns": col(inputs["L0_attn_gn_smqa"]),
        "gn2": col(inputs["L0_attn_gn_sgmlp"]),
        "bgrep": np.ascontiguousarray(np.broadcast_to(
            np.asarray(inputs["L0_ffn_bg"], np.float32), (128, NE))),
        "identf": np.eye(128, dtype=np.float32),
        "identb": np.eye(128, dtype=np.float32).astype(BFD),
        "onesb": np.ones((1, 128), np.float32).astype(BFD),
        "iota8": np.ascontiguousarray(np.broadcast_to(
            np.arange(NE, dtype=np.float32), (128, NE))),
    }
    mconst = np.zeros((128, BAND), np.float32)
    r = np.arange(128)[:, None]
    j = np.arange(BAND)[None, :]
    mconst[~((j >= r) & (j <= r + WSZ))] = NEG
    shared["mconst"] = mconst.astype(BFD)

    tabs = {h: _host_tables(h) for h in (0, 1)}
    in_maps = []
    for core in range(8):
        b, h = core // 2, core % 2
        if h == 1:
            xw = x[b]
        else:
            xw = np.concatenate([x[b, 0:2048], x[b, 0:2048]], 0)
        cost, sint, mb = tabs[h]
        m = dict(shared)
        m["xwin"] = np.ascontiguousarray(xw)
        m["flagc"] = np.full((128, 1), float(h), np.float32)
        m["epsc"] = np.full((128, 1), EPS, np.float32)
        m["costab"] = cost
        m["sintab"] = sint
        m["mbias"] = mb
        in_maps.append(m)

    nc = _get_nc()
    res = run_bass_kernel_spmd(nc, in_maps, core_ids=list(range(8)))
    out = np.empty((4, 4096, D), np.float32)
    for core in range(8):
        b, h = core // 2, core % 2
        out[b, h * 2048:(h + 1) * 2048] = np.asarray(res.results[core]["y"], np.float32)
    return out


# version 04 (final)
# speedup vs baseline: 2.0631x; 1.1224x over previous
import contextlib
import numpy as np
import ml_dtypes
import concourse.bass as bass
import concourse.mybir as mybir
from concourse.bass_utils import run_bass_kernel_spmd
from concourse.bass_core_rust import IndirectOffsetOnAxis

F32 = mybir.dt.float32
BF16 = mybir.dt.bfloat16
I32 = mybir.dt.int32
U32 = mybir.dt.uint32
AF = mybir.ActivationFunctionType
ALU = mybir.AluOpType
AX = mybir.AxisListType
BFD = ml_dtypes.bfloat16

D = 1024
DC = 8
HID = 1536
HC = 12
GH = 2048
HD = 128
QH = 8
NE = 8
FH = 4096
WIN = 4096
OWN = 2048
PRE = 2048
KV0 = 1024
ZLEN = 3072
BAND = 1280
WSZ = 1024
CAP = 1024
TRASH = NE * CAP
NEG = -1.0e30
EPS = 1e-5
SCL = float(1.0 / np.sqrt(128.0))

STRICT = False
DEBUG_PHASE = None

ENGINES = ("sync", "gpsimd", "vector", "scalar", "tensor")
SEMS = ("ds", "dg", "qv", "qs", "qt", "qg")


class Plan:
    def __init__(self, nc, strict):
        self.nc = nc
        self.strict = strict
        self.streams = {e: [] for e in ENGINES}
        self.sem = {}
        self.cnt = {s: 0 for s in SEMS}
        self.seen = {e: {s: 0 for s in SEMS} for e in ENGINES}

    def ev(self, name):
        return (name, self.cnt[name])

    def all_ev(self):
        return [self.ev(s) for s in SEMS]

    def emit(self, eng, f, waits=(), inc=None, n=1):
        if self.strict:
            waits = self.all_ev()
        pruned = []
        seen = self.seen[eng]
        for w in waits:
            if w is None:
                continue
            s, v = w
            if v <= 0 or seen[s] >= v:
                continue
            seen[s] = v
            pruned.append((s, v))
        if inc is not None:
            self.cnt[inc] += n
        sems = self.sem

        def runner(e, f=f, pruned=tuple(pruned), inc=inc, n=n):
            for s, v in pruned:
                e.wait_ge(sems[s], v)
            ins = f(e)
            if inc is not None:
                ins.then_inc(sems[inc], n)
        self.streams[eng].append(runner)
        return self.ev(inc) if inc is not None else None

    def wait(self, eng, waits):
        pruned = []
        seen = self.seen[eng]
        for w in waits:
            if w is None:
                continue
            s, v = w
            if v <= 0 or seen[s] >= v:
                continue
            seen[s] = v
            pruned.append((s, v))
        if not pruned:
            return
        sems = self.sem

        def runner(e, pruned=tuple(pruned)):
            for s, v in pruned:
                e.wait_ge(sems[s], v)
        self.streams[eng].append(runner)

    def dma(self, eng, out, in_, waits=(), **kw):
        sem = "ds" if eng in ("sync", "scalar") else "dg"
        return self.emit(eng, lambda e, o=out, i=in_: e.dma_start(out=o, in_=i, **kw),
                         waits=waits, inc=sem, n=16)

    def barrier(self):
        evs = self.all_ev()
        for eng in ENGINES:
            self.wait(eng, evs)


def _r(ap, pat, **kw):
    return ap.rearrange(pat, **kw)


def build_program(debug_phase=None, strict=STRICT):
    nc = bass.Bass("TRN2", target_bir_lowering=False, debug=False)

    def din(name, shape, dt=F32):
        return nc.dram_tensor(name, list(shape), dt, kind="ExternalInput").ap()

    xwin = din("xwin", [WIN, D])
    flagc = din("flagc", [128, 1])
    costab = din("costab", [128, ZLEN], BF16)
    sintab = din("sintab", [128, ZLEN], BF16)
    m01t = din("m01t", [16 * 128, BAND], BF16)
    identf = din("identf", [128, 128])
    identb = din("identb", [128, 128], BF16)
    onesb = din("onesb", [1, 128], BF16)
    iota8 = din("iota8", [128, NE])
    bgrep = din("bgrep", [128, NE])
    winT = din("winT", [D, 2 * HID])
    wgT = din("wgT", [HID, 2 * HID])
    woutT = din("woutT", [HID, D])
    cwi = din("cwi", [HID, 4])
    cbi = din("cbi", [HID, 1])
    hbg = din("hbg", [2 * HID, 1])
    hfb = din("hfb", [HID, 1])
    g1gT = din("g1gT", [D, 2 * GH])
    g1sT = din("g1sT", [GH, D])
    g2gT = din("g2gT", [D, 2 * GH])
    g2sT = din("g2sT", [GH, D])
    wqT = din("wqT", [D, D])
    wkvT = din("wkvT", [D, 2 * HD])
    woT = din("woT", [D, D])
    fw1 = din("fw1", [NE, D, FH])
    fw2 = din("fw2", [NE, FH, D])
    fwg = din("fwg", [D, NE])
    gnh = din("gnh", [D, 1])
    gn1 = din("gn1", [D, 1])
    gns = din("gns", [D, 1])
    gn2 = din("gn2", [D, 1])
    epsc = din("epsc", [128, 1])

    yout = nc.dram_tensor("y", [OWN, D], F32, kind="ExternalOutput").ap()

    gate_hbm = nc.dram_tensor("gate_hbm", [HID, ZLEN], BF16).ap()
    hconv_hbm = nc.dram_tensor("hconv_hbm", [HID, WIN], BF16).ap()
    hst_hbm = nc.dram_tensor("hst_hbm", [HID, ZLEN], BF16).ap()
    xg_hbm = nc.dram_tensor("xg_hbm", [TRASH + 1, D], BF16).ap()
    yg_hbm = nc.dram_tensor("yg_hbm", [TRASH + 1, D], BF16).ap()
    xmid_hbm = nc.dram_tensor("xmid_hbm", [OWN, D], F32).ap()

    taps = {}
    if debug_phase is not None:
        for nm, shape, dt in [
            ("t_u", [128, DC * WIN], BF16), ("t_hconv", [128, HC * WIN], BF16),
            ("t_gate", [128, HC * ZLEN], BF16), ("t_hst", [128, HC * ZLEN], BF16),
            ("t_z", [128, DC * OWN], F32), ("t_zh", [128, DC * KV0], BF16),
            ("t_k", [128, 3200], BF16), ("t_v", [128, 3200], BF16),
            ("t_logits", [128, 16 * NE], F32), ("t_slot", [128, 16], I32),
            ("t_xmid", [128, 16 * D], F32),
        ]:
            taps[nm] = nc.dram_tensor(nm, shape, dt, kind="ExternalOutput").ap()

    es = contextlib.ExitStack()
    with es:
        P = Plan(nc, strict)
        for s in SEMS:
            P.sem[s] = es.enter_context(nc.semaphore(s))

        _uid = [0]

        def SB(st, name, shape, dt):
            _uid[0] += 1
            return st.enter_context(nc.sbuf_tensor(f"{name}_{_uid[0]}", list(shape), dt))

        def PS(st, name, shape, dt):
            _uid[0] += 1
            return st.enter_context(nc.psum_tensor(f"{name}_{_uid[0]}", list(shape), dt))

        idf_sb = SB(es, "idf", [128, 128], F32)
        idb_sb = SB(es, "idb", [128, 128], BF16)
        ones_sb = SB(es, "ones1", [1, 128], BF16)
        flag_sb = SB(es, "flag", [128, 1], F32)
        gn_sb = SB(es, "gn", [128, DC * 4], F32)
        small = SB(es, "small", [128, 64], F32)
        cw_sb = SB(es, "cw4", [128, HC * 4], F32)
        stat = SB(es, "stat", [128, 8], F32)
        s6 = SB(es, "s6", [128, 12], F32)
        slot_i = SB(es, "slot_i", [128, 16], I32)
        eps_sb = SB(es, "eps_sb", [128, 1], F32)
        idxf = SB(es, "idxf", [128, 16], F32)

        P.dma("sync", idf_sb[:], identf)
        P.dma("sync", idb_sb[:], identb)
        P.dma("sync", ones_sb[:], onesb)
        P.dma("sync", flag_sb[:], flagc)
        P.dma("sync", eps_sb[:], epsc)
        for i, g in enumerate((gnh, gn1, gns, gn2)):
            P.dma("sync", _r(gn_sb[:, i * DC:(i + 1) * DC], "p (c one) -> p c one", c=DC),
                  _r(g, "(c p) one -> p c one", p=128), allow_slow_non_contiguous=True)
        P.dma("sync", _r(cw_sb[:], "p (c k) -> p c k", c=HC), _r(cwi, "(c p) k -> p c k", p=128), allow_slow_non_contiguous=True)
        P.dma("sync", _r(small[:, 36:48], "p (c o) -> p c o", c=HC),
              _r(cbi, "(c p) one -> p c one", p=128), allow_slow_non_contiguous=True)
        P.dma("sync", _r(small[:, 48:60], "p (c o) -> p c o", c=HC),
              _r(hfb, "(c p) one -> p c one", p=128), allow_slow_non_contiguous=True)
        P.dma("sync", _r(small[:, 12:24], "p (c o) -> p c o", c=HC),
              _r(hbg[0:HID], "(c p) one -> p c one", p=128), allow_slow_non_contiguous=True)
        P.dma("sync", _r(small[:, 24:36], "p (c o) -> p c o", c=HC),
              _r(hbg[HID:2 * HID], "(c p) one -> p c one", p=128), allow_slow_non_contiguous=True)
        ldev = P.ev("ds")
        fb = small[:, 48:60]
        P.emit("scalar", lambda e: e.activation(small[:, 0:12], fb, AF.Abs),
               waits=[ldev], inc="qs")
        P.emit("scalar", lambda e: e.activation(small[:, 0:12], small[:, 0:12], AF.Exp,
                                                scale=-1.0), waits=[P.ev("qs")], inc="qs")
        P.emit("scalar", lambda e: e.activation(small[:, 0:12], small[:, 0:12], AF.Ln,
                                                bias=1.0), waits=[P.ev("qs")], inc="qs")
        P.emit("vector", lambda e: e.tensor_scalar_max(stat[:, 0:6], fb[:, 0:6], 0.0),
               waits=[ldev], inc="qv")
        P.emit("vector", lambda e: e.tensor_scalar_max(s6[:, 0:6], fb[:, 6:12], 0.0),
               waits=[ldev], inc="qv")
        P.emit("vector", lambda e: e.tensor_add(small[:, 0:6], small[:, 0:6], stat[:, 0:6]),
               waits=[P.ev("qs"), P.ev("qv")], inc="qv")
        P.emit("vector", lambda e: e.tensor_add(small[:, 6:12], small[:, 6:12], s6[:, 0:6]),
               waits=[P.ev("qv")], inc="qv")
        P.emit("vector", lambda e: e.tensor_scalar_mul(small[:, 0:12], small[:, 0:12], -4.0),
               waits=[P.ev("qv")], inc="qv")
        P.emit("vector", lambda e: e.tensor_scalar_mul(small[:, 12:36], small[:, 12:36], 0.5),
               waits=[ldev], inc="qv")
        consts_ev = P.ev("qv")

        def load_wT(eng, dst_sb, src_ap, kdim, ncols, waits=()):
            kc = kdim // 128
            return P.dma(eng, _r(dst_sb[:, 0:kc * ncols], "p (c n) -> p c n", c=kc),
                         _r(src_ap, "(c p) n -> p c n", p=128), waits=waits)

        def stats_on(tok_tile, waits=()):
            P.emit("vector", lambda e: e.bn_stats(s6[:, 0:6], tok_tile[:, 0:512]),
                   waits=list(waits), inc="qv")
            P.emit("vector", lambda e: e.bn_stats(s6[:, 6:12], tok_tile[:, 512:1024]),
                   waits=[P.ev("qv")], inc="qv")
            return P.emit("vector", lambda e: e.bn_aggr(stat[:, 0:2], s6[:]),
                          waits=[P.ev("qv")], inc="qv")

        def mk_scale(col_in, col_sq, col_rs, eps, aggr_ev):
            bias = 0.0 if eps is None else eps_sb[:, 0:1]
            if eps is not None:
                aggr_ev = P.emit("vector", lambda e: e.tensor_scalar_add(
                    stat[:, col_in:col_in + 1], stat[:, col_in:col_in + 1], EPS),
                    waits=[aggr_ev], inc="qv")
            e1 = P.emit("scalar", lambda e: e.activation(
                stat[:, col_sq:col_sq + 1], stat[:, col_in:col_in + 1], AF.Sqrt),
                waits=[aggr_ev], inc="qs")
            e2 = P.emit("vector", lambda e: e.reciprocal(
                stat[:, col_rs:col_rs + 1], stat[:, col_sq:col_sq + 1]),
                waits=[e1], inc="qv")
            e3 = P.emit("vector", lambda e: e.tensor_mul(
                stat[:, col_sq:col_sq + 1], stat[:, col_rs:col_rs + 1],
                stat[:, col_rs:col_rs + 1]), waits=[e2], inc="qv")
            e4 = P.emit("vector", lambda e: e.tensor_mul(
                stat[:, col_sq:col_sq + 1], stat[:, col_sq:col_sq + 1],
                stat[:, col_in:col_in + 1]), waits=[e3], inc="qv")
            e5 = P.emit("vector", lambda e: e.tensor_scalar(
                stat[:, col_sq:col_sq + 1], stat[:, col_sq:col_sq + 1], -0.5, 1.5,
                ALU.mult, ALU.add), waits=[e4], inc="qv")
            return P.emit("vector", lambda e: e.tensor_mul(
                stat[:, col_rs:col_rs + 1], stat[:, col_rs:col_rs + 1],
                stat[:, col_sq:col_sq + 1]), waits=[e5], inc="qv")

        def neg_m_rs(col_rs, col_out, rs_ev):
            return P.emit("vector", lambda e: e.scalar_tensor_tensor(
                stat[:, col_out:col_out + 1], stat[:, 0:1], -1.0,
                stat[:, col_rs:col_rs + 1], ALU.mult, ALU.mult),
                waits=[rs_ev], inc="qv")

        def transpose_tokmaj_to_fm(src_tok, pst, dtype_f32, extra_waits=()):
            bank_cols = 4 if dtype_f32 else 8
            idm = idf_sb if dtype_f32 else idb_sb
            for c in range(DC):
                P.emit("tensor", lambda e, c=c: e.matmul(
                    pst[:, c * 128:(c + 1) * 128], src_tok[:, c * 128:(c + 1) * 128],
                    idm[:], is_transpose=True,
                    start=(c % bank_cols == 0), stop=(c == DC - 1)),
                    waits=(list(extra_waits) if c == 0 else []),
                    inc="qt" if c == DC - 1 else None)
            return P.ev("qt")

        def transpose_fm_to_tokmaj(src_fm_slices, pst, dtype_f32, extra_waits=()):
            bank_cols = 4 if dtype_f32 else 8
            idm = idf_sb if dtype_f32 else idb_sb
            for c, sl in enumerate(src_fm_slices):
                P.emit("tensor", lambda e, c=c, sl=sl: e.matmul(
                    pst[:, c * 128:(c + 1) * 128], sl, idm[:], is_transpose=True,
                    start=(c % bank_cols == 0), stop=(c == DC - 1)),
                    waits=(list(extra_waits) if c == 0 else []),
                    inc="qt" if c == DC - 1 else None)
            return P.ev("qt")

        def z_slices(wp, width=128):
            own = wp >= PRE
            src = z if own else zh
            base = (wp - PRE) if own else (wp - KV0)
            w = OWN if own else KV0
            return [src[:, c * w + base: c * w + base + width] for c in range(DC)], own

        zstack = es.enter_context(contextlib.ExitStack())

        with contextlib.ExitStack() as pha:
            wg_sb = SB(pha, "wg_sb", [128, HC * 2 * HID], BF16)

            with contextlib.ExitStack() as psa:
                win_sb = SB(psa, "win_sb", [128, DC * 2 * HID], BF16)
                xa = SB(psa, "xa", [128, 2 * D], F32)
                u_fm = SB(psa, "u_fm", [128, DC * 1024], BF16)
                utok = SB(psa, "utok", [128, D], BF16)
                hpre = SB(psa, "hpre", [128, 1024 + 3], BF16)
                hcv = SB(psa, "hcv", [128, 1024], BF16)
                gtmp = SB(psa, "gtmp", [128, 1024], BF16)
                carry3 = SB(psa, "carry3", [128, HC * 3], BF16)
                psT = PS(psa, "psT", [128, 1024], BF16)
                pm0 = PS(psa, "pm0", [128, 1024], F32)
                pm1 = PS(psa, "pm1", [128, 1024], F32)

                wev = load_wT("gpsimd", win_sb, winT, D, 2 * HID)
                for c in range(DC):
                    P.emit("vector", lambda e, c=c: e.tensor_scalar_mul(
                        win_sb[:, c * 2 * HID:(c + 1) * 2 * HID],
                        win_sb[:, c * 2 * HID:(c + 1) * 2 * HID], gn_sb[:, c:c + 1]),
                        waits=[wev, consts_ev], inc="qv")
                win_ready = P.ev("qv")
                wg_ev = load_wT("gpsimd", wg_sb, wgT, HID, 2 * HID)
                P.emit("gpsimd", lambda e: e.memset(carry3[:], 0.0), inc="qg")
                cz_ev = P.ev("qg")

                pm_free = {0: None, 1: None}
                TCH = 1024
                for ch in range(WIN // TCH):
                    t0 = ch * TCH
                    utile_evs = []
                    for i in range(TCH // 128):
                        buf = xa[:, (i % 2) * D:(i % 2) * D + D]
                        xev = P.dma("sync", buf, xwin[t0 + i * 128: t0 + (i + 1) * 128, :],
                                    waits=[P.ev("qs")])
                        agg = stats_on(buf, waits=[xev])
                        rse = mk_scale(1, 2, 4, None, agg)
                        nme = neg_m_rs(4, 6, rse)
                        aev = P.emit("scalar", lambda e, buf=buf: e.activation(
                            utok[:], buf, AF.Identity, bias=stat[:, 6:7], scale=stat[:, 4:5]),
                            waits=[nme], inc="qs")
                        tev = transpose_tokmaj_to_fm(utok, psT, False,
                                                     extra_waits=[aev, win_ready, P.ev("qv")])
                        cev = P.emit("vector", lambda e, i=i: e.tensor_copy(
                            _r(u_fm[:], "p (c n) -> p c n", c=DC)[:, :, i * 128:(i + 1) * 128],
                            _r(psT[:], "p (c n) -> p c n", c=DC)),
                            waits=[tev], inc="qv")
                        utile_evs.append(cev)
                    u_ready = P.ev("qv")
                    if debug_phase == "A1" and taps:
                        P.dma("sync", _r(taps["t_u"], "p (c w) -> p c w", c=DC)[:, :, t0:t0 + TCH],
                              _r(u_fm[:], "p (c n) -> p c n", c=DC), waits=[u_ready])

                    do_gate = t0 >= KV0
                    mlist = [("h", m) for m in range(HC)] + \
                            ([("g", m) for m in range(HC)] if do_gate else [])
                    for kind, m in mlist:
                        col0 = (HID if kind == "h" else 0) + m * 128
                        pmi = m % 2
                        pm = pm0 if pmi == 0 else pm1
                        for nb in range(2):
                            for kc in range(DC):
                                P.emit("tensor", lambda e, kc=kc, nb=nb, col0=col0, pm=pm: e.matmul(
                                    pm[:, nb * 512:(nb + 1) * 512],
                                    win_sb[:, kc * 2 * HID + col0: kc * 2 * HID + col0 + 128],
                                    u_fm[:, kc * 1024 + nb * 512: kc * 1024 + (nb + 1) * 512],
                                    start=(kc == 0), stop=(kc == DC - 1)),
                                    waits=([u_ready, pm_free[pmi], wg_ev if kind == "h" else None]
                                           if (kc == 0 and nb == 0) else []),
                                    inc="qt" if (kc == DC - 1 and nb == 1) else None)
                        mmev = P.ev("qt")
                        if kind == "h":
                            e1 = P.emit("vector", lambda e, pm=pm: e.tensor_copy(
                                hpre[:, 3:1027], pm[:]), waits=[mmev, P.ev("dg"), P.ev("qv")],
                                inc="qv")
                            pm_free[pmi] = e1
                            e2 = P.emit("vector", lambda e, m=m: e.tensor_copy(
                                hpre[:, 0:3], carry3[:, m * 3:m * 3 + 3]),
                                waits=[e1, cz_ev], inc="qv")
                            if t0 == PRE:
                                e2 = P.emit("vector", lambda e: e.tensor_scalar_mul(
                                    hpre[:, 0:3], hpre[:, 0:3], flag_sb[:, 0:1]),
                                    waits=[e2], inc="qv")
                            e3 = P.emit("vector", lambda e, m=m: e.tensor_copy(
                                carry3[:, m * 3:m * 3 + 3], hpre[:, 1024:1027]),
                                waits=[e1], inc="qv")
                            c0 = P.emit("vector", lambda e, m=m: e.tensor_scalar(
                                hcv[:], hpre[:, 0:1024], cw_sb[:, m * 4:m * 4 + 1],
                                small[:, 36 + m:37 + m], ALU.mult, ALU.add),
                                waits=[e2, e3], inc="qv")
                            for k in range(1, 4):
                                c0 = P.emit("vector", lambda e, k=k, m=m: e.scalar_tensor_tensor(
                                    hcv[:], hpre[:, k:k + 1024], cw_sb[:, m * 4 + k:m * 4 + k + 1],
                                    hcv[:], ALU.mult, ALU.add), waits=[c0], inc="qv")
                            P.dma("gpsimd", hconv_hbm[m * 128:(m + 1) * 128, t0:t0 + TCH],
                                  hcv[:], waits=[c0])
                            P.wait("vector", [P.ev("dg")])
                        else:
                            gev = P.emit("scalar", lambda e, pm=pm: e.activation(
                                gtmp[:], pm[:], AF.Identity), waits=[mmev, P.ev("dg")], inc="qs")
                            pm_free[pmi] = gev
                            P.dma("gpsimd",
                                  gate_hbm[m * 128:(m + 1) * 128, t0 - KV0:t0 - KV0 + TCH],
                                  gtmp[:], waits=[gev])
                            P.wait("scalar", [P.ev("dg")])
                P.barrier()

                if debug_phase == "A1":
                    tmp = SB(psa, "dbg1", [128, WIN], BF16)
                    for m in range(HC):
                        ev = P.dma("sync", tmp[:], hconv_hbm[m * 128:(m + 1) * 128, :])
                        P.dma("sync", taps["t_hconv"][:, m * WIN:(m + 1) * WIN], tmp[:],
                              waits=[ev])
                    _finish(P, nc, es)
                    return nc

            with contextlib.ExitStack() as psa:
                hcb = SB(psa, "hcb", [128, 2 * HC * 512], BF16)
                tf = SB(psa, "tf", [128, 512], F32)
                al = SB(psa, "al", [128, 2 * 512], F32)
                ti = SB(psa, "ti", [128, 2 * 512], F32)
                sg = SB(psa, "sg", [128, 512], F32)
                w1s = SB(psa, "w1s", [128, 2 * 512], F32)
                xs_ = SB(psa, "xs_", [128, 2 * 512], F32)
                hsc = SB(psa, "hsc", [128, 512], F32)
                hsb = SB(psa, "hsb", [128, 2 * 512], BF16)
                stcol = SB(psa, "stcol", [128, HC], F32)
                pf = [PS(psa, "pf0", [128, 512], F32), PS(psa, "pf1", [128, 512], F32)]
                pi_ = [PS(psa, "pi0", [128, 512], F32), PS(psa, "pi1", [128, 512], F32)]

                P.emit("vector", lambda e: e.memset(stcol[:], 0.0), waits=[P.ev("qt")], inc="qv")
                TB = 512
                hc_free = {0: None, 1: None}
                ev_e1 = {}; ev_e2 = {}; ev_e3 = {}; ev_g1 = {}; ev_g2 = {}
                ev_g5 = {}; ev_sev = {}; ev_hdma = {0: None, 1: None}
                gi = 0
                for tb in range(WIN // TB):
                    t0 = tb * TB
                    half = tb % 2
                    dst = hcb[:, half * HC * 512:(half + 1) * HC * 512]
                    hev = P.dma("sync", _r(dst, "p (c n) -> p c n", c=HC),
                                _r(hconv_hbm[:, t0:t0 + TB], "(c p) n -> p c n", p=128),
                                waits=[hc_free[half]])
                    for c in range(HC):
                        p = gi % 2
                        pfp, pip = pf[p], pi_[p]
                        als = al[:, p * 512:(p + 1) * 512]
                        tis = ti[:, p * 512:(p + 1) * 512]
                        w1p = w1s[:, p * 512:(p + 1) * 512]
                        xsp = xs_[:, p * 512:(p + 1) * 512]
                        for kc in range(HC):
                            P.emit("tensor", lambda e, kc=kc, c=c, pfp=pfp, dst=dst: e.matmul(
                                pfp[:],
                                wg_sb[:, kc * 2 * HID + c * 128:
                                      kc * 2 * HID + c * 128 + 128],
                                dst[:, kc * 512:(kc + 1) * 512],
                                start=(kc == 0), stop=(kc == HC - 1)),
                                waits=([hev, ev_e1.get(gi - 2)] if kc == 0 else []),
                                inc="qt" if kc == HC - 1 else None)
                        mmf = P.ev("qt")
                        for kc in range(HC):
                            P.emit("tensor", lambda e, kc=kc, c=c, pip=pip, dst=dst: e.matmul(
                                pip[:],
                                wg_sb[:, kc * 2 * HID + HID + c * 128:
                                      kc * 2 * HID + HID + c * 128 + 128],
                                dst[:, kc * 512:(kc + 1) * 512],
                                start=(kc == 0), stop=(kc == HC - 1)),
                                waits=([ev_e3.get(gi - 2)] if kc == 0 else []),
                                inc="qt" if kc == HC - 1 else None)
                        mmi = P.ev("qt")
                        ev_e1[gi] = P.emit("scalar", lambda e, c=c, pfp=pfp: e.activation(
                            tf[:], pfp[:], AF.Tanh, bias=small[:, 12 + c:13 + c], scale=0.5),
                            waits=[mmf], inc="qs")
                        ev_e2[gi] = P.emit("scalar", lambda e, c=c, als=als: e.activation(
                            als, tf[:], AF.Exp, bias=small[:, c:c + 1],
                            scale=small[:, c:c + 1]),
                            waits=[ev_e1[gi], ev_g2.get(gi - 2), ev_sev.get(gi - 2)],
                            inc="qs")
                        ev_e3[gi] = P.emit("scalar", lambda e, c=c, pip=pip, tis=tis: e.activation(
                            tis, pip[:], AF.Tanh, bias=small[:, 24 + c:25 + c], scale=0.5),
                            waits=[mmi, ev_g1.get(gi - 2)], inc="qs")
                        ev_g1[gi] = P.emit("gpsimd", lambda e, tis=tis: e.tensor_scalar(
                            sg[:], tis, 0.5, 0.5, ALU.mult, ALU.add),
                            waits=[ev_e3[gi], hev], inc="qg")
                        ev_g2[gi] = P.emit("gpsimd", lambda e, als=als, w1p=w1p: e.tensor_mul(
                            w1p, als, als), waits=[ev_e2[gi]], inc="qg")
                        g3 = P.emit("gpsimd", lambda e, w1p=w1p: e.tensor_scalar(
                            w1p, w1p, -1.0, 1.0 + 1e-6, ALU.mult, ALU.add),
                            waits=[ev_g2[gi]], inc="qg")
                        e4 = P.emit("scalar", lambda e, w1p=w1p: e.activation(
                            w1p, w1p, AF.Sqrt), waits=[g3], inc="qs")
                        g4 = P.emit("gpsimd", lambda e, w1p=w1p, xsp=xsp: e.tensor_mul(
                            xsp, sg[:], w1p), waits=[e4, ev_sev.get(gi - 2)], inc="qg")
                        g5 = P.emit("gpsimd", lambda e, c=c, dst=dst, xsp=xsp: e.tensor_mul(
                            xsp, xsp, dst[:, c * 512:(c + 1) * 512]),
                            waits=[g4], inc="qg")
                        ev_g5[gi] = g5
                        init_ev = None
                        if tb == PRE // TB:
                            init_ev = P.emit("vector", lambda e, c=c: e.tensor_scalar_mul(
                                stcol[:, c:c + 1], stcol[:, c:c + 1], flag_sb[:, 0:1]),
                                waits=[P.ev("qv")], inc="qv")
                        sev = P.emit("vector", lambda e, c=c, als=als, xsp=xsp: e.tensor_tensor_scan(
                            hsc[:], als, xsp, stcol[:, c:c + 1], ALU.mult, ALU.add),
                            waits=[g5, ev_e2[gi], init_ev], inc="qv")
                        ev_sev[gi] = sev
                        P.emit("vector", lambda e, c=c: e.tensor_copy(
                            stcol[:, c:c + 1], hsc[:, 511:512]), waits=[sev], inc="qv")
                        if t0 >= KV0:
                            hp = gi % 2
                            bev = P.emit("vector", lambda e, hp=hp: e.tensor_copy(
                                hsb[:, hp * 512:(hp + 1) * 512], hsc[:]),
                                waits=[sev, ev_hdma[hp]], inc="qv")
                            ev_hdma[hp] = P.dma(
                                "sync",
                                hst_hbm[c * 128:(c + 1) * 128, t0 - KV0:t0 - KV0 + TB],
                                hsb[:, hp * 512:(hp + 1) * 512], waits=[bev])
                        gi += 1
                    hc_free[half] = ev_g5[gi - 1]
                P.barrier()

            if debug_phase == "A2":
                tmp = SB(pha, "dbg2", [128, ZLEN], BF16)
                for m in range(HC):
                    ev = P.dma("sync", tmp[:], hst_hbm[m * 128:(m + 1) * 128, :])
                    P.dma("sync", taps["t_hst"][:, m * ZLEN:(m + 1) * ZLEN], tmp[:], waits=[ev])
                    ev = P.dma("sync", tmp[:], gate_hbm[m * 128:(m + 1) * 128, :])
                    P.dma("sync", taps["t_gate"][:, m * ZLEN:(m + 1) * ZLEN], tmp[:], waits=[ev])
                _finish(P, nc, es)
                return nc

        z = SB(zstack, "z", [128, DC * OWN], F32)
        zh = SB(zstack, "zh", [128, DC * KV0], BF16)

        def dump_z():
            P.dma("sync", taps["t_z"], z[:], waits=P.all_ev())
            P.dma("sync", taps["t_zh"], zh[:], waits=P.all_ev())
            _finish(P, nc, es)

        with contextlib.ExitStack() as phb, contextlib.ExitStack() as psb:
            wo_sb = SB(phb, "wo_sb", [128, HC * D], BF16)
            gh_sb = SB(phb, "gh_sb", [128, HC * 1024], BF16)
            gl_sb = SB(phb, "gl_sb", [128, 1024], BF16)
            gld = SB(phb, "gld", [128, 2 * 1024], BF16)
            hld = SB(phb, "hld", [128, 2 * 1024], BF16)
            xb2 = SB(phb, "xb2", [128, 2 * D], F32)
            ztok = SB(phb, "ztok", [128, D], F32)
            psz = PS(psb, "psz", [128, 1024], F32)
            pw0 = PS(psb, "pw0", [128, 1024], F32)

            woev = load_wT("gpsimd", wo_sb, woutT, HID, D)
            TCH = 1024
            for ci in range(ZLEN // TCH):
                w0 = KV0 + ci * TCH
                for c in range(HC):
                    half = c % 2
                    gev = P.dma("sync", gld[:, half * 1024:(half + 1) * 1024],
                                gate_hbm[c * 128:(c + 1) * 128, ci * TCH:(ci + 1) * TCH],
                                waits=[P.ev("qs")])
                    hev = P.dma("sync", hld[:, half * 1024:(half + 1) * 1024],
                                hst_hbm[c * 128:(c + 1) * 128, ci * TCH:(ci + 1) * TCH],
                                waits=[P.ev("qv")])
                    sev = P.emit("scalar", lambda e, half=half: e.activation(
                        gl_sb[:], gld[:, half * 1024:(half + 1) * 1024], AF.Gelu),
                        waits=[gev, P.ev("qv")], inc="qs")
                    P.emit("vector", lambda e, c=c, half=half: e.tensor_mul(
                        gh_sb[:, c * 1024:(c + 1) * 1024], gl_sb[:],
                        hld[:, half * 1024:(half + 1) * 1024]),
                        waits=[sev, hev], inc="qv")
                gh_ready = P.ev("qv")
                for i in range(TCH // 128):
                    wp = w0 + i * 128
                    buf = xb2[:, (i % 2) * D:(i % 2) * D + D]
                    xev = P.dma("sync", buf, xwin[wp:wp + 128, :], waits=[P.ev("qs")])
                    agg = stats_on(buf, waits=[xev])
                    rse = mk_scale(1, 3, 5, True, agg)
                    nme = neg_m_rs(5, 7, rse)
                    aev = P.emit("scalar", lambda e, buf=buf: e.activation(
                        ztok[:], buf, AF.Identity, bias=stat[:, 7:8], scale=stat[:, 5:6]),
                        waits=[nme], inc="qs")
                    tev = transpose_tokmaj_to_fm(ztok, psz, True,
                                                 extra_waits=[aev, woev, P.ev("qv")])
                    sls, own = z_slices(wp)
                    tgt = (z if own else zh)
                    w = OWN if own else KV0
                    base = (wp - PRE) if own else (wp - KV0)
                    P.emit("vector", lambda e, tgt=tgt, w=w, base=base: e.tensor_copy(
                        _r(tgt[:], "p (c n) -> p c n", c=DC)[:, :, base:base + 128],
                        _r(psz[:], "p (c n) -> p c n", c=DC)),
                        waits=[tev], inc="qv")
                z0_ready = P.ev("qv")
                for dm in range(DC):
                    for nb in range(2):
                        for kc in range(HC):
                            P.emit("tensor", lambda e, dm=dm, nb=nb, kc=kc: e.matmul(
                                pw0[:, nb * 512:(nb + 1) * 512],
                                wo_sb[:, kc * D + dm * 128: kc * D + dm * 128 + 128],
                                gh_sb[:, kc * 1024 + nb * 512: kc * 1024 + (nb + 1) * 512],
                                start=(kc == 0), stop=(kc == HC - 1)),
                                waits=([gh_ready, z0_ready, P.ev("qv")]
                                       if (kc == 0 and nb == 0) else []),
                                inc="qt" if (kc == HC - 1 and nb == 1) else None)
                    mmev = P.ev("qt")
                    own = w0 >= PRE
                    zt = (z[:, dm * OWN + (w0 - PRE): dm * OWN + (w0 - PRE) + TCH] if own
                          else zh[:, dm * KV0 + (w0 - KV0): dm * KV0 + (w0 - KV0) + TCH])
                    P.emit("vector", lambda e, zt=zt: e.tensor_add(zt, zt, pw0[:]),
                           waits=[mmev], inc="qv")
            P.barrier()

        if debug_phase == "B":
            dump_z()
            return nc

        def gmlp(gT, sT, gcol, rng_start, rng_len):
            with contextlib.ExitStack() as phc, contextlib.ExitStack() as psc:
                u_c = SB(phc, "u_c", [128, DC * 1024], BF16)
                utk = SB(phc, "utk", [128, D], BF16)
                gel = SB(phc, "gel", [128, 1024], BF16)
                ghm = SB(phc, "ghm", [128, 16 * 1024], BF16)
                wmA = SB(phc, "wmA", [128, 2 * DC * 128], BF16)
                wmB = SB(phc, "wmB", [128, 2 * DC * 128], BF16)
                wsA = SB(phc, "wsA", [128, 2 * 16 * 128], BF16)
                ptk = PS(psc, "ptk", [128, 1024], F32)
                ptb = PS(psc, "ptb", [128, 1024], BF16)
                pg = PS(psc, "pg", [128, 1024], F32)
                ph_ = PS(psc, "ph_", [128, 1024], F32)

                for ci in range(rng_len // 1024):
                    w0 = rng_start + ci * 1024
                    for i in range(8):
                        wp = w0 + i * 128
                        sls, own = z_slices(wp)
                        pin = ptk if own else ptb
                        tev = transpose_fm_to_tokmaj(sls, pin, own,
                                                     extra_waits=[P.ev("qv"), P.ev("qs")])
                        agg = stats_on(pin[:], waits=[tev])
                        e1 = P.emit("vector", lambda e: e.scalar_tensor_tensor(
                            stat[:, 2:3], stat[:, 0:1], 1.0, stat[:, 0:1], ALU.mult, ALU.mult),
                            waits=[agg], inc="qv")
                        e2 = P.emit("vector", lambda e: e.tensor_add(
                            stat[:, 2:3], stat[:, 2:3], stat[:, 1:2]), waits=[e1], inc="qv")
                        e4 = mk_scale(2, 3, 4, None, e2)
                        aev = P.emit("scalar", lambda e, pin=pin: e.activation(
                            utk[:], pin[:], AF.Identity, scale=stat[:, 4:5]),
                            waits=[e4], inc="qs")
                        tev = transpose_tokmaj_to_fm(utk, ptb, False,
                                                     extra_waits=[aev, P.ev("qv")])
                        P.emit("vector", lambda e, i=i: e.tensor_copy(
                            _r(u_c[:], "p (c n) -> p c n", c=DC)[:, :, i * 128:(i + 1) * 128],
                            _r(ptb[:], "p (c n) -> p c n", c=DC)),
                            waits=[tev], inc="qv")
                    for c in range(DC):
                        P.emit("vector", lambda e, c=c: e.tensor_scalar_mul(
                            u_c[:, c * 1024:(c + 1) * 1024], u_c[:, c * 1024:(c + 1) * 1024],
                            gn_sb[:, gcol * DC + c:gcol * DC + c + 1]),
                            waits=[P.ev("qv")], inc="qv")
                    u_ready = P.ev("qv")

                    wflip = 0
                    wfree = {0: None, 1: None}
                    for m in range(16):
                        wA = wmA[:, wflip * DC * 128:(wflip + 1) * DC * 128]
                        wB = wmB[:, wflip * DC * 128:(wflip + 1) * DC * 128]
                        evA = P.dma("gpsimd", _r(wA, "p (c n) -> p c n", c=DC),
                                    _r(gT[:, m * 128:(m + 1) * 128], "(c p) n -> p c n", p=128),
                                    waits=[wfree[wflip]])
                        evB = P.dma("gpsimd", _r(wB, "p (c n) -> p c n", c=DC),
                                    _r(gT[:, GH + m * 128: GH + (m + 1) * 128],
                                       "(c p) n -> p c n", p=128))
                        for nb in range(2):
                            for kc in range(DC):
                                P.emit("tensor", lambda e, kc=kc, nb=nb, wA=wA: e.matmul(
                                    pg[:, nb * 512:(nb + 1) * 512],
                                    wA[:, kc * 128:(kc + 1) * 128],
                                    u_c[:, kc * 1024 + nb * 512: kc * 1024 + (nb + 1) * 512],
                                    start=(kc == 0), stop=(kc == DC - 1)),
                                    waits=([u_ready, evA, P.ev("qs")]
                                           if (kc == 0 and nb == 0) else []),
                                    inc="qt" if (kc == DC - 1 and nb == 1) else None)
                        for nb in range(2):
                            for kc in range(DC):
                                P.emit("tensor", lambda e, kc=kc, nb=nb, wB=wB: e.matmul(
                                    ph_[:, nb * 512:(nb + 1) * 512],
                                    wB[:, kc * 128:(kc + 1) * 128],
                                    u_c[:, kc * 1024 + nb * 512: kc * 1024 + (nb + 1) * 512],
                                    start=(kc == 0), stop=(kc == DC - 1)),
                                    waits=([evB, P.ev("qv")] if (kc == 0 and nb == 0) else []),
                                    inc="qt" if (kc == DC - 1 and nb == 1) else None)
                        mmev = P.ev("qt")
                        gev = P.emit("scalar", lambda e: e.activation(gel[:], pg[:], AF.Gelu),
                                     waits=[mmev], inc="qs")
                        mev = P.emit("vector", lambda e, m=m: e.tensor_mul(
                            ghm[:, m * 1024:(m + 1) * 1024], gel[:], ph_[:]),
                            waits=[gev], inc="qv")
                        wfree[wflip] = mev
                        wflip ^= 1
                    gh_ready = P.ev("qv")

                    sflip = 0
                    sfree = {0: None, 1: None}
                    for dm in range(DC):
                        wS = wsA[:, sflip * 16 * 128:(sflip + 1) * 16 * 128]
                        evS = P.dma("gpsimd", _r(wS, "p (c n) -> p c n", c=16),
                                    _r(sT[:, dm * 128:(dm + 1) * 128], "(c p) n -> p c n", p=128),
                                    waits=[sfree[sflip]])
                        for nb in range(2):
                            for kc in range(16):
                                P.emit("tensor", lambda e, kc=kc, nb=nb, wS=wS: e.matmul(
                                    pg[:, nb * 512:(nb + 1) * 512],
                                    wS[:, kc * 128:(kc + 1) * 128],
                                    ghm[:, kc * 1024 + nb * 512: kc * 1024 + (nb + 1) * 512],
                                    start=(kc == 0), stop=(kc == 15)),
                                    waits=([gh_ready, evS, P.ev("qv"), P.ev("qs")]
                                           if (kc == 0 and nb == 0) else []),
                                    inc="qt" if (kc == 15 and nb == 1) else None)
                        mmev = P.ev("qt")
                        own = w0 >= PRE
                        zt = (z[:, dm * OWN + (w0 - PRE): dm * OWN + (w0 - PRE) + 1024] if own
                              else zh[:, dm * KV0 + (w0 - KV0): dm * KV0 + (w0 - KV0) + 1024])
                        sfree[sflip] = P.emit("vector", lambda e, zt=zt: e.tensor_add(
                            zt, zt, pg[:]), waits=[mmev], inc="qv")
                        sflip ^= 1
                P.barrier()

        gmlp(g1gT, g1sT, 1, KV0, ZLEN)
        if debug_phase == "C":
            dump_z()
            return nc

        with contextlib.ExitStack() as phd:
            wq_sb = SB(phd, "wq_sb", [128, DC * D], BF16)
            wkv_sb = SB(phd, "wkv_sb", [128, DC * 256], BF16)
            kfull = SB(phd, "kfull", [128, 3200], BF16)
            vtok = SB(phd, "vtok", [128, 25 * 128], BF16)
            qrot = SB(phd, "qrot", [128, QH * 1024], BF16)
            u_c = SB(phd, "u_cD", [128, DC * 1024], BF16)
            utk = SB(phd, "utkD", [128, D], BF16)
            cosl = SB(phd, "cosl", [128, ZLEN], BF16)
            sinl = SB(phd, "sinl", [128, ZLEN], BF16)
            tsh = SB(phd, "tsh", [128, 1024], BF16)
            tmix = SB(phd, "tmix", [128, 1024], BF16)

            wqe = load_wT("gpsimd", wq_sb, wqT, D, D)
            wkve = load_wT("gpsimd", wkv_sb, wkvT, D, 2 * HD)
            P.dma("sync", cosl[:], costab)
            P.dma("sync", sinl[:], sintab)
            tabev = P.ev("ds")
            for c in range(DC):
                P.emit("vector", lambda e, c=c: e.tensor_scalar_mul(
                    wq_sb[:, c * D:(c + 1) * D], wq_sb[:, c * D:(c + 1) * D],
                    gn_sb[:, 2 * DC + c:2 * DC + c + 1]), waits=[wqe], inc="qv")
                P.emit("vector", lambda e, c=c: e.tensor_scalar_mul(
                    wkv_sb[:, c * 256:(c + 1) * 256], wkv_sb[:, c * 256:(c + 1) * 256],
                    gn_sb[:, 2 * DC + c:2 * DC + c + 1]), waits=[wkve], inc="qv")
            w_ready = P.ev("qv")
            P.emit("gpsimd", lambda e: e.memset(kfull[:, 3072:3200], 0.0), inc="qg")
            kpad_ev = P.ev("qg")

            shmask = []
            for i in range(0, 32, 2):
                shmask += [i + 1, i]

            def rms_u_chunk(w0):
              with contextlib.ExitStack() as psn:
                psc_ptk = PS(psn, "ptk_n", [128, 1024], F32)
                psc_ptb = PS(psn, "ptb_n", [128, 1024], BF16)
                for i in range(8):
                    wp = w0 + i * 128
                    sls, own = z_slices(wp)
                    pin = psc_ptk if own else psc_ptb
                    tev = transpose_fm_to_tokmaj(sls, pin, own,
                                                 extra_waits=[P.ev("qv"), P.ev("qs")])
                    agg = stats_on(pin[:], waits=[tev])
                    e1 = P.emit("vector", lambda e: e.scalar_tensor_tensor(
                        stat[:, 2:3], stat[:, 0:1], 1.0, stat[:, 0:1], ALU.mult, ALU.mult),
                        waits=[agg], inc="qv")
                    e2 = P.emit("vector", lambda e: e.tensor_add(
                        stat[:, 2:3], stat[:, 2:3], stat[:, 1:2]), waits=[e1], inc="qv")
                    e4 = mk_scale(2, 3, 4, None, e2)
                    aev = P.emit("scalar", lambda e, pin=pin: e.activation(
                        utk[:], pin[:], AF.Identity, scale=stat[:, 4:5]), waits=[e4], inc="qs")
                    tev = transpose_tokmaj_to_fm(utk, psc_ptb, False,
                                                 extra_waits=[aev, P.ev("qv")])
                    P.emit("vector", lambda e, i=i: e.tensor_copy(
                        _r(u_c[:], "p (c n) -> p c n", c=DC)[:, :, i * 128:(i + 1) * 128],
                        _r(psc_ptb[:], "p (c n) -> p c n", c=DC)),
                        waits=[tev], inc="qv")
                return P.ev("qv")

            with contextlib.ExitStack() as psd:
                vfull = SB(psd, "vfull", [128, 3200], BF16)
                pkv = PS(psd, "pkv", [128, 1024], F32)
                P.emit("gpsimd", lambda e: e.memset(vfull[:, 3072:3200], 0.0), inc="qg")
                pad_ev = P.ev("qg")
                for ci in range(3):
                    w0 = KV0 + ci * 1024
                    uev = rms_u_chunk(w0)
                    for hm, dst in ((0, kfull), (1, vfull)):
                        for nb in range(2):
                            for kc in range(DC):
                                P.emit("tensor", lambda e, kc=kc, nb=nb, hm=hm: e.matmul(
                                    pkv[:, nb * 512:(nb + 1) * 512],
                                    wkv_sb[:, kc * 256 + hm * 128: kc * 256 + hm * 128 + 128],
                                    u_c[:, kc * 1024 + nb * 512: kc * 1024 + (nb + 1) * 512],
                                    start=(kc == 0), stop=(kc == DC - 1)),
                                    waits=([uev, w_ready, P.ev("qv")]
                                           if (kc == 0 and nb == 0) else []),
                                    inc="qt" if (kc == DC - 1 and nb == 1) else None)
                        mmev = P.ev("qt")
                        cols = slice(ci * 1024, (ci + 1) * 1024)
                        if hm == 1:
                            P.emit("vector", lambda e, cols=cols: e.tensor_copy(
                                vfull[:, cols], pkv[:]), waits=[mmev], inc="qv")
                        else:
                            e1 = P.emit("vector", lambda e: e.tensor_copy(tmix[:], pkv[:]),
                                        waits=[mmev, tabev], inc="qv")
                            e2 = P.emit("vector", lambda e: e.stream_shuffle(
                                tsh[:], tmix[:], shmask), waits=[e1], inc="qv")
                            e3 = P.emit("vector", lambda e, cols=cols: e.tensor_mul(
                                kfull[:, cols], tmix[:], cosl[:, cols]), waits=[e2], inc="qv")
                            e4 = P.emit("vector", lambda e, cols=cols: e.tensor_mul(
                                tsh[:], tsh[:], sinl[:, cols]), waits=[e3], inc="qv")
                            P.emit("vector", lambda e, cols=cols: e.tensor_add(
                                kfull[:, cols], kfull[:, cols], tsh[:]), waits=[e4], inc="qv")
                kv_ready = P.ev("qv")
                with contextlib.ExitStack() as psv:
                    pvt = PS(psv, "pvt", [128, 128], BF16)
                    for i in range(25):
                        P.emit("tensor", lambda e, i=i: e.matmul(
                            pvt[:], vfull[:, i * 128:(i + 1) * 128], idb_sb[:],
                            is_transpose=True, start=True, stop=True),
                            waits=[kv_ready, pad_ev, P.ev("qv")], inc="qt")
                        P.emit("vector", lambda e, i=i: e.tensor_copy(
                            vtok[:, i * 128:(i + 1) * 128], pvt[:]),
                            waits=[P.ev("qt")], inc="qv")
                vtok_ready = P.ev("qv")
                if debug_phase == "Dkv":
                    P.dma("sync", taps["t_k"], kfull[:], waits=[kv_ready, vtok_ready])
                    P.dma("sync", taps["t_v"], vfull[:], waits=[kv_ready])
                    _finish(P, nc, es)
                    return nc

            segs = ((0, 512), (512, 512), (1024, 256))
            for qc in range(2):
                w0 = PRE + qc * 1024
                guard = P.ev("qt")
                uev = rms_u_chunk(w0)
                with contextlib.ExitStack() as psd, contextlib.ExitStack() as sbd:
                    psS = PS(psd, "psS", [128, 1536], F32)
                    psPT = PS(psd, "psPT", [128, 1536], BF16)
                    psO = PS(psd, "psO", [128, 128], F32)
                    pop8 = PS(psd, "pop8", [128, 1024], F32)
                    p_sb2 = SB(sbd, "p_sb2", [128, 2 * BAND], BF16)
                    pt_sb2 = SB(sbd, "pt_sb2", [128, 2 * BAND], BF16)
                    wodq = SB(sbd, "wodq", [128, 2 * QH * 128], BF16)
                    otile = SB(sbd, "otile", [128, QH * 128], BF16)
                    msk2 = SB(sbd, "msk2", [128, 2 * BAND], BF16)

                    qrot_ev = {}
                    ev_qevac = None
                    for hq in range(QH):
                        for nb in range(2):
                            for kc in range(DC):
                                P.emit("tensor", lambda e, kc=kc, nb=nb, hq=hq: e.matmul(
                                    psS[:, nb * 512:(nb + 1) * 512],
                                    wq_sb[:, kc * D + hq * 128: kc * D + hq * 128 + 128],
                                    u_c[:, kc * 1024 + nb * 512: kc * 1024 + (nb + 1) * 512],
                                    start=(kc == 0), stop=(kc == DC - 1)),
                                    waits=([uev, w_ready, guard, ev_qevac]
                                           if (kc == 0 and nb == 0) else []),
                                    inc="qt" if kc == DC - 1 else None)
                            P.emit("vector", lambda e, nb=nb: e.tensor_copy(
                                tmix[:, nb * 512:(nb + 1) * 512],
                                psS[:, nb * 512:(nb + 1) * 512]),
                                waits=[P.ev("qt"), tabev], inc="qv")
                        ev_qevac = P.ev("qv")
                        qcols = slice(hq * 1024, (hq + 1) * 1024)
                        tcols = slice(1024 + qc * 1024, 1024 + (qc + 1) * 1024)
                        e2 = P.emit("vector", lambda e: e.stream_shuffle(
                            tsh[:], tmix[:], shmask), waits=[ev_qevac], inc="qv")
                        e3 = P.emit("vector", lambda e, qcols=qcols, tcols=tcols: e.tensor_mul(
                            qrot[:, qcols], tmix[:], cosl[:, tcols]),
                            waits=[e2, guard], inc="qv")
                        e4 = P.emit("vector", lambda e, tcols=tcols: e.tensor_mul(
                            tsh[:], tsh[:], sinl[:, tcols]), waits=[e3], inc="qv")
                        P.emit("vector", lambda e, qcols=qcols: e.tensor_add(
                            qrot[:, qcols], qrot[:, qcols], tsh[:]), waits=[e4], inc="qv")
                        qrot_ev[hq] = P.ev("qv")

                    NT = 64
                    ev_sc = {}; ev_exp = {}; ev_scaled = {}; ev_pt = {}
                    ev_ptc = {}; ev_pv = {}; ev_ot = {}; ev_op = {}
                    ev_popev = {}; ev_mk = {}; ev_msk = {}; ev_wfree = {}
                    for t in range(NT + 2):
                        if t < NT:
                            qi, hq = divmod(t, 8)
                            qt_ = qc * 8 + qi
                            par = t % 2
                            mpar = qi % 2
                            if hq == 0:
                                ev_mk[qi] = P.dma(
                                    "sync", msk2[:, mpar * BAND:(mpar + 1) * BAND],
                                    m01t[qt_ * 128:(qt_ + 1) * 128, :],
                                    waits=[ev_msk.get((qi - 2) * 8 + 7), vtok_ready, guard])
                            for si, (c0, cn) in enumerate(segs):
                                P.emit("tensor", lambda e, c0=c0, cn=cn, qt_=qt_, hq=hq,
                                       qi=qi: e.matmul(
                                    psS[:, c0:c0 + cn],
                                    qrot[:, hq * 1024 + qi * 128: hq * 1024 + qi * 128 + 128],
                                    kfull[:, qt_ * 128 + c0: qt_ * 128 + c0 + cn],
                                    start=True, stop=True),
                                    waits=([qrot_ev[hq], ev_exp.get(t - 1), kv_ready,
                                            vtok_ready, kpad_ev, ev_qevac]
                                           if si == 0 else []),
                                    inc="qt" if si == 2 else None)
                            ev_sc[t] = P.ev("qt")
                            emax = P.emit("vector", lambda e, par=par: e.reduce_max(
                                stat[:, 0 + par:1 + par], psS[:, 0:BAND], axis=AX.X,
                                negate=True), waits=[ev_sc[t]], inc="qv")
                            emsc = P.emit("vector", lambda e, par=par: e.tensor_scalar_mul(
                                stat[:, 2 + par:3 + par], stat[:, 0 + par:1 + par], SCL),
                                waits=[emax, ev_exp.get(t - 2)], inc="qv")
                            ev_exp[t] = P.emit("scalar", lambda e, par=par: e.activation(
                                p_sb2[:, par * BAND:(par + 1) * BAND], psS[:, 0:BAND],
                                AF.Exp, bias=stat[:, 2 + par:3 + par], scale=SCL),
                                waits=[emsc, ev_pt.get(t - 2)], inc="qs")
                            ev_msk[t] = P.emit("vector", lambda e, par=par,
                                               mpar=mpar: e.scalar_tensor_tensor(
                                p_sb2[:, par * BAND:(par + 1) * BAND],
                                p_sb2[:, par * BAND:(par + 1) * BAND], 1.0,
                                msk2[:, mpar * BAND:(mpar + 1) * BAND],
                                ALU.mult, ALU.mult,
                                accum_out=stat[:, 4 + par:5 + par]),
                                waits=[ev_exp[t], ev_mk.get(qi)], inc="qv")
                            erc = P.emit("vector", lambda e, par=par: e.reciprocal(
                                stat[:, 6 + par:7 + par], stat[:, 4 + par:5 + par]),
                                waits=[ev_msk[t]], inc="qv")
                            ev_scaled[t] = P.emit("vector", lambda e, par=par: e.tensor_scalar_mul(
                                p_sb2[:, par * BAND:(par + 1) * BAND],
                                p_sb2[:, par * BAND:(par + 1) * BAND],
                                stat[:, 6 + par:7 + par]), waits=[erc], inc="qv")
                        u = t - 1
                        if 0 <= u < NT:
                            paru = u % 2
                            for kt in range(10):
                                P.emit("tensor", lambda e, kt=kt, paru=paru: e.matmul(
                                    psPT[:, kt * 128:(kt + 1) * 128],
                                    p_sb2[:, paru * BAND + kt * 128:
                                          paru * BAND + (kt + 1) * 128],
                                    idb_sb[:], is_transpose=True,
                                    start=(kt in (0, 8)), stop=(kt == 9)),
                                    waits=([ev_scaled[u], ev_ptc.get(u - 1)]
                                           if kt == 0 else []),
                                    inc="qt" if kt == 9 else None)
                            ev_pt[u] = P.ev("qt")
                            ev_ptc[u] = P.emit("scalar", lambda e, paru=paru: e.activation(
                                pt_sb2[:, paru * BAND:(paru + 1) * BAND],
                                psPT[:, 0:BAND], AF.Identity),
                                waits=[ev_pt[u], ev_pv.get(u - 2)], inc="qs")
                        v = t - 2
                        if 0 <= v < NT:
                            qiv, hqv = divmod(v, 8)
                            qtv = qc * 8 + qiv
                            parv = v % 2
                            for kt in range(10):
                                P.emit("tensor", lambda e, kt=kt, qtv=qtv, parv=parv: e.matmul(
                                    psO[:], vtok[:, (qtv + kt) * 128:(qtv + kt + 1) * 128],
                                    pt_sb2[:, parv * BAND + kt * 128:
                                           parv * BAND + (kt + 1) * 128],
                                    start=(kt == 0), stop=(kt == 9)),
                                    waits=([ev_ptc[v], ev_ot.get(v - 1)] if kt == 0 else []),
                                    inc="qt" if kt == 9 else None)
                            ev_pv[v] = P.ev("qt")
                            ev_ot[v] = P.emit("vector", lambda e, hqv=hqv: e.tensor_copy(
                                otile[:, hqv * 128:(hqv + 1) * 128], psO[:]),
                                waits=[ev_pv[v], ev_op.get(qiv - 1)], inc="qv")
                            if hqv == 7:
                                for dm in range(DC):
                                    wpar = dm % 2
                                    wsl = wodq[:, wpar * QH * 128:(wpar + 1) * QH * 128]
                                    evw = P.dma("gpsimd", _r(wsl, "p (c n) -> p c n", c=QH),
                                                _r(woT[:, dm * 128:(dm + 1) * 128],
                                                   "(c p) n -> p c n", p=128),
                                                waits=[ev_wfree.get((qiv, dm - 2)),
                                                       ev_wfree.get((qiv - 1, DC - 2 + wpar)),
                                                       guard, vtok_ready])
                                    for kc in range(QH):
                                        P.emit("tensor", lambda e, kc=kc, dm=dm, wsl=wsl: e.matmul(
                                            pop8[:, dm * 128:(dm + 1) * 128],
                                            wsl[:, kc * 128:(kc + 1) * 128],
                                            otile[:, kc * 128:(kc + 1) * 128],
                                            start=(kc == 0), stop=(kc == QH - 1)),
                                            waits=([ev_ot[v], evw, ev_popev.get(qiv - 1)]
                                                   if kc == 0 else []),
                                            inc="qt" if kc == QH - 1 else None)
                                    ev_wfree[(qiv, dm)] = P.ev("qt")
                                ev_op[qiv] = P.ev("qt")
                                for dm in range(DC):
                                    P.emit("vector", lambda e, dm=dm, qtv=qtv: e.tensor_add(
                                        z[:, dm * OWN + qtv * 128: dm * OWN + qtv * 128 + 128],
                                        z[:, dm * OWN + qtv * 128: dm * OWN + qtv * 128 + 128],
                                        pop8[:, dm * 128:(dm + 1) * 128]),
                                        waits=([ev_op[qiv]] if dm == 0 else []),
                                        inc="qv" if dm == DC - 1 else None)
                                ev_popev[qiv] = P.ev("qv")
            P.barrier()

        if debug_phase == "D":
            dump_z()
            return nc

        gmlp(g2gT, g2sT, 3, PRE, OWN)
        if debug_phase == "E":
            dump_z()
            return nc

        with contextlib.ExitStack() as phf, contextlib.ExitStack() as psf:
            xb3 = SB(phf, "xb3", [128, 2 * D], F32)
            xmt = SB(phf, "xmt", [128, D], F32)
            uft = SB(phf, "uft", [128, D], BF16)
            fwg_sb = SB(phf, "fwg_sb", [128, DC * NE], F32)
            bgr_sb = SB(phf, "bgr_sb", [128, NE], F32)
            io8_sb = SB(phf, "io8_sb", [128, NE], F32)
            lgt = SB(phf, "lgt", [128, NE], F32)
            oh_t = SB(phf, "oh_t", [8, OWN], F32)
            onest = SB(phf, "onest", [8, OWN], F32)
            post = SB(phf, "post", [8, OWN], F32)
            mx8 = SB(phf, "mx8", [128, 8], F32)
            ix8 = SB(phf, "ix8", [128, 8], U32)
            ohm = SB(phf, "ohm", [128, 8], F32)
            posm = SB(phf, "posm", [128, 8], F32)
            selc = SB(phf, "selc", [128, 8], F32)
            csr = SB(phf, "csr", [1, 8], F32)
            mrow = SB(phf, "mrow", [1, 128], F32)
            onesf = SB(phf, "onesf", [128, 1], F32)
            ptk = PS(psf, "ptkF", [128, 1024], F32)
            pl = PS(psf, "plF", [128, 512], F32)
            pcs = PS(psf, "pcsF", [128, 256], F32)

            P.dma("sync", _r(fwg_sb[:], "p (c n) -> p c n", c=DC),
                  _r(fwg, "(c p) n -> p c n", p=128))
            P.dma("sync", bgr_sb[:], bgrep)
            P.dma("sync", io8_sb[:], iota8)
            fev = P.ev("ds")
            P.emit("gpsimd", lambda e: e.memset(onest[:], 1.0), inc="qg")
            P.emit("gpsimd", lambda e: e.memset(onesf[:], 1.0), inc="qg")
            gz = P.ev("qg")
            for kc in range(DC):
                P.emit("tensor", lambda e, kc=kc: e.matmul(
                    pcs[0:8, 0:1], fwg_sb[:, kc * NE:(kc + 1) * NE], onesf[:],
                    start=(kc == 0), stop=(kc == DC - 1)),
                    waits=([fev, gz] if kc == 0 else []),
                    inc="qt" if kc == DC - 1 else None)
            P.emit("vector", lambda e: e.tensor_copy(stat[0:8, 6:7], pcs[0:8, 0:1]),
                   waits=[P.ev("qt")], inc="qv")
            P.emit("tensor", lambda e: e.matmul(
                pcs[0:1, 8:16], stat[0:8, 6:7], idf_sb[0:8, 0:8],
                is_transpose=True, start=True, stop=True),
                waits=[P.ev("qv")], inc="qt")
            P.emit("vector", lambda e: e.tensor_copy(csr[:], pcs[0:1, 8:16]),
                   waits=[P.ev("qt")], inc="qv")
            P.emit("vector", lambda e: e.tensor_scalar_mul(csr[:], csr[:], -1.0),
                   waits=[P.ev("qv")], inc="qv")
            cs_ready = P.ev("qv")

            for i in range(16):
                buf = xb3[:, (i % 2) * D:(i % 2) * D + D]
                xev = P.dma("sync", buf, xwin[PRE + i * 128: PRE + (i + 1) * 128, :],
                            waits=[P.ev("qt"), P.ev("qv")])
                tev = transpose_fm_to_tokmaj(
                    [z[:, c * OWN + i * 128: c * OWN + i * 128 + 128] for c in range(DC)],
                    ptk, True, extra_waits=[P.ev("qv"), P.ev("qs")])
                xmev = P.emit("vector", lambda e, buf=buf: e.tensor_add(xmt[:], buf, ptk[:]),
                              waits=[tev, xev], inc="qv")
                P.dma("sync", xmid_hbm[i * 128:(i + 1) * 128, :], xmt[:], waits=[xmev])
                agg = stats_on(xmt[:], waits=[xmev])
                rse = mk_scale(1, 3, 5, True, agg)
                nme = neg_m_rs(5, 7, rse)
                tev = transpose_tokmaj_to_fm(buf, ptk, True, extra_waits=[xmev, nme])
                zfm = P.emit("vector", lambda e, i=i: e.tensor_add(
                    _r(z[:], "p (c n) -> p c n", c=DC)[:, :, i * 128:(i + 1) * 128],
                    _r(z[:], "p (c n) -> p c n", c=DC)[:, :, i * 128:(i + 1) * 128],
                    _r(ptk[:], "p (c n) -> p c n", c=DC)),
                    waits=[tev], inc="qv")
                P.emit("tensor", lambda e: e.matmul(
                    pcs[0:1, 128:256], stat[:, 0:1], idf_sb[:],
                    is_transpose=True, start=True, stop=True),
                    waits=[nme, zfm], inc="qt")
                mre = P.emit("vector", lambda e: e.tensor_copy(mrow[:], pcs[0:1, 128:256]),
                             waits=[P.ev("qt")], inc="qv")
                for kc in range(DC):
                    P.emit("tensor", lambda e, kc=kc, i=i: e.matmul(
                        pl[:, 0:NE], z[:, kc * OWN + i * 128: kc * OWN + i * 128 + 128],
                        fwg_sb[:, kc * NE:(kc + 1) * NE],
                        start=(kc == 0), stop=False),
                        waits=([zfm, cs_ready, mre] if kc == 0 else []))
                P.emit("tensor", lambda e: e.matmul(
                    pl[:, 0:NE], mrow[:], csr[:], start=False, stop=True), inc="qt")
                lgev = P.ev("qt")
                e1 = P.emit("vector", lambda e: e.tensor_scalar_mul(
                    lgt[:], pl[:, 0:NE], stat[:, 5:6]), waits=[lgev], inc="qv")
                e2 = P.emit("vector", lambda e: e.tensor_add(lgt[:], lgt[:], bgr_sb[:]),
                            waits=[e1, fev], inc="qv")
                if debug_phase == "F":
                    P.dma("sync", taps["t_logits"][:, i * NE:(i + 1) * NE], lgt[:], waits=[e2])
                e3 = P.emit("vector", lambda e: e.max(mx8[:], lgt[:]), waits=[e2], inc="qv")
                e4 = P.emit("vector", lambda e: e.max_index(ix8[:], mx8[:], lgt[:]),
                            waits=[e3], inc="qv")
                e5 = P.emit("vector", lambda e, i=i: e.tensor_copy(idxf[:, i:i + 1], ix8[:, 0:1]),
                            waits=[e4], inc="qv")
                e6 = P.emit("vector", lambda e, i=i: e.tensor_single_scalar(
                    ohm[:], io8_sb[:], idxf[:, i:i + 1], ALU.is_equal),
                    waits=[e5], inc="qv")
                P.emit("tensor", lambda e: e.matmul(
                    pcs[0:8, 0:128], ohm[:], idf_sb[:], is_transpose=True,
                    start=True, stop=True), waits=[e6], inc="qt")
                P.emit("vector", lambda e, i=i: e.tensor_copy(
                    oh_t[:, i * 128:(i + 1) * 128], pcs[0:8, 0:128]),
                    waits=[P.ev("qt")], inc="qv")
            P.barrier()

            scev = P.emit("vector", lambda e: e.tensor_tensor_scan(
                post[:], onest[:], oh_t[:], 0.0, ALU.mult, ALU.add),
                waits=[P.ev("qv"), gz], inc="qv")
            for i in range(16):
                P.emit("tensor", lambda e, i=i: e.matmul(
                    pcs[0:128, 0:8], post[:, i * 128:(i + 1) * 128], idf_sb[0:8, 0:8],
                    is_transpose=True, start=True, stop=True),
                    waits=[scev, P.ev("qv")], inc="qt")
                e0 = P.emit("vector", lambda e: e.tensor_copy(posm[:], pcs[0:128, 0:8]),
                            waits=[P.ev("qt")], inc="qv")
                e1 = P.emit("vector", lambda e, i=i: e.tensor_single_scalar(
                    ohm[:], io8_sb[:], idxf[:, i:i + 1], ALU.is_equal), waits=[e0], inc="qv")
                e2 = P.emit("vector", lambda e: e.tensor_mul(posm[:], posm[:], ohm[:]),
                            waits=[e1], inc="qv")
                e3 = P.emit("vector", lambda e: e.reduce_sum(selc[:, 0:1], posm[:], axis=AX.X),
                            waits=[e2], inc="qv")
                e4 = P.emit("vector", lambda e, i=i: e.scalar_tensor_tensor(
                    selc[:, 1:2], idxf[:, i:i + 1], float(CAP), selc[:, 0:1],
                    ALU.mult, ALU.add), waits=[e3], inc="qv")
                e5 = P.emit("vector", lambda e: e.tensor_scalar_add(
                    selc[:, 1:2], selc[:, 1:2], -1.0), waits=[e4], inc="qv")
                e6 = P.emit("vector", lambda e: e.tensor_single_scalar(
                    selc[:, 2:3], selc[:, 0:1], float(CAP) + 0.5, ALU.is_gt),
                    waits=[e5], inc="qv")
                e7 = P.emit("vector", lambda e: e.scalar_tensor_tensor(
                    selc[:, 3:4], selc[:, 1:2], -1.0, selc[:, 2:3], ALU.mult, ALU.mult),
                    waits=[e6], inc="qv")
                e8 = P.emit("vector", lambda e: e.scalar_tensor_tensor(
                    selc[:, 4:5], selc[:, 2:3], float(TRASH), selc[:, 3:4],
                    ALU.mult, ALU.add), waits=[e7], inc="qv")
                e9 = P.emit("vector", lambda e: e.tensor_add(
                    selc[:, 1:2], selc[:, 1:2], selc[:, 4:5]), waits=[e8], inc="qv")
                e10 = P.emit("vector", lambda e, i=i: e.tensor_copy(
                    slot_i[:, i:i + 1], selc[:, 1:2]), waits=[e9], inc="qv")
                if debug_phase == "F":
                    P.dma("sync", taps["t_slot"][:, i:i + 1], slot_i[:, i:i + 1], waits=[e10],
                          allow_slow_non_contiguous=True)
            slots_ready = P.ev("qv")

            for i in range(16):
                buf = xb3[:, (i % 2) * D:(i % 2) * D + D]
                xev = P.dma("sync", buf, xmid_hbm[i * 128:(i + 1) * 128, :],
                            waits=[slots_ready, P.ev("qs")])
                agg = stats_on(buf, waits=[xev])
                rse = mk_scale(1, 3, 5, True, agg)
                nme = neg_m_rs(5, 7, rse)
                aev = P.emit("scalar", lambda e, buf=buf: e.activation(
                    uft[:], buf, AF.Identity, bias=stat[:, 7:8], scale=stat[:, 5:6]),
                    waits=[nme], inc="qs")
                P.wait("gpsimd", [aev, slots_ready, P.ev("dg")])
                P.emit("gpsimd", lambda e, i=i: e.indirect_dma_start(
                    out=xg_hbm, out_offset=IndirectOffsetOnAxis(slot_i[:, i:i + 1], 0),
                    in_=uft[:], in_offset=None), inc="dg", n=16)
            P.barrier()

            if debug_phase == "F":
                tmp = SB(phf, "dbgf", [128, D], F32)
                for i in range(16):
                    ev = P.dma("sync", tmp[:], xmid_hbm[i * 128:(i + 1) * 128, :])
                    P.dma("sync", taps["t_xmid"][:, i * D:(i + 1) * D], tmp[:], waits=[ev])
                _finish(P, nc, es)
                return nc
        zstack.close()

        with contextlib.ExitStack() as phx, contextlib.ExitStack() as psx:
            xg_t = SB(phx, "xg_t", [128, 2 * D], BF16)
            xg_f = SB(phx, "xg_f", [128, DC * CAP], BF16)
            h_sb = SB(phx, "h_sb", [128, 32 * CAP], BF16)
            w1s_ = SB(phx, "w1s_", [128, 2 * DC * 512], BF16)
            w2c = SB(phx, "w2c", [128, 2 * 32 * 128], BF16)
            y_f = SB(phx, "y_f", [128, DC * CAP], BF16)
            y_t = SB(phx, "y_t", [128, D], BF16)
            ph1 = PS(psx, "ph1", [128, CAP], F32)
            ph2 = PS(psx, "ph2", [128, CAP], F32)
            py = PS(psx, "py", [128, CAP], F32)
            ptf = PS(psx, "ptfF", [128, 1024], BF16)

            for ex in range(NE):
                for i in range(CAP // 128):
                    gev = P.dma("sync", xg_t[:, (i % 2) * D:(i % 2) * D + D],
                                xg_hbm[ex * CAP + i * 128: ex * CAP + (i + 1) * 128, :],
                                waits=[P.ev("qt"), P.ev("qv")])
                    tev = transpose_tokmaj_to_fm(
                        xg_t[:, (i % 2) * D:(i % 2) * D + D], ptf, False, extra_waits=[gev])
                    P.emit("vector", lambda e, i=i: e.tensor_copy(
                        _r(xg_f[:], "p (c n) -> p c n", c=DC)[:, :, i * 128:(i + 1) * 128],
                        _r(ptf[:], "p (c n) -> p c n", c=DC)),
                        waits=[tev], inc="qv")
                xg_ready = P.ev("qv")

                wflip = 0
                wfree = {0: None, 1: None}
                for hb in range(FH // 512):
                    wsl = w1s_[:, wflip * DC * 512:(wflip + 1) * DC * 512]
                    wev = P.dma("gpsimd", _r(wsl, "p (c n) -> p c n", c=DC),
                                _r(fw1[ex, :, hb * 512:(hb + 1) * 512], "(c p) n -> p c n", p=128),
                                waits=[wfree[wflip]])
                    last_mm = None
                    for sub in range(4):
                        hp = ph1 if sub % 2 == 0 else ph2
                        m0 = hb * 4 + sub
                        for nb in range(CAP // 512):
                            for kc in range(DC):
                                P.emit("tensor", lambda e, kc=kc, nb=nb, sub=sub, wsl=wsl,
                                       hp=hp: e.matmul(
                                    hp[:, nb * 512:(nb + 1) * 512],
                                    wsl[:, kc * 512 + sub * 128: kc * 512 + sub * 128 + 128],
                                    xg_f[:, kc * CAP + nb * 512: kc * CAP + (nb + 1) * 512],
                                    start=(kc == 0), stop=(kc == DC - 1)),
                                    waits=([xg_ready, wev, P.ev("qs")]
                                           if (kc == 0 and nb == 0) else []),
                                    inc="qt" if (kc == DC - 1 and nb == CAP // 512 - 1) else None)
                        mmev = P.ev("qt")
                        last_mm = mmev
                        P.emit("scalar", lambda e, m0=m0, hp=hp: e.activation(
                            h_sb[:, m0 * CAP:(m0 + 1) * CAP], hp[:], AF.Gelu_apprx_tanh),
                            waits=[mmev], inc="qs")
                    wfree[wflip] = last_mm
                    wflip ^= 1
                h_ready = P.ev("qs")

                w2flip = 0
                w2free = {0: None, 1: None}
                for dm in range(DC):
                    wsl = w2c[:, w2flip * 32 * 128:(w2flip + 1) * 32 * 128]
                    wev = P.dma("gpsimd", _r(wsl, "p (c n) -> p c n", c=32),
                                _r(fw2[ex, :, dm * 128:(dm + 1) * 128], "(c p) n -> p c n", p=128),
                                waits=[w2free[w2flip]])
                    for nb in range(CAP // 512):
                        for kc in range(32):
                            P.emit("tensor", lambda e, kc=kc, nb=nb, wsl=wsl: e.matmul(
                                py[:, nb * 512:(nb + 1) * 512],
                                wsl[:, kc * 128:(kc + 1) * 128],
                                h_sb[:, kc * CAP + nb * 512: kc * CAP + (nb + 1) * 512],
                                start=(kc == 0), stop=(kc == 31)),
                                waits=([h_ready, wev, P.ev("qs")]
                                       if (kc == 0 and nb == 0) else []),
                                inc="qt" if (kc == 31 and nb == CAP // 512 - 1) else None)
                    mmev = P.ev("qt")
                    w2free[w2flip] = mmev
                    w2flip ^= 1
                    P.emit("scalar", lambda e, dm=dm: e.activation(
                        y_f[:, dm * CAP:(dm + 1) * CAP], py[:], AF.Identity),
                        waits=[mmev], inc="qs")
                yf_ready = P.ev("qs")
                for i in range(CAP // 128):
                    tev = transpose_fm_to_tokmaj(
                        [y_f[:, c * CAP + i * 128: c * CAP + i * 128 + 128] for c in range(DC)],
                        ptf, False, extra_waits=[yf_ready, P.ev("ds")])
                    yev = P.emit("vector", lambda e: e.tensor_copy(y_t[:], ptf[:]),
                                 waits=[tev, P.ev("ds")], inc="qv")
                    P.dma("sync", yg_hbm[ex * CAP + i * 128: ex * CAP + (i + 1) * 128, :],
                          y_t[:], waits=[yev])
            P.barrier()

        with contextlib.ExitStack() as pho:
            xmo = SB(pho, "xmo", [128, D], F32)
            ygo = SB(pho, "ygo", [128, D], BF16)
            oto = SB(pho, "oto", [128, D], F32)
            for i in range(16):
                xev = P.dma("sync", xmo[:], xmid_hbm[i * 128:(i + 1) * 128, :],
                            waits=[P.ev("ds"), P.ev("qv")])
                gev = P.emit("gpsimd", lambda e, i=i: e.indirect_dma_start(
                    out=ygo[:], out_offset=None, in_=yg_hbm,
                    in_offset=IndirectOffsetOnAxis(slot_i[:, i:i + 1], 0)),
                    waits=[P.ev("ds"), P.ev("qv")], inc="dg", n=16)
                aev = P.emit("vector", lambda e: e.tensor_add(oto[:], xmo[:], ygo[:]),
                             waits=[xev, gev], inc="qv")
                P.dma("sync", yout[i * 128:(i + 1) * 128, :], oto[:], waits=[aev])
        _finish(P, nc, es)
    return nc


def _finish(P, nc, es):
    P.barrier()
    blk = es.enter_context(nc.Block())
    for eng in ENGINES:
        fns = P.streams[eng]
        getattr(blk, eng)(lambda e, fns=fns: [f(e) for f in fns])



_NC_CACHE = {}


def _get_nc():
    key = (DEBUG_PHASE, STRICT)
    if key not in _NC_CACHE:
        _NC_CACHE[key] = build_program(DEBUG_PHASE, STRICT)
    return _NC_CACHE[key]


def _host_tables(h):
    own0 = h * 2048
    g = own0 - 1024 + np.arange(ZLEN, dtype=np.float64)
    inv = (1.0 / 10000.0) ** (np.arange(0, HD, 2, dtype=np.float64) / HD)
    invrep = np.repeat(inv, 2)
    ang = invrep[:, None] * g[None, :]
    cost = np.cos(ang).astype(np.float32)
    sint = np.sin(ang).astype(np.float32)
    sint[0::2, :] *= -1.0
    r = np.arange(128)[:, None]
    j = np.arange(BAND)[None, :]
    band = (j >= r) & (j <= r + WSZ)
    m01 = np.zeros((16, 128, BAND), np.float32)
    for qt in range(16):
        gk = own0 - 1024 + qt * 128 + np.arange(BAND)[None, :]
        m01[qt] = (band & (gk >= 0)).astype(np.float32)
    m01 = np.ascontiguousarray(m01.reshape(16 * 128, BAND)).astype(BFD)
    return (cost.astype(BFD), sint.astype(BFD), m01)


def kernel(**inputs):
    x = np.asarray(inputs["L0_attn_x"], np.float32)
    T = lambda a: np.ascontiguousarray(np.asarray(a, np.float32).T)
    col = lambda a: np.asarray(a, np.float32).reshape(-1, 1)

    shared = {
        "winT": T(inputs["L0_attn_hawk_Win"]),
        "wgT": T(inputs["L0_attn_hawk_Wg"]),
        "woutT": T(inputs["L0_attn_hawk_Wout"]),
        "cwi": np.ascontiguousarray(np.asarray(inputs["L0_attn_hawk_cw"], np.float32)
                                    .reshape(HID, 4)),
        "cbi": col(inputs["L0_attn_hawk_cb"]),
        "hbg": col(inputs["L0_attn_hawk_bg"]),
        "hfb": col(inputs["L0_attn_hawk_fb"]),
        "g1gT": T(inputs["L0_attn_g1_Wgrow"]),
        "g1sT": T(inputs["L0_attn_g1_Wshrink"]),
        "g2gT": T(inputs["L0_attn_g2_Wgrow"]),
        "g2sT": T(inputs["L0_attn_g2_Wshrink"]),
        "wqT": T(inputs["L0_attn_smqa_Wq"]),
        "wkvT": T(inputs["L0_attn_smqa_Wkv"]),
        "woT": T(inputs["L0_attn_smqa_Wout"]),
        "fw1": np.ascontiguousarray(np.asarray(inputs["L0_ffn_W1"], np.float32)),
        "fw2": np.ascontiguousarray(np.asarray(inputs["L0_ffn_W2"], np.float32)),
        "fwg": np.ascontiguousarray(np.asarray(inputs["L0_ffn_Wg"], np.float32)),
        "gnh": col(inputs["L0_attn_gn_hawk"]),
        "gn1": col(inputs["L0_attn_gn_hgmlp"]),
        "gns": col(inputs["L0_attn_gn_smqa"]),
        "gn2": col(inputs["L0_attn_gn_sgmlp"]),
        "bgrep": np.ascontiguousarray(np.broadcast_to(
            np.asarray(inputs["L0_ffn_bg"], np.float32), (128, NE))),
        "identf": np.eye(128, dtype=np.float32),
        "identb": np.eye(128, dtype=np.float32).astype(BFD),
        "onesb": np.ones((1, 128), np.float32).astype(BFD),
        "iota8": np.ascontiguousarray(np.broadcast_to(
            np.arange(NE, dtype=np.float32), (128, NE))),
    }
    tabs = {h: _host_tables(h) for h in (0, 1)}
    in_maps = []
    for core in range(8):
        b, h = core // 2, core % 2
        if h == 1:
            xw = x[b]
        else:
            xw = np.concatenate([x[b, 0:2048], x[b, 0:2048]], 0)
        cost, sint, m01 = tabs[h]
        m = dict(shared)
        m["xwin"] = np.ascontiguousarray(xw)
        m["flagc"] = np.full((128, 1), float(h), np.float32)
        m["epsc"] = np.full((128, 1), EPS, np.float32)
        m["costab"] = cost
        m["sintab"] = sint
        m["m01t"] = m01
        in_maps.append(m)

    nc = _get_nc()
    res = run_bass_kernel_spmd(nc, in_maps, core_ids=list(range(8)))
    out = np.empty((4, 4096, D), np.float32)
    for core in range(8):
        b, h = core // 2, core % 2
        out[b, h * 2048:(h + 1) * 2048] = np.asarray(res.results[core]["y"], np.float32)
    return out
